# version 06
# baseline (speedup 1.0000x reference)
import math
from contextlib import ExitStack

import numpy as np
import ml_dtypes

import concourse.bass as bass
import concourse.mybir as mybir
from concourse.bass_utils import run_bass_kernel_spmd

F32 = mybir.dt.float32
BF16 = mybir.dt.bfloat16
AF = mybir.ActivationFunctionType
ALU = mybir.AluOpType
BF = ml_dtypes.bfloat16

B, T, D = 16, 2048, 1024
NCORES = 8
SEQ_PER_CORE = B // NCORES
NT = T // 128
NE = D // 128
H = 16
DH = 64
NB = 10
NEG = -1.0e9
ROPE_BASE = 10000.0

ENGINES = ("tensor", "vector", "scalar", "gpsimd", "sync")

SERIAL = False
SAME_ENGINE_WAITS = True


class Buf:
    __slots__ = ("name", "writers", "readers", "overlaps")

    def __init__(self, name="", overlaps=()):
        self.name = name
        self.writers = []
        self.readers = []
        self.overlaps = list(overlaps)


class ColView:

    def __init__(self, t, base, width):
        self.t = t
        self.base = base
        self.width = width

    def __getitem__(self, key):
        ps, cs = key
        start = self.base if cs.start is None else self.base + cs.start
        stop = self.base + (self.width if cs.stop is None else cs.stop)
        return self.t[ps, start:stop]


class _Op:
    __slots__ = ("emit", "deps", "marked", "is_dma", "semval")

    def __init__(self, emit, deps, is_dma):
        self.emit = emit
        self.deps = deps
        self.marked = False
        self.is_dma = is_dma
        self.semval = None


class Sched:
    def __init__(self):
        self.ops = {e: [] for e in ENGINES}
        self.last = None

    def op(self, eng, emit, reads=(), writes=(), is_dma=False):
        deps = []
        for b in reads:
            deps.extend(b.writers)
            for o in b.overlaps:
                deps.extend(o.writers)
        for b in writes:
            deps.extend(b.readers)
            deps.extend(b.writers)
            for o in b.overlaps:
                deps.extend(o.readers)
                deps.extend(o.writers)
        if SERIAL and self.last is not None:
            deps.append(self.last)
        uniq = []
        seen = set()
        for d in deps:
            de, di = d
            if (de, di) in seen:
                continue
            seen.add((de, di))
            dop = self.ops[de][di]
            if de == eng and not dop.is_dma and not SERIAL:
                if eng == "tensor" or not SAME_ENGINE_WAITS:
                    continue
            uniq.append(d)
        idx = len(self.ops[eng])
        rec = _Op(emit, uniq, is_dma)
        if is_dma:
            rec.marked = True
        self.ops[eng].append(rec)
        tok = (eng, idx)
        for d in uniq:
            self.ops[d[0]][d[1]].marked = True
        for b in reads:
            b.readers.append(tok)
        for b in writes:
            b.writers = [tok]
            b.readers = []
        self.last = tok
        return tok

    def barrier(self, eng, tokens):
        for t in tokens:
            self.ops[t[0]][t[1]].marked = True
        rec = _Op(None, list(dict.fromkeys(tokens)), False)
        self.ops[eng].append(rec)

    def finalize(self, nc, block, stack):
        sems = {}
        for e in ENGINES:
            if any(o.marked for o in self.ops[e]):
                sems[e] = stack.enter_context(nc.semaphore(f"sem_{e}"))
            cnt = 0
            for o in self.ops[e]:
                if o.marked:
                    cnt += 16 if o.is_dma else 1
                    o.semval = cnt

        def mk(e):
            def run(eng):
                hw = {}
                for o in self.ops[e]:
                    for de, di in o.deps:
                        v = self.ops[de][di].semval
                        if hw.get(de, 0) < v:
                            eng.wait_ge(sems[de], v)
                            hw[de] = v
                    if o.emit is None:
                        continue
                    inst = o.emit(eng)
                    if o.marked:
                        inst.then_inc(sems[e], 16 if o.is_dma else 1)
            return run

        for e in ENGINES:
            if self.ops[e]:
                getattr(block, e)(mk(e))


def psum_chunks(c0, c1):
    out = []
    c = c0
    while c < c1:
        nxt = min(c1, (c // 512 + 1) * 512)
        out.append((c, nxt))
        c = nxt
    return out



def host_consts():
    c = {}
    c["C_IDENT"] = np.eye(128, dtype=np.float32).astype(BF)
    c["C_ONESROW"] = np.ones((1, 512), np.float32).astype(BF)
    c["C_ONESCOL"] = np.ones((128, 1), np.float32).astype(BF)
    c["C_ONESF1"] = np.ones((1, 64), np.float32)
    inv = 1.0 / ROPE_BASE ** (np.arange(0, DH, 2, dtype=np.float32) / DH)
    tpos = np.arange(T, dtype=np.float32)
    freqs = np.einsum("i,j->ij", tpos, inv)
    emb = np.concatenate([freqs, freqs], axis=-1)
    cosf = np.cos(emb).T.astype(np.float32)
    sinf = np.sin(emb).T.astype(np.float32)
    c["C_COS2"] = np.concatenate([cosf, cosf], 0).astype(BF)
    c["C_SIN2"] = np.concatenate([sinf, sinf], 0).astype(BF)
    p64 = np.zeros((64, 64), np.float32)
    for i in range(32):
        p64[2 * i + 1, 2 * i] = -1.0
        p64[2 * i, 2 * i + 1] = 1.0
    P = np.zeros((128, 128), np.float32)
    P[:64, :64] = p64
    P[64:, 64:] = p64
    c["C_PROT"] = P.astype(BF)
    return c


CONST_SHAPES = {
    "C_IDENT": ([128, 128], BF16),
    "C_ONESROW": ([1, 512], BF16),
    "C_ONESCOL": ([128, 1], BF16),
    "C_ONESF1": ([1, 64], F32),
    "C_COS2": ([128, T], BF16),
    "C_SIN2": ([128, T], BF16),
    "C_PROT": ([128, 128], BF16),
}


def add_consts(nc, names):
    aps = {}
    for n in names:
        s, d = CONST_SHAPES[n]
        aps[n] = nc.dram_tensor(n, s, d, kind="ExternalInput").ap()
    return aps



class Common:
    def __init__(self, p, nc, ec, consts, n_ppt=2):
        self.p = p
        self.nc = nc
        self.consts = consts
        self.xtile = [ec(nc.sbuf_tensor(f"xtile{i}", [128, D], F32)) for i in range(2)]
        self.bxtile = [Buf(f"xtile{i}") for i in range(2)]
        self.sq = None
        self.bsq = Buf("lnsq")
        self.cols = ec(nc.sbuf_tensor("lncols", [128, 80], F32))
        self.bsums, self.bsumsq, self.bsmall = Buf("sums"), Buf("sumsq"), Buf("small")
        self.lntile = ec(nc.sbuf_tensor("lntile", [128, D], BF16))
        self.blntile = Buf("lntile")
        self.ident = ec(nc.sbuf_tensor("identc", [128, 128], BF16))
        self.bident = Buf("ident")
        self.ppt = [ec(nc.psum_tensor(f"pptc{i}", [128, 512], BF16)) for i in range(n_ppt)]
        self.bppt = [Buf(f"ppt{i}") for i in range(n_ppt)]
        self.onesrow = ec(nc.sbuf_tensor("onesrowc", [1, 512], BF16))
        self.bonesrow = Buf("onesrow")
        self.onescol = ec(nc.sbuf_tensor("onescolc", [128, 1], BF16))
        self.bonescol = Buf("onescol")
        self.onesf1 = ec(nc.sbuf_tensor("onesf1c", [1, 64], F32))
        self.bonesf1 = Buf("onesf1")
        p.op("sync", lambda e: e.dma_start(self.ident[:, :], consts["C_IDENT"]),
             writes=(self.bident,), is_dma=True)
        p.op("sync", lambda e: e.dma_start(self.onesrow[:, :], consts["C_ONESROW"]),
             writes=(self.bonesrow,), is_dma=True)
        p.op("sync", lambda e: e.dma_start(self.onescol[:, :], consts["C_ONESCOL"]),
             writes=(self.bonescol,), is_dma=True)
        p.op("sync", lambda e: e.dma_start(self.onesf1[:, :], consts["C_ONESF1"]),
             writes=(self.bonesf1,), is_dma=True)

    def load_xtile(self, xin, seq, j, slot, q="sync"):
        off = seq * T * D + j * 128 * D
        self.p.op(q, lambda e, off=off, slot=slot: e.dma_start(
            self.xtile[slot][:, :], bass.AP(xin.tensor, off, [[D, 128], [1, D]])),
            writes=(self.bxtile[slot],), is_dma=True)

    def emit_ln_to_lnt(self, xin, seq, lnt, blnt):
        p = self.p
        cols = self.cols
        for j in range(NT):
            s = j % 2
            self.load_xtile(xin, seq, j, s)
            p.op("vector", lambda v, j=j, s=s: v.reduce_sum(
                cols[:, j:j + 1], self.xtile[s][:, :], mybir.AxisListType.X),
                reads=(self.bxtile[s],), writes=(self.bsums,))
            p.op("scalar", lambda sc, j=j, s=s: sc.activation(
                self.xtile[s][:, :], self.xtile[s][:, :], AF.Square,
                accum_out=cols[:, 16 + j:17 + j]),
                reads=(self.bxtile[s],), writes=(self.bxtile[s], self.bsumsq))
        p.op("vector", lambda v: v.tensor_scalar(
            cols[:, 32:48], cols[:, 0:16], 1.0 / D, None, ALU.mult),
            reads=(self.bsums,), writes=(self.bsmall,))
        p.op("vector", lambda v: v.tensor_scalar(
            cols[:, 48:64], cols[:, 16:32], 1.0 / D, None, ALU.mult),
            reads=(self.bsumsq,), writes=(self.bsmall,))
        p.op("vector", lambda v: v.scalar_tensor_tensor(
            cols[:, 64:80], cols[:, 32:48], -1.0, cols[:, 32:48], ALU.mult, ALU.mult),
            reads=(self.bsmall,), writes=(self.bsmall,))
        p.op("vector", lambda v: v.tensor_tensor(
            cols[:, 64:80], cols[:, 48:64], cols[:, 64:80], ALU.add),
            reads=(self.bsmall,), writes=(self.bsmall,))
        p.op("vector", lambda v: v.tensor_scalar(
            cols[:, 64:80], cols[:, 64:80], 1e-5, None, ALU.add),
            reads=(self.bsmall,), writes=(self.bsmall,))
        p.op("vector", lambda v: v.reciprocal(cols[:, 64:80], cols[:, 64:80]),
            reads=(self.bsmall,), writes=(self.bsmall,))
        p.op("scalar", lambda sc: sc.activation(
            cols[:, 64:80], cols[:, 64:80], AF.Sqrt),
            reads=(self.bsmall,), writes=(self.bsmall,))
        si = 0
        for j in range(NT):
            s = j % 2
            self.load_xtile(xin, seq, j, s)
            p.op("vector", lambda v, j=j, s=s: v.tensor_scalar(
                self.lntile[:, :], self.xtile[s][:, :],
                cols[:, 32 + j:33 + j], cols[:, 64 + j:65 + j],
                ALU.subtract, ALU.mult),
                reads=(self.bxtile[s], self.bsmall), writes=(self.blntile,))
            for e2 in range(0, NE, 4):
                ps = si % len(self.ppt)
                si += 1
                for k in range(4):
                    e = e2 + k
                    p.op("tensor", lambda t, e=e, ps=ps, k=k: t.transpose(
                        self.ppt[ps][:, k * 128:(k + 1) * 128],
                        self.lntile[:, e * 128:(e + 1) * 128], self.ident[:, :]),
                        reads=(self.blntile, self.bident), writes=(self.bppt[ps],))
                eng = "vector" if (si % 2) else "scalar"
                for k in range(4):
                    e = e2 + k
                    dst = lnt[:, e * T + j * 128:e * T + (j + 1) * 128]
                    src = self.ppt[ps][:, k * 128:(k + 1) * 128]
                    if eng == "vector":
                        p.op(eng, lambda en, dst=dst, src=src: en.tensor_copy(dst, src),
                             reads=(self.bppt[ps],), writes=(blnt,))
                    else:
                        p.op(eng, lambda en, dst=dst, src=src: en.copy(dst, src),
                             reads=(self.bppt[ps],), writes=(blnt,))

    def emit_weight_cast(self, wdram, wsb, bwsb, rows, cols, wtmp, bwtmp):
        p = self.p
        nr = rows // 128
        for r in range(nr):
            off = r * 128 * cols
            p.op("gpsimd", lambda e, off=off, cols=cols: e.dma_start(
                wtmp[:, 0:cols], bass.AP(wdram.tensor, off, [[cols, 128], [1, cols]])),
                writes=(bwtmp,), is_dma=True)
            p.op("vector", lambda v, r=r, cols=cols: v.tensor_copy(
                wsb[:, r * cols:(r + 1) * cols], wtmp[:, 0:cols]),
                reads=(bwtmp,), writes=(bwsb,))

    def emit_weight_transpose(self, wdram, wtsb, bwtsb, wtmp, bwtmp, wtb, bwtb):
        p = self.p
        si = 0
        for r in range(NE):
            off = r * 128 * D
            p.op("gpsimd", lambda e, off=off: e.dma_start(
                wtmp[:, 0:D], bass.AP(wdram.tensor, off, [[D, 128], [1, D]])),
                writes=(bwtmp,), is_dma=True)
            p.op("vector", lambda v: v.tensor_copy(wtb[:, 0:D], wtmp[:, 0:D]),
                 reads=(bwtmp,), writes=(bwtb,))
            for c2 in range(0, NE, 4):
                ps = si % len(self.ppt)
                si += 1
                for k in range(4):
                    c = c2 + k
                    p.op("tensor", lambda t, c=c, ps=ps, k=k: t.transpose(
                        self.ppt[ps][:, k * 128:(k + 1) * 128],
                        wtb[:, c * 128:(c + 1) * 128], self.ident[:, :]),
                        reads=(bwtb, self.bident), writes=(self.bppt[ps],))
                eng = "vector" if (si % 2) else "scalar"
                for k in range(4):
                    c = c2 + k
                    dst = wtsb[:, c * D + r * 128:c * D + (r + 1) * 128]
                    src = self.ppt[ps][:, k * 128:(k + 1) * 128]
                    if eng == "vector":
                        p.op(eng, lambda en, dst=dst, src=src: en.tensor_copy(dst, src),
                             reads=(self.bppt[ps],), writes=(bwtsb,))
                    else:
                        p.op(eng, lambda en, dst=dst, src=src: en.copy(dst, src),
                             reads=(self.bppt[ps],), writes=(bwtsb,))

    def residual_add_store(self, xin, xout, seq, j, psums, scale=None):
        p = self.p
        s = j % 2
        self.load_xtile(xin, seq, j, s)
        for oc, (ps, bps) in enumerate(psums):
            if scale is None:
                p.op("vector", lambda v, oc=oc, s=s, ps=ps, bps=bps: v.tensor_tensor(
                    self.xtile[s][:, oc * 512:(oc + 1) * 512],
                    self.xtile[s][:, oc * 512:(oc + 1) * 512], ps, ALU.add),
                    reads=(bps, self.bxtile[s]), writes=(self.bxtile[s],))
            else:
                p.op("vector", lambda v, oc=oc, s=s, ps=ps, bps=bps: v.scalar_tensor_tensor(
                    self.xtile[s][:, oc * 512:(oc + 1) * 512],
                    ps, scale, self.xtile[s][:, oc * 512:(oc + 1) * 512],
                    ALU.mult, ALU.add),
                    reads=(bps, self.bxtile[s]), writes=(self.bxtile[s],))
        off = seq * T * D + j * 128 * D
        return p.op("sync", lambda e, off=off, s=s: e.dma_start(
            bass.AP(xout.tensor, off, [[D, 128], [1, D]]), self.xtile[s][:, :]),
            reads=(self.bxtile[s],), is_dma=True)



class Shared:
    def __init__(self, p, nc, ec, consts):
        self.p = p
        self.nc = nc
        self.cm = Common(p, nc, ec, consts, n_ppt=1)
        self.lnt = ec(nc.sbuf_tensor("lnt", [128, NE * T], BF16))
        self.blnt = Buf("lnt")
        self.A = ec(nc.sbuf_tensor("arA", [128, NE * T], BF16))
        self.bA = Buf("arA")
        self.Bb = ec(nc.sbuf_tensor("arB", [128, NE * T], BF16))
        self.bB = Buf("arB")
        self.Cc = ec(nc.sbuf_tensor("arC", [128, NT * (H * 65)], BF16))
        self.bC_ = Buf("arC")
        self.W1 = ec(nc.sbuf_tensor("W1", [128, NE * D], BF16))
        self.bW1 = Buf("W1")
        self.W2 = None
        self.bW2 = Buf("W2")
        self.Wrsb = ec(nc.sbuf_tensor("Wrsb", [128, NE * (H * NB)], BF16))
        self.bWrsb = Buf("Wrsb")
        self.wtmp = ec(nc.sbuf_tensor("wtmp", [128, D], F32))
        self.bwtmp = Buf("wtmp")
        self.wtb = ec(nc.sbuf_tensor("wtb", [128, D], BF16))
        self.bwtb = Buf("wtb")
        self.halfpi = ec(nc.sbuf_tensor("halfpi", [128, 1], F32))
        self.bhalfpi = Buf("halfpi")
        self.f32s = [ec(nc.sbuf_tensor(f"f32s{i}", [128, 512], F32)) for i in range(4)]
        self.bf32s = [Buf(f"f32s{i}") for i in range(4)]
        self.krelu = [ec(nc.sbuf_tensor(f"krelu{i}", [128, 512], BF16)) for i in range(2)]
        self.bkrelu = [Buf(f"krelu{i}") for i in range(2)]
        self.rsb = [ec(nc.sbuf_tensor(f"rsb{i}", [128, 512], BF16)) for i in range(2)]
        self.brsb = [Buf(f"rsb{i}") for i in range(2)]
        self.expt = [ec(nc.sbuf_tensor(f"expt{i}", [128, 1024], BF16)) for i in range(2)]
        self.bexpt = [Buf(f"expt{i}") for i in range(2)]
        self.mus = ec(nc.sbuf_tensor("mus", [128, 32], F32))
        self.bmus = Buf("mus")
        self.brow = ec(nc.sbuf_tensor("brow", [1, 4 * D], BF16))
        self.bbrow = Buf("brow")
        self.prot = ec(nc.sbuf_tensor("prot", [128, 128], BF16))
        self.bprot = Buf("prot")
        self.bndsb = ec(nc.sbuf_tensor("bndsb", [NB, 1024], BF16))
        self.bbnd = Buf("bnd")
        self.rstage = ec(nc.sbuf_tensor("rstage", [80, 512], BF16))
        self.brstage = Buf("rstage")
        self.negtile = ec(nc.sbuf_tensor("negtile", [128, 128], BF16))
        self.bneg = Buf("negtile")
        self.rslice = [ec(nc.sbuf_tensor(f"rslice{i}", [NB, T], BF16)) for i in range(2)]
        self.brslice = [Buf(f"rslice{i}") for i in range(2)]
        self.recrow = ec(nc.sbuf_tensor("recrow", [1, 512], F32))
        self.brecrow = Buf("recrow")
        self.rbsb = ec(nc.sbuf_tensor("rbsb", [64, 512], F32))
        self.brbsb = Buf("rbsb")
        self.pg = ec(nc.psum_tensor("pg", [128, 1024], F32))
        self.bpg = [Buf("pg0"), Buf("pg1")]
        self.plog = ec(nc.psum_tensor("plog", [128, 1152], F32))
        self.bplog = Buf("plog")
        self.bplogA = Buf("plogA", overlaps=(self.bplog,))
        self.bplogB = Buf("plogB", overlaps=(self.bplog,))
        self.bplogC = Buf("plogC", overlaps=(self.bplog,))
        self.bplog.overlaps.extend([self.bplogA, self.bplogB, self.bplogC])
        self.pav = ec(nc.psum_tensor("pav", [65, 512], F32))
        self.bpav = Buf("pav")
        self.prb = ec(nc.psum_tensor("prb", [64, 512], F32))
        self.bprb = Buf("prb")
        self.g2 = [ColView(self.lnt, 0, 1152), ColView(self.lnt, 1152, 1152),
                   ColView(self.lnt, 12288, 1152)]
        self.probs = [ColView(self.lnt, 2304, 1152), ColView(self.lnt, 3456, 1152)]
        self.probsT = ColView(self.lnt, 4608, 1152)
        self.gstage = [ColView(self.lnt, 5760, 1024), ColView(self.lnt, 13440, 1024)]
        self.AT0 = ColView(self.lnt, 6784, NE * 128)
        self.bg2 = [Buf(f"g2_{i}", overlaps=(self.blnt,)) for i in range(3)]
        self.bprobs = [Buf(f"probs{i}", overlaps=(self.blnt,)) for i in range(2)]
        self.bprobsT = Buf("probsT", overlaps=(self.blnt,))
        self.bgstage = [Buf(f"gstage{i}", overlaps=(self.blnt,)) for i in range(2)]
        self.bAT0 = Buf("AT0", overlaps=(self.blnt,))
        for bb in self.bg2 + self.bprobs + self.bgstage + [self.bprobsT, self.bAT0]:
            self.blnt.overlaps.append(bb)
        self.AT1 = ColView(self.lnt, 8192, NE * 512)
        self.bAT1 = Buf("AT1", overlaps=(self.blnt,))
        self.blnt.overlaps.append(self.bAT1)
        self.W2 = ColView(self.Cc, 0, NE * D)
        self.cos2 = ColView(self.Cc, 0, T)
        self.sin2 = ColView(self.Cc, T, T)
        self.qraw = [ColView(self.Cc, 2 * T + 512 * i, 512) for i in range(2)]
        self.rtmp = [ColView(self.Cc, 2 * T + 1024 + 512 * i, 512) for i in range(2)]
        cgroup = []
        self.bcos2 = Buf("cos2")
        self.bsin2 = Buf("sin2")
        self.bqraw = [Buf(f"qraw{i}") for i in range(2)]
        self.brtmp = [Buf(f"rtmp{i}") for i in range(2)]
        cgroup = [self.bcos2, self.bsin2] + self.bqraw + self.brtmp + [self.bC_, self.bW2]
        for bb in cgroup:
            bb.overlaps.extend([o for o in cgroup if o is not bb])

        p.op("vector", lambda v: v.memset(self.halfpi[:, :], float(np.pi / 2)),
             writes=(self.bhalfpi,))



def emit_ffn0(p, sh, xin, xout, w):
    cm = sh.cm
    lnt, blnt = sh.lnt, sh.blnt
    wtmp, bwtmp = sh.wtmp, sh.bwtmp
    Cb = ColView(sh.A, 0, NE * D)
    Sb = ColView(sh.A, NE * D, NE * D)
    bC = bS = sh.bA
    sqr = [sh.f32s[0], sh.f32s[1]]
    sqi = [sh.f32s[2], sh.f32s[3]]
    froot = sqi
    bsqr = [sh.bf32s[0], sh.bf32s[1]]
    bsqi = [sh.bf32s[2], sh.bf32s[3]]
    bfroot = bsqi
    pr_ap = [sh.pg[:, 0:512], sh.pg[:, 512:1024]]
    bpr = sh.bpg
    pi_ap = [sh.plog[:, 0:512], sh.plog[:, 512:1024]]
    bpi = [sh.bplogA, sh.bplogB]

    for (wa, wb, dst, bdst, is_cos) in (
            (w["theta"], w["theta_n"], Cb, bC, True),
            (w["phi"], w["phi_n"], Sb, bS, False)):
        for r in range(NE):
            off = r * 128 * D
            p.op("gpsimd", lambda e, wa=wa, off=off: e.dma_start(
                wtmp[:, 0:D], bass.AP(wa.tensor, off, [[D, 128], [1, D]])),
                writes=(bwtmp,), is_dma=True)
            p.op("gpsimd", lambda e, wb=wb, off=off: e.dma_start(
                wtmp[:, 0:D], bass.AP(wb.tensor, off, [[D, 128], [1, D]]),
                accum_op=ALU.add),
                reads=(bwtmp,), writes=(bwtmp,), is_dma=True)
            if is_cos:
                p.op("scalar", lambda s: s.activation(
                    wtmp[:, 0:D], wtmp[:, 0:D], AF.Sin, bias=sh.halfpi[:, 0:1]),
                    reads=(bwtmp, sh.bhalfpi), writes=(bwtmp,))
            else:
                p.op("scalar", lambda s: s.activation(
                    wtmp[:, 0:D], wtmp[:, 0:D], AF.Sin),
                    reads=(bwtmp,), writes=(bwtmp,))
            p.op("vector", lambda v, dst=dst, r=r, bdst=bdst: v.tensor_scalar(
                dst[:, r * D:(r + 1) * D], wtmp[:, 0:D], 0.5, None, ALU.mult),
                reads=(bwtmp,), writes=(bdst,))

    out_toks = []
    for seq in range(SEQ_PER_CORE):
        cm.emit_ln_to_lnt(xin, seq, lnt, blnt)
        for j in range(NT):
            for oc in range(2):
                s = oc
                for k in range(NE):
                    p.op("tensor", lambda t, j=j, oc=oc, k=k, s=s: t.matmul(
                        pr_ap[s],
                        lnt[:, k * T + j * 128:k * T + (j + 1) * 128],
                        Cb[:, k * D + oc * 512:k * D + (oc + 1) * 512],
                        start=(k == 0), stop=(k == NE - 1)),
                        reads=(blnt, bC), writes=(bpr[s],))
                for k in range(NE):
                    p.op("tensor", lambda t, j=j, oc=oc, k=k, s=s: t.matmul(
                        pi_ap[s],
                        lnt[:, k * T + j * 128:k * T + (j + 1) * 128],
                        Sb[:, k * D + oc * 512:k * D + (oc + 1) * 512],
                        start=(k == 0), stop=(k == NE - 1)),
                        reads=(blnt, bS), writes=(bpi[s],))
                p.op("scalar", lambda s2, s=s: s2.activation(
                    sqr[s][:, :], pr_ap[s], AF.Square),
                    reads=(bpr[s],), writes=(bsqr[s],))
                p.op("scalar", lambda s2, s=s: s2.activation(
                    sqi[s][:, :], pi_ap[s], AF.Square),
                    reads=(bpi[s],), writes=(bsqi[s],))
                p.op("vector", lambda v, s=s: v.tensor_tensor(
                    sqr[s][:, :], sqr[s][:, :], sqi[s][:, :], ALU.add),
                    reads=(bsqr[s], bsqi[s]), writes=(bsqr[s],))
                p.op("scalar", lambda s2, s=s: s2.activation(
                    froot[s][:, :], sqr[s][:, :], AF.Sqrt),
                    reads=(bsqr[s],), writes=(bfroot[s],))
            out_toks.append(cm.residual_add_store(
                xin, xout, seq, j,
                [(froot[0][:, :], bfroot[0]), (froot[1][:, :], bfroot[1])]))
    return out_toks



def emit_ffn1(p, sh, xin, xout, w):
    cm = sh.cm
    lnt, blnt = sh.lnt, sh.blnt
    kact = lnt
    wtmp, bwtmp, wtb, bwtb = sh.wtmp, sh.bwtmp, sh.wtb, sh.bwtb
    WTa, bWTa = sh.W1, sh.bW1
    WTb, bWTb = sh.W2, sh.bW2
    mus, bmus = sh.mus, sh.bmus
    xr, bxr = sh.A, sh.bA
    xk, bxk = sh.Bb, sh.bB
    krelu, bkrelu = sh.krelu, sh.bkrelu
    rsb, brsb = sh.rsb, sh.brsb
    vtmp = [sh.f32s[0], sh.f32s[1]]
    bvtmp = [sh.bf32s[0], sh.bf32s[1]]
    pf_ap = [sh.pg[:, 0:512], sh.pg[:, 512:1024],
             sh.plog[:, 0:512], sh.plog[:, 512:1024]]
    bpf = [sh.bpg[0], sh.bpg[1], sh.bplogA, sh.bplogB]

    for (src, colbase) in ((w["mu_r"], 0), (w["mu_k"], 8)):
        for cc in range(NE):
            p.op("gpsimd", lambda e, src=src, cc=cc, colbase=colbase: e.dma_start(
                mus[:, colbase + cc:colbase + cc + 1],
                bass.AP(src.tensor, cc * 128, [[1, 128], [1, 1]])),
                writes=(bmus,), is_dma=True)
    p.op("vector", lambda v: v.tensor_scalar(
        mus[:, 16:32], mus[:, 0:16], -1.0, 1.0, ALU.mult, ALU.add),
        reads=(bmus,), writes=(bmus,))

    out_toks = []
    for seq in range(SEQ_PER_CORE):
        cm.emit_ln_to_lnt(xin, seq, lnt, blnt)
        for cc in range(NE):
            for (dst, bdst, mcol) in ((xr, bxr, 0), (xk, bxk, 8)):
                p.op("vector", lambda v, cc=cc, dst=dst, mcol=mcol: v.tensor_scalar(
                    dst[:, cc * T:(cc + 1) * T], lnt[:, cc * T:(cc + 1) * T],
                    mus[:, mcol + cc:mcol + cc + 1], None, ALU.mult),
                    reads=(blnt, bmus), writes=(bdst,))
                p.op("vector", lambda v, cc=cc, dst=dst, mcol=mcol: v.scalar_tensor_tensor(
                    dst[:, cc * T + 1:(cc + 1) * T],
                    lnt[:, cc * T:(cc + 1) * T - 1],
                    mus[:, 16 + mcol + cc:17 + mcol + cc],
                    dst[:, cc * T + 1:(cc + 1) * T],
                    ALU.mult, ALU.add),
                    reads=(blnt, bmus, bdst), writes=(bdst,))
        cm.emit_weight_transpose(w["W_k"], WTa, bWTa, wtmp, bwtmp, wtb, bwtb)
        for g in range(NE):
            for tc in range(4):
                s = (g * 4 + tc) % 2
                for cc in range(NE):
                    p.op("tensor", lambda t, g=g, tc=tc, cc=cc, s=s: t.matmul(
                        pf_ap[s],
                        WTa[:, cc * D + g * 128:cc * D + (g + 1) * 128],
                        xk[:, cc * T + tc * 512:cc * T + (tc + 1) * 512],
                        start=(cc == 0), stop=(cc == NE - 1)),
                        reads=(bWTa, bxk), writes=(bpf[s],))
                p.op("scalar", lambda sc, s=s: sc.activation(
                    krelu[s][:, :], pf_ap[s], AF.Relu),
                    reads=(bpf[s],), writes=(bkrelu[s],))
                p.op("vector", lambda v, g=g, tc=tc, s=s: v.scalar_tensor_tensor(
                    kact[:, g * T + tc * 512:g * T + (tc + 1) * 512],
                    krelu[s][:, :], 0.5, krelu[s][:, :], ALU.mult, ALU.mult),
                    reads=(bkrelu[s],), writes=(blnt,))
        cm.emit_weight_transpose(w["W_r"], WTb, bWTb, wtmp, bwtmp, wtb, bwtb)
        cm.emit_weight_transpose(w["W_v"], WTa, bWTa, wtmp, bwtmp, wtb, bwtb)
        for j in range(NT):
            pso = []
            for oc in range(2):
                s = oc
                for cc in range(NE):
                    p.op("tensor", lambda t, j=j, oc=oc, cc=cc, s=s: t.matmul(
                        pf_ap[s],
                        xr[:, cc * T + j * 128:cc * T + (j + 1) * 128],
                        WTb[:, cc * D + oc * 512:cc * D + (oc + 1) * 512],
                        start=(cc == 0), stop=(cc == NE - 1)),
                        reads=(bxr, bWTb), writes=(bpf[s],))
                p.op("scalar", lambda sc, s=s: sc.activation(
                    rsb[s][:, :], pf_ap[s], AF.Sigmoid),
                    reads=(bpf[s],), writes=(brsb[s],))
                for dd in range(NE):
                    p.op("tensor", lambda t, j=j, oc=oc, dd=dd, s=s: t.matmul(
                        pf_ap[2 + s],
                        kact[:, dd * T + j * 128:dd * T + (j + 1) * 128],
                        WTa[:, dd * D + oc * 512:dd * D + (oc + 1) * 512],
                        start=(dd == 0), stop=(dd == NE - 1)),
                        reads=(blnt, bWTa), writes=(bpf[2 + s],))
                p.op("vector", lambda v, s=s: v.tensor_tensor(
                    vtmp[s][:, :], pf_ap[2 + s], rsb[s][:, :], ALU.mult),
                    reads=(bpf[2 + s], brsb[s]), writes=(bvtmp[s],))
                pso.append((vtmp[s][:, :], bvtmp[s]))
            out_toks.append(cm.residual_add_store(xin, xout, seq, j, pso))
    return out_toks



def emit_att1(p, sh, xin, xout, w, consts):
    cm = sh.cm
    lnt, blnt = sh.lnt, sh.blnt
    wtmp, bwtmp, wtb, bwtb = sh.wtmp, sh.bwtmp, sh.wtb, sh.bwtb
    WT, bWT = sh.W1, sh.bW1
    cos2, sin2 = sh.cos2, sh.sin2
    bcos2, bsin2 = sh.bcos2, sh.bsin2
    prot, bprot = sh.prot, sh.bprot
    brow, bbrow = sh.brow, sh.bbrow
    QT, bQT = sh.A, sh.bA
    KT, bKT = sh.Bb, sh.bB
    Vsb, bV = sh.Cc, sh.bC_
    qraw, bqraw = sh.qraw, sh.bqraw
    rtmp, brtmp = sh.rtmp, sh.brtmp
    expt, bexpt = sh.expt, sh.bexpt
    AT, bAT = sh.AT1, sh.bAT1
    recrow, brecrow = sh.recrow, sh.brecrow
    rbsb, brbsb = sh.rbsb, sh.brbsb
    pl_ap = [sh.plog[:, 0:512], sh.plog[:, 512:1024], sh.pg[:, 0:512]]
    bpl = [sh.bplogA, sh.bplogB, sh.bpg[0]]
    prot_ps = sh.pg[:, 512:1024]
    bprot_ps = sh.bpg[1]
    pav, bpav = sh.pav, sh.bpav
    prb, bprb = sh.prb, sh.bprb

    p.op("sync", lambda e: e.dma_start(cos2[:, :], consts["C_COS2"]),
         writes=(bcos2,), is_dma=True)
    p.op("sync", lambda e: e.dma_start(sin2[:, :], consts["C_SIN2"]),
         writes=(bsin2,), is_dma=True)
    p.op("sync", lambda e: e.dma_start(prot[:, :], consts["C_PROT"]),
         writes=(bprot,), is_dma=True)
    for i, bn in enumerate(("bq", "bk", "bv", "bo")):
        bsrc = w[bn]
        p.op("gpsimd", lambda e, bsrc=bsrc: e.dma_start(
            wtmp[0:1, 0:D], bass.AP(bsrc.tensor, 0, [[D, 1], [1, D]])),
            writes=(bwtmp,), is_dma=True)
        p.op("vector", lambda v, i=i: v.tensor_copy(
            brow[0:1, i * D:(i + 1) * D], wtmp[0:1, 0:D]),
            reads=(bwtmp,), writes=(bbrow,))

    out_toks = []
    for seq in range(SEQ_PER_CORE):
        cm.emit_ln_to_lnt(xin, seq, lnt, blnt)

        for (wi, kind, browoff) in ((w["Wq"], "q", 0), (w["Wk"], "k", 1),
                                    (w["Wv"], "v", 2)):
            cm.emit_weight_transpose(wi, WT, bWT, wtmp, bwtmp, wtb, bwtb)
            if kind == "v":
                for j in range(NT):
                    for oc in range(2):
                        s = oc
                        for cc in range(NE):
                            p.op("tensor", lambda t, j=j, oc=oc, cc=cc, s=s: t.matmul(
                                pl_ap[s],
                                lnt[:, cc * T + j * 128:cc * T + (j + 1) * 128],
                                WT[:, cc * D + oc * 512:cc * D + (oc + 1) * 512],
                                start=(cc == 0), stop=False),
                                reads=(blnt, bWT), writes=(bpl[s],))
                        p.op("tensor", lambda t, oc=oc, s=s, browoff=browoff: t.matmul(
                            pl_ap[s],
                            cm.onesrow[0:1, 0:128],
                            brow[0:1, browoff * D + oc * 512:browoff * D + (oc + 1) * 512],
                            start=False, stop=True),
                            reads=(cm.bonesrow, bbrow), writes=(bpl[s],))
                        p.op("scalar", lambda sc, j=j, oc=oc, s=s: sc.activation(
                            bass.AP(Vsb, j * 1040 + oc * 8 * 65,
                                    [[NT * 1040, 128], [65, 8], [1, 64]]),
                            pl_ap[s], AF.Identity),
                            reads=(bpl[s],), writes=(bV,))
            else:
                scale = 0.125 if kind == "q" else 1.0
                dst, bdst = (QT, bQT) if kind == "q" else (KT, bKT)
                for g in range(NE):
                    for tc in range(4):
                        s = (g * 4 + tc) % 2
                        for cc in range(NE):
                            p.op("tensor", lambda t, g=g, tc=tc, cc=cc, s=s: t.matmul(
                                pl_ap[s],
                                WT[:, cc * D + g * 128:cc * D + (g + 1) * 128],
                                lnt[:, cc * T + tc * 512:cc * T + (tc + 1) * 512],
                                start=(cc == 0), stop=False),
                                reads=(blnt, bWT), writes=(bpl[s],))
                        p.op("tensor", lambda t, g=g, s=s, browoff=browoff: t.matmul(
                            pl_ap[s],
                            brow[0:1, browoff * D + g * 128:browoff * D + (g + 1) * 128],
                            cm.onesrow[0:1, 0:512],
                            start=False, stop=True),
                            reads=(bbrow, cm.bonesrow), writes=(bpl[s],))
                        p.op("scalar", lambda sc, s=s, scale=scale: sc.activation(
                            qraw[s][:, :], pl_ap[s], AF.Identity, scale=scale),
                            reads=(bpl[s],), writes=(bqraw[s],))
                        p.op("tensor", lambda t, s=s: t.matmul(
                            prot_ps, prot[:, :], qraw[s][:, :],
                            start=True, stop=True),
                            reads=(bprot, bqraw[s]), writes=(bprot_ps,))
                        p.op("vector", lambda v, tc=tc, s=s: v.tensor_tensor(
                            rtmp[s][:, :], prot_ps,
                            sin2[:, tc * 512:(tc + 1) * 512], ALU.mult),
                            reads=(bprot_ps, bsin2), writes=(brtmp[s],))
                        p.op("vector", lambda v, tc=tc, s=s: v.tensor_tensor(
                            qraw[s][:, :], qraw[s][:, :],
                            cos2[:, tc * 512:(tc + 1) * 512], ALU.mult),
                            reads=(bqraw[s], bcos2), writes=(bqraw[s],))
                        p.op("vector", lambda v, g=g, tc=tc, s=s, dst=dst, bdst=bdst: v.tensor_tensor(
                            dst[:, g * T + tc * 512:g * T + (tc + 1) * 512],
                            qraw[s][:, :], rtmp[s][:, :], ALU.add),
                            reads=(bqraw[s], brtmp[s]), writes=(bdst,))
        p.op("vector", lambda v: v.memset(
            bass.AP(Vsb, 64, [[NT * 1040, 128], [1040, NT], [65, H], [1, 1]]), 1.0),
            reads=(), writes=(bV,))
        cm.emit_weight_transpose(w["Wo"], WT, bWT, wtmp, bwtmp, wtb, bwtb)

        qk_ps = [(sh.plog, (sh.bplogA, sh.bplogB)), (sh.pg, (sh.bpg[0], sh.bpg[1]))]
        for qc in range(4):
            for h in range(H):
                g = h // 2
                hr = (h % 2) * 64

                def qk(pt_i):
                    pr_ = (pt_i // 2) % 2
                    half = pt_i % 2
                    pst, pbufs = qk_ps[pr_]
                    p.op("tensor", lambda t, g=g, hr=hr, qc=qc, pt_i=pt_i, pst=pst, half=half: t.matmul(
                        pst[:, half * 512:(half + 1) * 512],
                        KT[hr:hr + 64, g * T + pt_i * 128:g * T + (pt_i + 1) * 128],
                        QT[hr:hr + 64, g * T + qc * 512:g * T + (qc + 1) * 512],
                        start=True, stop=True),
                        reads=(bKT, bQT), writes=(pbufs[half],))

                def expair(pair):
                    pr_ = pair % 2
                    pst, pbufs = qk_ps[pr_]
                    es = pair % 2
                    p.op("scalar", lambda sc, pst=pst, es=es: sc.activation(
                        expt[es][:, :], pst[:, 0:1024], AF.Exp),
                        reads=pbufs, writes=(bexpt[es],))

                def pv(pt_i):
                    es = (pt_i // 2) % 2
                    half = pt_i % 2
                    p.op("tensor", lambda t, h=h, pt_i=pt_i, es=es, half=half: t.matmul(
                        pav[0:65, :],
                        Vsb[:, pt_i * 1040 + h * 65:pt_i * 1040 + h * 65 + 65],
                        expt[es][:, half * 512:(half + 1) * 512],
                        start=(pt_i == 0), stop=(pt_i == NT - 1)),
                        reads=(bV, bexpt[es]), writes=(bpav,))

                for pair in range(NT // 2):
                    qk(2 * pair)
                    qk(2 * pair + 1)
                    expair(pair)
                    if pair > 0:
                        pv(2 * pair - 2)
                        pv(2 * pair - 1)
                pv(NT - 2)
                pv(NT - 1)
                avraw = sh.f32s[2]
                bavraw = sh.bf32s[2]
                p.op("scalar", lambda sc, avraw=avraw: sc.activation(
                    avraw[0:65, :], pav[0:65, :], AF.Identity),
                    reads=(bpav,), writes=(bavraw,))
                p.op("vector", lambda v, avraw=avraw: v.reciprocal(
                    recrow[0:1, :], avraw[64:65, :]),
                    reads=(bavraw,), writes=(brecrow,))
                p.op("tensor", lambda t: t.matmul(
                    prb[:, :], cm.onesf1[0:1, 0:64], recrow[0:1, :],
                    start=True, stop=True),
                    reads=(cm.bonesf1, brecrow), writes=(bprb,))
                p.op("scalar", lambda sc: sc.activation(
                    rbsb[:, :], prb[:, :], AF.Identity),
                    reads=(bprb,), writes=(brbsb,))
                p.op("vector", lambda v, g=g, hr=hr, avraw=avraw: v.tensor_tensor(
                    AT[hr:hr + 64, g * 512:(g + 1) * 512],
                    avraw[0:64, :], rbsb[:, :], ALU.mult),
                    reads=(bavraw, brbsb), writes=(bAT,))
            for ts_ in range(4):
                j = qc * 4 + ts_
                pso = []
                for oc in range(2):
                    s = oc
                    for k in range(NE):
                        p.op("tensor", lambda t, ts_=ts_, oc=oc, k=k, s=s: t.matmul(
                            pl_ap[s],
                            AT[:, k * 512 + ts_ * 128:k * 512 + (ts_ + 1) * 128],
                            WT[:, k * D + oc * 512:k * D + (oc + 1) * 512],
                            start=(k == 0), stop=False),
                            reads=(bAT, bWT), writes=(bpl[s],))
                    p.op("tensor", lambda t, oc=oc, s=s: t.matmul(
                        pl_ap[s],
                        cm.onesrow[0:1, 0:128],
                        brow[0:1, 3 * D + oc * 512:3 * D + (oc + 1) * 512],
                        start=False, stop=True),
                        reads=(cm.bonesrow, bbrow), writes=(bpl[s],))
                    pso.append((pl_ap[s], bpl[s]))
                out_toks.append(cm.residual_add_store(xin, xout, seq, j, pso))
    return out_toks



def emit_att0(p, sh, nc, xin, xout, w, gdram, rbd):
    cm = sh.cm
    lnt, blnt = sh.lnt, sh.blnt
    wtmp, bwtmp = sh.wtmp, sh.bwtmp
    Wsb, bWsb = sh.W1, sh.bW1
    Wrsb, bWrsb = sh.Wrsb, sh.bWrsb
    bndsb, bbnd = sh.bndsb, sh.bbnd
    brow, bbrow = sh.brow, sh.bbrow
    negtile, bneg = sh.negtile, sh.bneg
    QT, bQT = sh.A, sh.bA
    KT, bKT = sh.Bb, sh.bB
    Vsb, bV = sh.Cc, sh.bC_
    rslice, brslice = sh.rslice, sh.brslice
    rstage, brstage = sh.rstage, sh.brstage
    g2, bg2 = sh.g2, sh.bg2
    probs, bprobs = sh.probs, sh.bprobs
    probsT, bprobsT = sh.probsT, sh.bprobsT
    gstage, bgstage = sh.gstage, sh.bgstage
    AT, bAT = sh.AT0, sh.bAT0
    recrow, brecrow = sh.recrow, sh.brecrow
    rbsb, brbsb = sh.rbsb, sh.brbsb
    plog, bplog = sh.plog, sh.bplog
    pg = sh.pg
    bpg = sh.bpg
    pav, bpav = sh.pav, sh.bpav
    prb, bprb = sh.prb, sh.bprb
    NSLOT = 3
    bgslot = [Buf(f"gslot{i}") for i in range(NSLOT)]
    brbd = Buf("rbd")

    for (bn, coff, n) in (("bqv", 0, D), ("bprojv", D, D), ("brv", 2 * D, 160)):
        bsrc = w[bn]
        p.op("gpsimd", lambda e, bsrc=bsrc, n=n: e.dma_start(
            wtmp[0:1, 0:n], bass.AP(bsrc.tensor, 0, [[n, 1], [1, n]])),
            writes=(bwtmp,), is_dma=True)
        p.op("vector", lambda v, coff=coff, n=n: v.tensor_copy(
            brow[0:1, coff:coff + n], wtmp[0:1, 0:n]),
            reads=(bwtmp,), writes=(bbrow,))
    p.op("gpsimd", lambda e: e.dma_start(
        wtmp[0:NB, 0:1024], bass.AP(w["bndrev"].tensor, 0, [[1024, NB], [1, 1024]])),
        writes=(bwtmp,), is_dma=True)
    p.op("vector", lambda v: v.tensor_copy(bndsb[0:NB, :], wtmp[0:NB, 0:1024]),
         reads=(bwtmp,), writes=(bbnd,))
    cm.emit_weight_cast(w["Wr"], Wrsb, bWrsb, D, 160, wtmp, bwtmp)
    p.op("vector", lambda v: v.memset(negtile[:, :], NEG), writes=(bneg,))
    for sl in range(NSLOT):
        p.op("gpsimd", lambda e, sl=sl: e.dma_start(
            bass.AP(gdram, sl * 128 * 1279, [[1279, 128], [1, 128]]),
            negtile[:, :]),
            reads=(bneg,), writes=(bgslot[sl],), is_dma=True)
        p.op("gpsimd", lambda e, sl=sl: e.dma_start(
            bass.AP(gdram, sl * 128 * 1279 + 1152, [[1279, 128], [1, 127]]),
            negtile[:, 0:127]),
            reads=(bneg,), writes=(bgslot[sl],), is_dma=True)

    out_toks = []
    for seq in range(SEQ_PER_CORE):
        cm.emit_ln_to_lnt(xin, seq, lnt, blnt)

        for (wi, kind) in ((w["Wq"], "q"), (w["Wk"], "k"), (w["Wv"], "v")):
            cm.emit_weight_cast(wi, Wsb, bWsb, D, D, wtmp, bwtmp)
            if kind == "v":
                for j in range(NT):
                    for oc in range(2):
                        for cc in range(NE):
                            p.op("tensor", lambda t, j=j, oc=oc, cc=cc: t.matmul(
                                pg[:, oc * 512:(oc + 1) * 512],
                                lnt[:, cc * T + j * 128:cc * T + (j + 1) * 128],
                                Wsb[:, cc * D + oc * 512:cc * D + (oc + 1) * 512],
                                start=(cc == 0), stop=(cc == NE - 1)),
                                reads=(blnt, bWsb), writes=(bpg[oc],))
                        p.op("scalar", lambda sc, j=j, oc=oc: sc.activation(
                            bass.AP(Vsb, j * 1040 + oc * 8 * 65,
                                    [[NT * 1040, 128], [65, 8], [1, 64]]),
                            pg[:, oc * 512:(oc + 1) * 512], AF.Identity),
                            reads=(bpg[oc],), writes=(bV,))
            else:
                scale = 0.125 if kind == "q" else 1.0
                dst, bdst = (QT, bQT) if kind == "q" else (KT, bKT)
                has_bias = (kind == "q")
                for g in range(NE):
                    for tc in range(4):
                        oc2 = (g * 4 + tc) % 2
                        for cc in range(NE):
                            p.op("tensor", lambda t, g=g, tc=tc, cc=cc, oc2=oc2, has_bias=has_bias: t.matmul(
                                pg[:, oc2 * 512:(oc2 + 1) * 512],
                                Wsb[:, cc * D + g * 128:cc * D + (g + 1) * 128],
                                lnt[:, cc * T + tc * 512:cc * T + (tc + 1) * 512],
                                start=(cc == 0),
                                stop=(cc == NE - 1 and not has_bias)),
                                reads=(blnt, bWsb), writes=(bpg[oc2],))
                        if has_bias:
                            p.op("tensor", lambda t, g=g, oc2=oc2: t.matmul(
                                pg[:, oc2 * 512:(oc2 + 1) * 512],
                                brow[0:1, g * 128:(g + 1) * 128],
                                cm.onesrow[0:1, 0:512],
                                start=False, stop=True),
                                reads=(bbrow, cm.bonesrow), writes=(bpg[oc2],))
                        p.op("scalar", lambda sc, g=g, tc=tc, oc2=oc2, dst=dst, bdst=bdst, scale=scale: sc.activation(
                            dst[:, g * T + tc * 512:g * T + (tc + 1) * 512],
                            pg[:, oc2 * 512:(oc2 + 1) * 512], AF.Identity, scale=scale),
                            reads=(bpg[oc2],), writes=(bdst,))
        p.op("vector", lambda v: v.memset(
            bass.AP(Vsb, 64, [[NT * 1040, 128], [1040, NT], [65, H], [1, 1]]), 1.0),
            reads=(), writes=(bV,))
        cm.emit_weight_cast(w["Wproj"], Wsb, bWsb, D, D, wtmp, bwtmp)
        for half in range(2):
            for tc in range(4):
                for cc in range(NE):
                    p.op("tensor", lambda t, half=half, tc=tc, cc=cc: t.matmul(
                        pg[0:80, 0:512],
                        Wrsb[:, cc * 160 + half * 80:cc * 160 + half * 80 + 80],
                        lnt[:, cc * T + tc * 512:cc * T + (tc + 1) * 512],
                        start=(cc == 0), stop=False),
                        reads=(blnt, bWrsb), writes=(bpg[0],))
                p.op("tensor", lambda t, half=half: t.matmul(
                    pg[0:80, 0:512],
                    brow[0:1, 2 * D + half * 80:2 * D + half * 80 + 80],
                    cm.onesrow[0:1, 0:512],
                    start=False, stop=True),
                    reads=(bbrow, cm.bonesrow), writes=(bpg[0],))
                p.op("scalar", lambda sc: sc.activation(
                    rstage[0:80, 0:512], pg[0:80, 0:512], AF.Identity),
                    reads=(bpg[0],), writes=(brstage,))
                p.op("gpsimd", lambda e, half=half, tc=tc: e.dma_start(
                    bass.AP(rbd, half * 80 * T + tc * 512, [[T, 80], [1, 512]]),
                    rstage[0:80, 0:512]),
                    reads=(brstage,), writes=(brbd,), is_dma=True)

        idxs = [(i, h) for i in range(NT) for h in range(H)]

        def emit_g(idx):
            i, h = idxs[idx]
            sl = idx % NSLOT
            gs2 = idx % 2
            gs3 = idx % 3
            p.op("sync", lambda e, h=h, gs2=gs2: e.dma_start(
                rslice[gs2][0:NB, :],
                bass.AP(rbd, (h // 8 * 80 + (h % 8) * NB) * T, [[T, NB], [1, T]])),
                reads=(brbd,), writes=(brslice[gs2],), is_dma=True)
            for half in range(2):
                p.op("tensor", lambda t, i=i, half=half, gs2=gs2: t.matmul(
                    pg[:, half * 512:(half + 1) * 512],
                    rslice[gs2][0:NB, i * 128:(i + 1) * 128],
                    bndsb[0:NB, half * 512:(half + 1) * 512],
                    start=True, stop=True),
                    reads=(brslice[gs2], bbnd), writes=(bpg[half],))
            p.op("vector", lambda v, gs2=gs2: v.tensor_copy(gstage[gs2][:, :], pg[:, :]),
                reads=(bpg[0], bpg[1]), writes=(bgstage[gs2],))
            p.op("gpsimd", lambda e, sl=sl, gs2=gs2: e.dma_start(
                bass.AP(gdram, sl * 128 * 1279 + 128, [[1279, 128], [1, 1024]]),
                gstage[gs2][:, :]),
                reads=(bgstage[gs2],), writes=(bgslot[sl],), is_dma=True)
            p.op("sync", lambda e, sl=sl, gs3=gs3: e.dma_start(
                g2[gs3][:, :],
                bass.AP(gdram, sl * 128 * 1279 + 127, [[1278, 128], [1, 1152]])),
                reads=(bgslot[sl],), writes=(bg2[gs3],), is_dma=True)

        bplogch = {0: sh.bplogA, 512: sh.bplogB, 1024: sh.bplogC}

        def emit_qk(idx):
            i, h = idxs[idx]
            gs = idx % 3
            g = h // 2
            hr = (h % 2) * 64
            nb = min(i, 8) + 1
            bc = (9 - nb) * 128
            p0 = (i - (nb - 1)) * 128
            ps_ = idx % 2
            for (c0, c1) in psum_chunks(bc, 1152):
                bch = bplogch[(c0 // 512) * 512]
                p.op("tensor", lambda t, g=g, hr=hr, i=i, c0=c0, c1=c1, bc=bc, p0=p0: t.matmul(
                    plog[:, c0:c1],
                    QT[hr:hr + 64, g * T + i * 128:g * T + (i + 1) * 128],
                    KT[hr:hr + 64, g * T + p0 + (c0 - bc):g * T + p0 + (c1 - bc)],
                    start=True, stop=False),
                    reads=(bQT, bKT), writes=(bch,))
                p.op("tensor", lambda t, c0=c0, c1=c1, gs=gs: t.matmul(
                    plog[:, c0:c1],
                    cm.ident[:, :],
                    g2[gs][:, c0:c1],
                    start=False, stop=True),
                    reads=(cm.bident, bg2[gs]), writes=(bch,))
                p.op("scalar", lambda sc, c0=c0, c1=c1, ps_=ps_: sc.activation(
                    probs[ps_][:, c0:c1], plog[:, c0:c1], AF.Exp),
                    reads=(bch,), writes=(bprobs[ps_],))

        def emit_tail(idx):
            i, h = idxs[idx]
            g = h // 2
            hr = (h % 2) * 64
            nb = min(i, 8) + 1
            bc = (9 - nb) * 128
            ps_ = idx % 2
            blocks = list(range(bc // 128, 9))
            si = 0
            for b2 in range(0, len(blocks), 4):
                grp = blocks[b2:b2 + 4]
                pslot = si % len(cm.ppt)
                si += 1
                for k, m in enumerate(grp):
                    p.op("tensor", lambda t, m=m, k=k, pslot=pslot, ps_=ps_: t.transpose(
                        cm.ppt[pslot][:, k * 128:(k + 1) * 128],
                        probs[ps_][:, m * 128:(m + 1) * 128], cm.ident[:, :]),
                        reads=(bprobs[ps_], cm.bident), writes=(cm.bppt[pslot],))
                eng = "vector" if (si % 2) else "scalar"
                for k, m in enumerate(grp):
                    dstp = probsT[:, m * 128:(m + 1) * 128]
                    srcp = cm.ppt[pslot][:, k * 128:(k + 1) * 128]
                    if eng == "vector":
                        p.op(eng, lambda en, dstp=dstp, srcp=srcp: en.tensor_copy(dstp, srcp),
                             reads=(cm.bppt[pslot],), writes=(bprobsT,))
                    else:
                        p.op(eng, lambda en, dstp=dstp, srcp=srcp: en.copy(dstp, srcp),
                             reads=(cm.bppt[pslot],), writes=(bprobsT,))
            for k, m in enumerate(blocks):
                jj = i - (nb - 1) + k
                p.op("tensor", lambda t, jj=jj, h=h, m=m, k=k, nb=nb: t.matmul(
                    pav[0:65, 0:128],
                    Vsb[:, jj * 1040 + h * 65:jj * 1040 + h * 65 + 65],
                    probsT[:, m * 128:(m + 1) * 128],
                    start=(k == 0), stop=(k == nb - 1)),
                    reads=(bV, bprobsT), writes=(bpav,))
            avraw = sh.f32s[2]
            bavraw = sh.bf32s[2]
            p.op("scalar", lambda sc, avraw=avraw: sc.activation(
                avraw[0:65, 0:128], pav[0:65, 0:128], AF.Identity),
                reads=(bpav,), writes=(bavraw,))
            p.op("vector", lambda v, avraw=avraw: v.reciprocal(
                recrow[0:1, 0:128], avraw[64:65, 0:128]),
                reads=(bavraw,), writes=(brecrow,))
            p.op("tensor", lambda t: t.matmul(
                prb[0:64, 0:128], cm.onesf1[0:1, 0:64], recrow[0:1, 0:128],
                start=True, stop=True),
                reads=(cm.bonesf1, brecrow), writes=(bprb,))
            p.op("scalar", lambda sc: sc.activation(
                rbsb[0:64, 0:128], prb[0:64, 0:128], AF.Identity),
                reads=(bprb,), writes=(brbsb,))
            p.op("vector", lambda v, g=g, hr=hr, avraw=avraw: v.tensor_tensor(
                AT[hr:hr + 64, g * 128:(g + 1) * 128],
                avraw[0:64, 0:128], rbsb[0:64, 0:128], ALU.mult),
                reads=(bavraw, brbsb), writes=(bAT,))

        def emit_outproj(i):
            pso = []
            for oc in range(2):
                bch = bplogch[oc * 512]
                for k in range(NE):
                    p.op("tensor", lambda t, oc=oc, k=k: t.matmul(
                        plog[:, oc * 512:(oc + 1) * 512],
                        AT[:, k * 128:(k + 1) * 128],
                        Wsb[:, k * D + oc * 512:k * D + (oc + 1) * 512],
                        start=(k == 0), stop=False),
                        reads=(bAT, bWsb), writes=(bch,))
                p.op("tensor", lambda t, oc=oc: t.matmul(
                    plog[:, oc * 512:(oc + 1) * 512],
                    cm.onesrow[0:1, 0:128],
                    brow[0:1, D + oc * 512:D + (oc + 1) * 512],
                    start=False, stop=True),
                    reads=(cm.bonesrow, bbrow), writes=(bch,))
                pso.append((plog[:, oc * 512:(oc + 1) * 512], bch))
            out_toks.append(cm.residual_add_store(xin, xout, seq, i, pso))

        emit_g(0)
        emit_g(1)
        for idx in range(len(idxs)):
            i, h = idxs[idx]
            if idx + 2 < len(idxs):
                emit_g(idx + 2)
            emit_qk(idx)
            if idx > 0:
                emit_tail(idx - 1)
                pi_, ph_ = idxs[idx - 1]
                if ph_ == H - 1:
                    emit_outproj(pi_)
        emit_tail(len(idxs) - 1)
        emit_outproj(NT - 1)
    return out_toks



FFN0_W = ("theta", "phi", "theta_n", "phi_n")
FFN1_W = ("W_r", "W_k", "W_v", "mu_r", "mu_k")
ATT1_W = ("Wq", "Wk", "Wv", "Wo", "bq", "bk", "bv", "bo")
ATT0_W = ("Wq", "Wk", "Wv", "Wproj", "Wr", "bqv", "bprojv", "brv", "bndrev")
_SHAPES = {
    "theta": D * D, "phi": D * D, "theta_n": D * D, "phi_n": D * D,
    "W_r": D * D, "W_k": D * D, "W_v": D * D, "mu_r": D, "mu_k": D,
    "Wq": D * D, "Wk": D * D, "Wv": D * D, "Wo": D * D, "Wproj": D * D,
    "Wr": D * H * NB, "bq": D, "bk": D, "bv": D, "bo": D,
    "bqv": D, "bprojv": D, "brv": H * NB, "bndrev": NB * 1024,
}
ALL_CONSTS = ["C_IDENT", "C_ONESROW", "C_ONESCOL", "C_ONESF1",
              "C_COS2", "C_SIN2", "C_PROT"]


def _w_aps(nc, names, prefix=""):
    return {n: nc.dram_tensor(prefix + n, [_SHAPES[n]], F32, kind="ExternalInput").ap()
            for n in names}


def _build_single(which):
    nc = bass.Bass("TRN2", target_bir_lowering=False)
    xin = nc.dram_tensor("xin", [SEQ_PER_CORE * T * D], F32, kind="ExternalInput").ap()
    names = {"ffn0": FFN0_W, "ffn1": FFN1_W, "att1": ATT1_W, "att0": ATT0_W}[which]
    w = _w_aps(nc, names)
    consts = add_consts(nc, ALL_CONSTS)
    xout = nc.dram_tensor("xout", [SEQ_PER_CORE * T * D], F32, kind="ExternalOutput").ap()
    p = Sched()
    with ExitStack() as stack:
        ec = stack.enter_context
        sh = Shared(p, nc, ec, consts)
        if which == "ffn0":
            toks = emit_ffn0(p, sh, xin, xout, w)
        elif which == "ffn1":
            toks = emit_ffn1(p, sh, xin, xout, w)
        elif which == "att1":
            toks = emit_att1(p, sh, xin, xout, w, consts)
        else:
            gdram = nc.dram_tensor("gdram", [3 * 128 * 1279], BF16, kind="Internal")
            rbd = nc.dram_tensor("rbd", [2 * 80 * T], BF16, kind="Internal")
            toks = emit_att0(p, sh, nc, xin, xout, w, gdram, rbd)
        p.barrier("sync", toks)
        block = ec(nc.Block())
        p.finalize(nc, block, stack)
    return nc


def build_ffn0():
    return _build_single("ffn0")


def build_ffn1():
    return _build_single("ffn1")


def build_att1():
    return _build_single("att1")


def build_att0():
    return _build_single("att0")


def build_fused():
    nc = bass.Bass("TRN2", target_bir_lowering=False)
    xin = nc.dram_tensor("xin", [SEQ_PER_CORE * T * D], F32, kind="ExternalInput").ap()
    w_f0 = _w_aps(nc, FFN0_W, "f0_")
    w_a0 = _w_aps(nc, ATT0_W, "a0_")
    w_f1 = _w_aps(nc, FFN1_W, "f1_")
    w_a1 = _w_aps(nc, ATT1_W, "a1_")
    consts = add_consts(nc, ALL_CONSTS)
    xout = nc.dram_tensor("xout", [SEQ_PER_CORE * T * D], F32, kind="ExternalOutput").ap()
    xb = [nc.dram_tensor(f"xb{i}", [SEQ_PER_CORE * T * D], F32, kind="Internal").ap()
          for i in range(5)]
    gdram = nc.dram_tensor("gdram", [3 * 128 * 1279], BF16, kind="Internal")
    rbd = nc.dram_tensor("rbd", [2 * 80 * T], BF16, kind="Internal")

    p = Sched()
    with ExitStack() as stack:
        ec = stack.enter_context
        sh = Shared(p, nc, ec, consts)
        emit_ffn0(p, sh, xin, xb[0], w_f0)
        emit_att0(p, sh, nc, xb[0], xb[1], w_a0, gdram, rbd)
        emit_ffn0(p, sh, xb[1], xb[2], w_f0)
        emit_ffn1(p, sh, xb[2], xb[3], w_f1)
        emit_att1(p, sh, xb[3], xb[4], w_a1, consts)
        toks = emit_ffn1(p, sh, xb[4], xout, w_f1)
        p.barrier("sync", toks)
        block = ec(nc.Block())
        p.finalize(nc, block, stack)
    return nc



_cache = {}


def _get(name, builder):
    if name not in _cache:
        _cache[name] = builder()
    return _cache[name]


def _input_maps(inputs):
    consts = host_consts()
    xs = [np.ascontiguousarray(s).reshape(-1) for s in
          np.split(np.ascontiguousarray(inputs["L0_attn_X"]), NCORES, axis=0)]
    ffn0_w = {
        "theta": inputs["L0_ffn_theta"].reshape(-1),
        "phi": inputs["L0_ffn_phi"].reshape(-1),
        "theta_n": inputs["L0_ffn_theta_noise"].reshape(-1),
        "phi_n": inputs["L0_ffn_phi_noise"].reshape(-1),
    }
    att0_w = {
        "Wq": inputs["L0_attn_Wq"].reshape(-1),
        "Wk": inputs["L0_attn_Wk"].reshape(-1),
        "Wv": inputs["L0_attn_Wv"].reshape(-1),
        "Wproj": inputs["L0_attn_Wproj"].reshape(-1),
        "Wr": inputs["L0_attn_Wr"].reshape(-1),
        "bqv": inputs["L0_attn_bq"].reshape(-1),
        "bprojv": inputs["L0_attn_bproj"].reshape(-1),
        "brv": inputs["L0_attn_br"].reshape(-1),
        "bndrev": np.ascontiguousarray(inputs["L0_attn_b_nd"][:, ::-1]).reshape(-1),
    }
    ffn1_w = {
        "W_r": inputs["L1_ffn_W_r"].reshape(-1),
        "W_k": inputs["L1_ffn_W_k"].reshape(-1),
        "W_v": inputs["L1_ffn_W_v"].reshape(-1),
        "mu_r": inputs["L1_ffn_mu_r"].reshape(-1),
        "mu_k": inputs["L1_ffn_mu_k"].reshape(-1),
    }
    att1_w = {
        "Wq": inputs["L1_attn_Wq"].reshape(-1),
        "Wk": inputs["L1_attn_Wk"].reshape(-1),
        "Wv": inputs["L1_attn_Wv"].reshape(-1),
        "Wo": inputs["L1_attn_Wo"].reshape(-1),
        "bq": inputs["L1_attn_bq"].reshape(-1),
        "bk": inputs["L1_attn_bk"].reshape(-1),
        "bv": inputs["L1_attn_bv"].reshape(-1),
        "bo": inputs["L1_attn_bo"].reshape(-1),
    }
    return consts, xs, ffn0_w, att0_w, ffn1_w, att1_w


def kernel(**inputs):
    consts, xs, ffn0_w, att0_w, ffn1_w, att1_w = _input_maps(inputs)
    wmap = {}
    for pref, w in (("f0_", ffn0_w), ("a0_", att0_w), ("f1_", ffn1_w), ("a1_", att1_w)):
        for k, v in w.items():
            wmap[pref + k] = v
    for k in ALL_CONSTS:
        wmap[k] = consts[k]
    nc = _get("fused", build_fused)
    in_maps = [dict(wmap, xin=xs[c]) for c in range(NCORES)]
    res = run_bass_kernel_spmd(nc, in_maps, core_ids=list(range(NCORES)))
    out = np.concatenate(
        [np.asarray(r["xout"], np.float32).reshape(SEQ_PER_CORE, T, D)
         for r in res.results], axis=0)
    return out.astype(np.float32)


def kernel_unfused(**inputs):
    consts, xs, ffn0_w, att0_w, ffn1_w, att1_w = _input_maps(inputs)
    cmap = {k: consts[k] for k in ALL_CONSTS}

    def launch(nc, wm):
        nonlocal xs
        in_maps = [dict(cmap, xin=xs[c], **wm) for c in range(NCORES)]
        res = run_bass_kernel_spmd(nc, in_maps, core_ids=list(range(NCORES)))
        xs = [np.asarray(r["xout"], np.float32) for r in res.results]

    launch(_get("ffn0", build_ffn0), ffn0_w)
    launch(_get("att0", build_att0), att0_w)
    launch(_get("ffn0", build_ffn0), ffn0_w)
    launch(_get("ffn1", build_ffn1), ffn1_w)
    launch(_get("att1", build_att1), att1_w)
    launch(_get("ffn1", build_ffn1), ffn1_w)
    out = np.concatenate([x.reshape(SEQ_PER_CORE, T, D) for x in xs], axis=0)
    return out.astype(np.float32)


# version 07
# speedup vs baseline: 1.1671x; 1.1671x over previous
import math
from contextlib import ExitStack

import numpy as np
import ml_dtypes

import concourse.bass as bass
import concourse.mybir as mybir
from concourse.bass_utils import run_bass_kernel_spmd

F32 = mybir.dt.float32
BF16 = mybir.dt.bfloat16
AF = mybir.ActivationFunctionType
ALU = mybir.AluOpType
BF = ml_dtypes.bfloat16

B, T, D = 16, 2048, 1024
NCORES = 8
SEQ_PER_CORE = B // NCORES
NT = T // 128
NE = D // 128
H = 16
DH = 64
NB = 10
NEG = -1.0e9
ROPE_BASE = 10000.0

ENGINES = ("tensor", "vector", "scalar", "gpsimd", "sync")

SERIAL = False
SAME_ENGINE_WAITS = True


class Buf:
    __slots__ = ("name", "writers", "readers", "overlaps")

    def __init__(self, name="", overlaps=()):
        self.name = name
        self.writers = []
        self.readers = []
        self.overlaps = list(overlaps)


class ColView:

    def __init__(self, t, base, width):
        self.t = t
        self.base = base
        self.width = width

    def __getitem__(self, key):
        ps, cs = key
        start = self.base if cs.start is None else self.base + cs.start
        stop = self.base + (self.width if cs.stop is None else cs.stop)
        return self.t[ps, start:stop]


class _Op:
    __slots__ = ("emit", "deps", "marked", "is_dma", "semval")

    def __init__(self, emit, deps, is_dma):
        self.emit = emit
        self.deps = deps
        self.marked = False
        self.is_dma = is_dma
        self.semval = None


class Sched:
    def __init__(self):
        self.ops = {e: [] for e in ENGINES}
        self.last = None

    def op(self, eng, emit, reads=(), writes=(), is_dma=False):
        deps = []
        for b in reads:
            deps.extend(b.writers)
            for o in b.overlaps:
                deps.extend(o.writers)
        for b in writes:
            deps.extend(b.readers)
            deps.extend(b.writers)
            for o in b.overlaps:
                deps.extend(o.readers)
                deps.extend(o.writers)
        if SERIAL and self.last is not None:
            deps.append(self.last)
        uniq = []
        seen = set()
        for d in deps:
            de, di = d
            if (de, di) in seen:
                continue
            seen.add((de, di))
            dop = self.ops[de][di]
            if de == eng and not dop.is_dma and not SERIAL:
                if eng == "tensor" or not SAME_ENGINE_WAITS:
                    continue
            uniq.append(d)
        idx = len(self.ops[eng])
        rec = _Op(emit, uniq, is_dma)
        if is_dma:
            rec.marked = True
        self.ops[eng].append(rec)
        tok = (eng, idx)
        for d in uniq:
            self.ops[d[0]][d[1]].marked = True
        for b in reads:
            b.readers.append(tok)
        for b in writes:
            b.writers = [tok]
            b.readers = []
        self.last = tok
        return tok

    def barrier(self, eng, tokens):
        for t in tokens:
            self.ops[t[0]][t[1]].marked = True
        rec = _Op(None, list(dict.fromkeys(tokens)), False)
        self.ops[eng].append(rec)

    def finalize(self, nc, block, stack):
        sems = {}
        for e in ENGINES:
            if any(o.marked for o in self.ops[e]):
                sems[e] = stack.enter_context(nc.semaphore(f"sem_{e}"))
            cnt = 0
            for o in self.ops[e]:
                if o.marked:
                    cnt += 16 if o.is_dma else 1
                    o.semval = cnt

        def mk(e):
            def run(eng):
                hw = {}
                for o in self.ops[e]:
                    for de, di in o.deps:
                        v = self.ops[de][di].semval
                        if hw.get(de, 0) < v:
                            eng.wait_ge(sems[de], v)
                            hw[de] = v
                    if o.emit is None:
                        continue
                    inst = o.emit(eng)
                    if o.marked:
                        inst.then_inc(sems[e], 16 if o.is_dma else 1)
            return run

        for e in ENGINES:
            if self.ops[e]:
                getattr(block, e)(mk(e))


def psum_chunks(c0, c1):
    out = []
    c = c0
    while c < c1:
        nxt = min(c1, (c // 512 + 1) * 512)
        out.append((c, nxt))
        c = nxt
    return out



def host_consts():
    c = {}
    c["C_IDENT"] = np.eye(128, dtype=np.float32).astype(BF)
    c["C_ONESROW"] = np.ones((1, 512), np.float32).astype(BF)
    c["C_ONESCOL"] = np.ones((128, 1), np.float32).astype(BF)
    c["C_ONESF1"] = np.ones((1, 64), np.float32)
    inv = 1.0 / ROPE_BASE ** (np.arange(0, DH, 2, dtype=np.float32) / DH)
    tpos = np.arange(T, dtype=np.float32)
    freqs = np.einsum("i,j->ij", tpos, inv)
    emb = np.concatenate([freqs, freqs], axis=-1)
    cosf = np.cos(emb).T.astype(np.float32)
    sinf = np.sin(emb).T.astype(np.float32)
    c["C_COS2"] = np.concatenate([cosf, cosf], 0).astype(BF)
    c["C_SIN2"] = np.concatenate([sinf, sinf], 0).astype(BF)
    p64 = np.zeros((64, 64), np.float32)
    for i in range(32):
        p64[2 * i + 1, 2 * i] = -1.0
        p64[2 * i, 2 * i + 1] = 1.0
    P = np.zeros((128, 128), np.float32)
    P[:64, :64] = p64
    P[64:, 64:] = p64
    c["C_PROT"] = P.astype(BF)
    return c


CONST_SHAPES = {
    "C_IDENT": ([128, 128], BF16),
    "C_ONESROW": ([1, 512], BF16),
    "C_ONESCOL": ([128, 1], BF16),
    "C_ONESF1": ([1, 64], F32),
    "C_COS2": ([128, T], BF16),
    "C_SIN2": ([128, T], BF16),
    "C_PROT": ([128, 128], BF16),
}


def add_consts(nc, names):
    aps = {}
    for n in names:
        s, d = CONST_SHAPES[n]
        aps[n] = nc.dram_tensor(n, s, d, kind="ExternalInput").ap()
    return aps



class Common:
    def __init__(self, p, nc, ec, consts, n_ppt=2):
        self.p = p
        self.nc = nc
        self.consts = consts
        self.sh = None
        self.xtile = [ec(nc.sbuf_tensor(f"xtile{i}", [128, D], F32)) for i in range(2)]
        self.bxtile = [Buf(f"xtile{i}") for i in range(2)]
        self.sq = None
        self.bsq = Buf("lnsq")
        self.cols = ec(nc.sbuf_tensor("lncols", [128, 80], F32))
        self.bsums, self.bsumsq, self.bsmall = Buf("sums"), Buf("sumsq"), Buf("small")
        self.lntile = ec(nc.sbuf_tensor("lntile", [128, D], BF16))
        self.blntile = Buf("lntile")
        self.ident = ec(nc.sbuf_tensor("identc", [128, 128], BF16))
        self.bident = Buf("ident")
        self.ppt = [ec(nc.psum_tensor(f"pptc{i}", [128, 512], BF16)) for i in range(n_ppt)]
        self.bppt = [Buf(f"ppt{i}") for i in range(n_ppt)]
        self.onesrow = ec(nc.sbuf_tensor("onesrowc", [1, 512], BF16))
        self.bonesrow = Buf("onesrow")
        self.onescol = ec(nc.sbuf_tensor("onescolc", [128, 1], BF16))
        self.bonescol = Buf("onescol")
        self.onesf1 = ec(nc.sbuf_tensor("onesf1c", [1, 64], F32))
        self.bonesf1 = Buf("onesf1")
        p.op("sync", lambda e: e.dma_start(self.ident[:, :], consts["C_IDENT"]),
             writes=(self.bident,), is_dma=True)
        p.op("sync", lambda e: e.dma_start(self.onesrow[:, :], consts["C_ONESROW"]),
             writes=(self.bonesrow,), is_dma=True)
        p.op("sync", lambda e: e.dma_start(self.onescol[:, :], consts["C_ONESCOL"]),
             writes=(self.bonescol,), is_dma=True)
        p.op("sync", lambda e: e.dma_start(self.onesf1[:, :], consts["C_ONESF1"]),
             writes=(self.bonesf1,), is_dma=True)

    def load_xtile(self, xin, seq, j, slot, q="sync"):
        off = seq * T * D + j * 128 * D
        reads = (self.sh.xbuf_of(xin, seq, j),) if self.sh is not None else ()
        self.p.op(q, lambda e, off=off, slot=slot: e.dma_start(
            self.xtile[slot][:, :], bass.AP(xin.tensor, off, [[D, 128], [1, D]])),
            reads=reads, writes=(self.bxtile[slot],), is_dma=True)

    def emit_ln_to_lnt(self, xin, seq, lnt, blnt):
        p = self.p
        cols = self.cols
        for j in range(NT):
            s = j % 2
            self.load_xtile(xin, seq, j, s)
            p.op("vector", lambda v, j=j, s=s: v.reduce_sum(
                cols[:, j:j + 1], self.xtile[s][:, :], mybir.AxisListType.X),
                reads=(self.bxtile[s],), writes=(self.bsums,))
            p.op("scalar", lambda sc, j=j, s=s: sc.activation(
                self.xtile[s][:, :], self.xtile[s][:, :], AF.Square,
                accum_out=cols[:, 16 + j:17 + j]),
                reads=(self.bxtile[s],), writes=(self.bxtile[s], self.bsumsq))
        p.op("vector", lambda v: v.tensor_scalar(
            cols[:, 32:48], cols[:, 0:16], 1.0 / D, None, ALU.mult),
            reads=(self.bsums,), writes=(self.bsmall,))
        p.op("vector", lambda v: v.tensor_scalar(
            cols[:, 48:64], cols[:, 16:32], 1.0 / D, None, ALU.mult),
            reads=(self.bsumsq,), writes=(self.bsmall,))
        p.op("vector", lambda v: v.scalar_tensor_tensor(
            cols[:, 64:80], cols[:, 32:48], -1.0, cols[:, 32:48], ALU.mult, ALU.mult),
            reads=(self.bsmall,), writes=(self.bsmall,))
        p.op("vector", lambda v: v.tensor_tensor(
            cols[:, 64:80], cols[:, 48:64], cols[:, 64:80], ALU.add),
            reads=(self.bsmall,), writes=(self.bsmall,))
        p.op("vector", lambda v: v.tensor_scalar(
            cols[:, 64:80], cols[:, 64:80], 1e-5, None, ALU.add),
            reads=(self.bsmall,), writes=(self.bsmall,))
        p.op("vector", lambda v: v.reciprocal(cols[:, 64:80], cols[:, 64:80]),
            reads=(self.bsmall,), writes=(self.bsmall,))
        p.op("scalar", lambda sc: sc.activation(
            cols[:, 64:80], cols[:, 64:80], AF.Sqrt),
            reads=(self.bsmall,), writes=(self.bsmall,))
        si = 0
        for j in range(NT):
            s = j % 2
            self.load_xtile(xin, seq, j, s)
            p.op("vector", lambda v, j=j, s=s: v.tensor_scalar(
                self.lntile[:, :], self.xtile[s][:, :],
                cols[:, 32 + j:33 + j], cols[:, 64 + j:65 + j],
                ALU.subtract, ALU.mult),
                reads=(self.bxtile[s], self.bsmall), writes=(self.blntile,))
            for e2 in range(0, NE, 4):
                ps = si % len(self.ppt)
                si += 1
                for k in range(4):
                    e = e2 + k
                    p.op("tensor", lambda t, e=e, ps=ps, k=k: t.transpose(
                        self.ppt[ps][:, k * 128:(k + 1) * 128],
                        self.lntile[:, e * 128:(e + 1) * 128], self.ident[:, :]),
                        reads=(self.blntile, self.bident), writes=(self.bppt[ps],))
                eng = "vector" if (si % 2) else "scalar"
                for k in range(4):
                    e = e2 + k
                    dst = lnt[:, e * T + j * 128:e * T + (j + 1) * 128]
                    src = self.ppt[ps][:, k * 128:(k + 1) * 128]
                    if eng == "vector":
                        p.op(eng, lambda en, dst=dst, src=src: en.tensor_copy(dst, src),
                             reads=(self.bppt[ps],), writes=(blnt,))
                    else:
                        p.op(eng, lambda en, dst=dst, src=src: en.copy(dst, src),
                             reads=(self.bppt[ps],), writes=(blnt,))

    def emit_weight_cast(self, wdram, wsb, bwsb, rows, cols, wtmp, bwtmp):
        p = self.p
        nr = rows // 128
        for r in range(nr):
            off = r * 128 * cols
            p.op("gpsimd", lambda e, off=off, cols=cols: e.dma_start(
                wtmp[:, 0:cols], bass.AP(wdram.tensor, off, [[cols, 128], [1, cols]])),
                writes=(bwtmp,), is_dma=True)
            p.op("vector", lambda v, r=r, cols=cols: v.tensor_copy(
                wsb[:, r * cols:(r + 1) * cols], wtmp[:, 0:cols]),
                reads=(bwtmp,), writes=(bwsb,))

    def emit_weight_transpose(self, wdram, wtsb, bwtsb, wtmp, bwtmp, wtb, bwtb):
        p = self.p
        si = 0
        for r in range(NE):
            off = r * 128 * D
            p.op("gpsimd", lambda e, off=off: e.dma_start(
                wtmp[:, 0:D], bass.AP(wdram.tensor, off, [[D, 128], [1, D]])),
                writes=(bwtmp,), is_dma=True)
            p.op("vector", lambda v: v.tensor_copy(wtb[:, 0:D], wtmp[:, 0:D]),
                 reads=(bwtmp,), writes=(bwtb,))
            for c2 in range(0, NE, 4):
                ps = si % len(self.ppt)
                si += 1
                for k in range(4):
                    c = c2 + k
                    p.op("tensor", lambda t, c=c, ps=ps, k=k: t.transpose(
                        self.ppt[ps][:, k * 128:(k + 1) * 128],
                        wtb[:, c * 128:(c + 1) * 128], self.ident[:, :]),
                        reads=(bwtb, self.bident), writes=(self.bppt[ps],))
                eng = "vector" if (si % 2) else "scalar"
                for k in range(4):
                    c = c2 + k
                    dst = wtsb[:, c * D + r * 128:c * D + (r + 1) * 128]
                    src = self.ppt[ps][:, k * 128:(k + 1) * 128]
                    if eng == "vector":
                        p.op(eng, lambda en, dst=dst, src=src: en.tensor_copy(dst, src),
                             reads=(self.bppt[ps],), writes=(bwtsb,))
                    else:
                        p.op(eng, lambda en, dst=dst, src=src: en.copy(dst, src),
                             reads=(self.bppt[ps],), writes=(bwtsb,))

    def residual_add_store(self, xin, xout, seq, j, psums, scale=None):
        p = self.p
        s = j % 2
        self.load_xtile(xin, seq, j, s)
        for oc, (ps, bps) in enumerate(psums):
            if scale is None:
                p.op("vector", lambda v, oc=oc, s=s, ps=ps, bps=bps: v.tensor_tensor(
                    self.xtile[s][:, oc * 512:(oc + 1) * 512],
                    self.xtile[s][:, oc * 512:(oc + 1) * 512], ps, ALU.add),
                    reads=(bps, self.bxtile[s]), writes=(self.bxtile[s],))
            else:
                p.op("vector", lambda v, oc=oc, s=s, ps=ps, bps=bps: v.scalar_tensor_tensor(
                    self.xtile[s][:, oc * 512:(oc + 1) * 512],
                    ps, scale, self.xtile[s][:, oc * 512:(oc + 1) * 512],
                    ALU.mult, ALU.add),
                    reads=(bps, self.bxtile[s]), writes=(self.bxtile[s],))
        off = seq * T * D + j * 128 * D
        writes = (self.sh.xbuf_of(xout, seq, j),) if self.sh is not None else ()
        return p.op("sync", lambda e, off=off, s=s: e.dma_start(
            bass.AP(xout.tensor, off, [[D, 128], [1, D]]), self.xtile[s][:, :]),
            reads=(self.bxtile[s],), writes=writes, is_dma=True)



class Shared:
    def __init__(self, p, nc, ec, consts):
        self.p = p
        self.nc = nc
        self.cm = Common(p, nc, ec, consts, n_ppt=1)
        self.cm.sh = self
        self.lnt = ec(nc.sbuf_tensor("lnt", [128, NE * T], BF16))
        self.blnt = Buf("lnt")
        self.A = ec(nc.sbuf_tensor("arA", [128, NE * T], BF16))
        self.bA = Buf("arA")
        self.Bb = ec(nc.sbuf_tensor("arB", [128, NE * T], BF16))
        self.bB = Buf("arB")
        self.Cc = ec(nc.sbuf_tensor("arC", [128, NT * (H * 65)], BF16))
        self.bC_ = Buf("arC")
        self.W1 = ec(nc.sbuf_tensor("W1", [128, NE * D], BF16))
        self.bW1 = Buf("W1")
        self.W2 = None
        self.bW2 = Buf("W2")
        self.Wrsb = ec(nc.sbuf_tensor("Wrsb", [128, NE * (H * NB)], BF16))
        self.bWrsb = Buf("Wrsb")
        self.wtmp = ec(nc.sbuf_tensor("wtmp", [128, D], F32))
        self.bwtmp = Buf("wtmp")
        self.wtb = ec(nc.sbuf_tensor("wtb", [128, D], BF16))
        self.bwtb = Buf("wtb")
        self.halfpi = ec(nc.sbuf_tensor("halfpi", [128, 1], F32))
        self.bhalfpi = Buf("halfpi")
        self.f32s = [ec(nc.sbuf_tensor(f"f32s{i}", [128, 512], F32)) for i in range(4)]
        self.bf32s = [Buf(f"f32s{i}") for i in range(4)]
        self.krelu = [ec(nc.sbuf_tensor(f"krelu{i}", [128, 512], BF16)) for i in range(2)]
        self.bkrelu = [Buf(f"krelu{i}") for i in range(2)]
        self.rsb = [ec(nc.sbuf_tensor(f"rsb{i}", [128, 512], BF16)) for i in range(2)]
        self.brsb = [Buf(f"rsb{i}") for i in range(2)]
        self.expt = [ec(nc.sbuf_tensor(f"expt{i}", [128, 1024], BF16)) for i in range(2)]
        self.bexpt = [Buf(f"expt{i}") for i in range(2)]
        self.mus = ec(nc.sbuf_tensor("mus", [128, 32], F32))
        self.bmus = Buf("mus")
        self.brow = ec(nc.sbuf_tensor("brow", [1, 4 * D], BF16))
        self.bbrow = Buf("brow")
        self.prot = ec(nc.sbuf_tensor("prot", [128, 128], BF16))
        self.bprot = Buf("prot")
        self.bndsb = ec(nc.sbuf_tensor("bndsb", [NB, 1024], BF16))
        self.bbnd = Buf("bnd")
        self.rstage = ec(nc.sbuf_tensor("rstage", [80, 512], BF16))
        self.brstage = Buf("rstage")
        self.negtile = ec(nc.sbuf_tensor("negtile", [128, 128], BF16))
        self.bneg = Buf("negtile")
        self.rslice = [ec(nc.sbuf_tensor(f"rslice{i}", [NB, T], BF16)) for i in range(2)]
        self.brslice = [Buf(f"rslice{i}") for i in range(2)]
        self.recrow = ec(nc.sbuf_tensor("recrow", [1, 512], F32))
        self.brecrow = Buf("recrow")
        self.rbsb = ec(nc.sbuf_tensor("rbsb", [64, 512], F32))
        self.brbsb = Buf("rbsb")
        self.pg = ec(nc.psum_tensor("pg", [128, 1024], F32))
        self.bpg = [Buf("pg0"), Buf("pg1")]
        self.plog = ec(nc.psum_tensor("plog", [128, 1152], F32))
        self.bplog = Buf("plog")
        self.bplogA = Buf("plogA", overlaps=(self.bplog,))
        self.bplogB = Buf("plogB", overlaps=(self.bplog,))
        self.bplogC = Buf("plogC", overlaps=(self.bplog,))
        self.bplog.overlaps.extend([self.bplogA, self.bplogB, self.bplogC])
        self.pav = ec(nc.psum_tensor("pav", [65, 512], F32))
        self.bpav = Buf("pav")
        self.prb = ec(nc.psum_tensor("prb", [64, 512], F32))
        self.bprb = Buf("prb")
        self.g2 = [ColView(self.lnt, 0, 1152), ColView(self.lnt, 1152, 1152),
                   ColView(self.lnt, 12288, 1152)]
        self.probs = [ColView(self.lnt, 2304, 1152), ColView(self.lnt, 3456, 1152)]
        self.probsT = ColView(self.lnt, 4608, 1152)
        self.gstage = [ColView(self.lnt, 5760, 1024), ColView(self.lnt, 13440, 1024)]
        self.AT0 = ColView(self.lnt, 6784, NE * 128)
        self.bg2 = [Buf(f"g2_{i}", overlaps=(self.blnt,)) for i in range(3)]
        self.bprobs = [Buf(f"probs{i}", overlaps=(self.blnt,)) for i in range(2)]
        self.bprobsT = Buf("probsT", overlaps=(self.blnt,))
        self.bgstage = [Buf(f"gstage{i}", overlaps=(self.blnt,)) for i in range(2)]
        self.bAT0 = Buf("AT0", overlaps=(self.blnt,))
        for bb in self.bg2 + self.bprobs + self.bgstage + [self.bprobsT, self.bAT0]:
            self.blnt.overlaps.append(bb)
        self.AT1 = ColView(self.lnt, 8192, NE * 512)
        self.bAT1 = Buf("AT1", overlaps=(self.blnt,))
        self.blnt.overlaps.append(self.bAT1)
        self.W2 = ColView(self.Cc, 0, NE * D)
        self.cos2 = ColView(self.Cc, 0, T)
        self.sin2 = ColView(self.Cc, T, T)
        self.qraw = [ColView(self.Cc, 2 * T + 512 * i, 512) for i in range(2)]
        self.rtmp = [ColView(self.Cc, 2 * T + 1024 + 512 * i, 512) for i in range(2)]
        cgroup = []
        self.bcos2 = Buf("cos2")
        self.bsin2 = Buf("sin2")
        self.bqraw = [Buf(f"qraw{i}") for i in range(2)]
        self.brtmp = [Buf(f"rtmp{i}") for i in range(2)]
        cgroup = [self.bcos2, self.bsin2] + self.bqraw + self.brtmp + [self.bC_, self.bW2]
        for bb in cgroup:
            bb.overlaps.extend([o for o in cgroup if o is not bb])

        self.xtile_bufs = {}
        p.op("vector", lambda v: v.memset(self.halfpi[:, :], float(np.pi / 2)),
             writes=(self.bhalfpi,))

    def xbuf_of(self, ap, seq, j):
        key = (ap.tensor.name, seq, j)
        if key not in self.xtile_bufs:
            self.xtile_bufs[key] = Buf(f"xt_{key}")
        return self.xtile_bufs[key]



def emit_ffn0(p, sh, xin, xout, w):
    cm = sh.cm
    lnt, blnt = sh.lnt, sh.blnt
    wtmp, bwtmp = sh.wtmp, sh.bwtmp
    Cb = ColView(sh.A, 0, NE * D)
    Sb = ColView(sh.A, NE * D, NE * D)
    bC = bS = sh.bA
    sqr = [sh.f32s[0], sh.f32s[1]]
    sqi = [sh.f32s[2], sh.f32s[3]]
    froot = sqi
    bsqr = [sh.bf32s[0], sh.bf32s[1]]
    bsqi = [sh.bf32s[2], sh.bf32s[3]]
    bfroot = bsqi
    pr_ap = [sh.pg[:, 0:512], sh.pg[:, 512:1024]]
    bpr = sh.bpg
    pi_ap = [sh.plog[:, 0:512], sh.plog[:, 512:1024]]
    bpi = [sh.bplogA, sh.bplogB]

    for (wa, wb, dst, bdst, is_cos) in (
            (w["theta"], w["theta_n"], Cb, bC, True),
            (w["phi"], w["phi_n"], Sb, bS, False)):
        for r in range(NE):
            off = r * 128 * D
            p.op("gpsimd", lambda e, wa=wa, off=off: e.dma_start(
                wtmp[:, 0:D], bass.AP(wa.tensor, off, [[D, 128], [1, D]])),
                writes=(bwtmp,), is_dma=True)
            p.op("gpsimd", lambda e, wb=wb, off=off: e.dma_start(
                wtmp[:, 0:D], bass.AP(wb.tensor, off, [[D, 128], [1, D]]),
                accum_op=ALU.add),
                reads=(bwtmp,), writes=(bwtmp,), is_dma=True)
            if is_cos:
                p.op("scalar", lambda s: s.activation(
                    wtmp[:, 0:D], wtmp[:, 0:D], AF.Sin, bias=sh.halfpi[:, 0:1]),
                    reads=(bwtmp, sh.bhalfpi), writes=(bwtmp,))
            else:
                p.op("scalar", lambda s: s.activation(
                    wtmp[:, 0:D], wtmp[:, 0:D], AF.Sin),
                    reads=(bwtmp,), writes=(bwtmp,))
            p.op("vector", lambda v, dst=dst, r=r, bdst=bdst: v.tensor_scalar(
                dst[:, r * D:(r + 1) * D], wtmp[:, 0:D], 0.5, None, ALU.mult),
                reads=(bwtmp,), writes=(bdst,))

    out_toks = []
    for seq in range(SEQ_PER_CORE):
        cm.emit_ln_to_lnt(xin, seq, lnt, blnt)
        for j in range(NT):
            for oc in range(2):
                s = oc
                for k in range(NE):
                    p.op("tensor", lambda t, j=j, oc=oc, k=k, s=s: t.matmul(
                        pr_ap[s],
                        lnt[:, k * T + j * 128:k * T + (j + 1) * 128],
                        Cb[:, k * D + oc * 512:k * D + (oc + 1) * 512],
                        start=(k == 0), stop=(k == NE - 1)),
                        reads=(blnt, bC), writes=(bpr[s],))
                for k in range(NE):
                    p.op("tensor", lambda t, j=j, oc=oc, k=k, s=s: t.matmul(
                        pi_ap[s],
                        lnt[:, k * T + j * 128:k * T + (j + 1) * 128],
                        Sb[:, k * D + oc * 512:k * D + (oc + 1) * 512],
                        start=(k == 0), stop=(k == NE - 1)),
                        reads=(blnt, bS), writes=(bpi[s],))
                p.op("scalar", lambda s2, s=s: s2.activation(
                    sqr[s][:, :], pr_ap[s], AF.Square),
                    reads=(bpr[s],), writes=(bsqr[s],))
                p.op("scalar", lambda s2, s=s: s2.activation(
                    sqi[s][:, :], pi_ap[s], AF.Square),
                    reads=(bpi[s],), writes=(bsqi[s],))
                p.op("vector", lambda v, s=s: v.tensor_tensor(
                    sqr[s][:, :], sqr[s][:, :], sqi[s][:, :], ALU.add),
                    reads=(bsqr[s], bsqi[s]), writes=(bsqr[s],))
                p.op("scalar", lambda s2, s=s: s2.activation(
                    froot[s][:, :], sqr[s][:, :], AF.Sqrt),
                    reads=(bsqr[s],), writes=(bfroot[s],))
            out_toks.append(cm.residual_add_store(
                xin, xout, seq, j,
                [(froot[0][:, :], bfroot[0]), (froot[1][:, :], bfroot[1])]))
    return out_toks



def emit_ffn1(p, sh, xin, xout, w):
    cm = sh.cm
    lnt, blnt = sh.lnt, sh.blnt
    kact = lnt
    wtmp, bwtmp, wtb, bwtb = sh.wtmp, sh.bwtmp, sh.wtb, sh.bwtb
    WTa, bWTa = sh.W1, sh.bW1
    WTb, bWTb = sh.W2, sh.bW2
    mus, bmus = sh.mus, sh.bmus
    xr, bxr = sh.A, sh.bA
    xk, bxk = sh.Bb, sh.bB
    krelu, bkrelu = sh.krelu, sh.bkrelu
    rsb, brsb = sh.rsb, sh.brsb
    vtmp = [sh.f32s[0], sh.f32s[1]]
    bvtmp = [sh.bf32s[0], sh.bf32s[1]]
    pf_ap = [sh.pg[:, 0:512], sh.pg[:, 512:1024],
             sh.plog[:, 0:512], sh.plog[:, 512:1024]]
    bpf = [sh.bpg[0], sh.bpg[1], sh.bplogA, sh.bplogB]

    for (src, colbase) in ((w["mu_r"], 0), (w["mu_k"], 8)):
        for cc in range(NE):
            p.op("gpsimd", lambda e, src=src, cc=cc, colbase=colbase: e.dma_start(
                mus[:, colbase + cc:colbase + cc + 1],
                bass.AP(src.tensor, cc * 128, [[1, 128], [1, 1]])),
                writes=(bmus,), is_dma=True)
    p.op("vector", lambda v: v.tensor_scalar(
        mus[:, 16:32], mus[:, 0:16], -1.0, 1.0, ALU.mult, ALU.add),
        reads=(bmus,), writes=(bmus,))

    out_toks = []
    for seq in range(SEQ_PER_CORE):
        cm.emit_ln_to_lnt(xin, seq, lnt, blnt)
        for cc in range(NE):
            for (dst, bdst, mcol) in ((xr, bxr, 0), (xk, bxk, 8)):
                p.op("vector", lambda v, cc=cc, dst=dst, mcol=mcol: v.tensor_scalar(
                    dst[:, cc * T:(cc + 1) * T], lnt[:, cc * T:(cc + 1) * T],
                    mus[:, mcol + cc:mcol + cc + 1], None, ALU.mult),
                    reads=(blnt, bmus), writes=(bdst,))
                p.op("vector", lambda v, cc=cc, dst=dst, mcol=mcol: v.scalar_tensor_tensor(
                    dst[:, cc * T + 1:(cc + 1) * T],
                    lnt[:, cc * T:(cc + 1) * T - 1],
                    mus[:, 16 + mcol + cc:17 + mcol + cc],
                    dst[:, cc * T + 1:(cc + 1) * T],
                    ALU.mult, ALU.add),
                    reads=(blnt, bmus, bdst), writes=(bdst,))
        cm.emit_weight_transpose(w["W_k"], WTa, bWTa, wtmp, bwtmp, wtb, bwtb)
        for g in range(NE):
            for tc in range(4):
                s = (g * 4 + tc) % 2
                for cc in range(NE):
                    p.op("tensor", lambda t, g=g, tc=tc, cc=cc, s=s: t.matmul(
                        pf_ap[s],
                        WTa[:, cc * D + g * 128:cc * D + (g + 1) * 128],
                        xk[:, cc * T + tc * 512:cc * T + (tc + 1) * 512],
                        start=(cc == 0), stop=(cc == NE - 1)),
                        reads=(bWTa, bxk), writes=(bpf[s],))
                p.op("scalar", lambda sc, s=s: sc.activation(
                    krelu[s][:, :], pf_ap[s], AF.Relu),
                    reads=(bpf[s],), writes=(bkrelu[s],))
                p.op("vector", lambda v, g=g, tc=tc, s=s: v.scalar_tensor_tensor(
                    kact[:, g * T + tc * 512:g * T + (tc + 1) * 512],
                    krelu[s][:, :], 0.5, krelu[s][:, :], ALU.mult, ALU.mult),
                    reads=(bkrelu[s],), writes=(blnt,))
        cm.emit_weight_transpose(w["W_r"], WTb, bWTb, wtmp, bwtmp, wtb, bwtb)
        cm.emit_weight_transpose(w["W_v"], WTa, bWTa, wtmp, bwtmp, wtb, bwtb)
        for j in range(NT):
            pso = []
            for oc in range(2):
                s = oc
                for cc in range(NE):
                    p.op("tensor", lambda t, j=j, oc=oc, cc=cc, s=s: t.matmul(
                        pf_ap[s],
                        xr[:, cc * T + j * 128:cc * T + (j + 1) * 128],
                        WTb[:, cc * D + oc * 512:cc * D + (oc + 1) * 512],
                        start=(cc == 0), stop=(cc == NE - 1)),
                        reads=(bxr, bWTb), writes=(bpf[s],))
                p.op("scalar", lambda sc, s=s: sc.activation(
                    rsb[s][:, :], pf_ap[s], AF.Sigmoid),
                    reads=(bpf[s],), writes=(brsb[s],))
                for dd in range(NE):
                    p.op("tensor", lambda t, j=j, oc=oc, dd=dd, s=s: t.matmul(
                        pf_ap[2 + s],
                        kact[:, dd * T + j * 128:dd * T + (j + 1) * 128],
                        WTa[:, dd * D + oc * 512:dd * D + (oc + 1) * 512],
                        start=(dd == 0), stop=(dd == NE - 1)),
                        reads=(blnt, bWTa), writes=(bpf[2 + s],))
                p.op("vector", lambda v, s=s: v.tensor_tensor(
                    vtmp[s][:, :], pf_ap[2 + s], rsb[s][:, :], ALU.mult),
                    reads=(bpf[2 + s], brsb[s]), writes=(bvtmp[s],))
                pso.append((vtmp[s][:, :], bvtmp[s]))
            out_toks.append(cm.residual_add_store(xin, xout, seq, j, pso))
    return out_toks



def emit_att1(p, sh, xin, xout, w, consts):
    cm = sh.cm
    lnt, blnt = sh.lnt, sh.blnt
    wtmp, bwtmp, wtb, bwtb = sh.wtmp, sh.bwtmp, sh.wtb, sh.bwtb
    WT, bWT = sh.W1, sh.bW1
    cos2, sin2 = sh.cos2, sh.sin2
    bcos2, bsin2 = sh.bcos2, sh.bsin2
    prot, bprot = sh.prot, sh.bprot
    brow, bbrow = sh.brow, sh.bbrow
    QT, bQT = sh.A, sh.bA
    KT, bKT = sh.Bb, sh.bB
    Vsb, bV = sh.Cc, sh.bC_
    qraw, bqraw = sh.qraw, sh.bqraw
    rtmp, brtmp = sh.rtmp, sh.brtmp
    expt, bexpt = sh.expt, sh.bexpt
    AT, bAT = sh.AT1, sh.bAT1
    recrow, brecrow = sh.recrow, sh.brecrow
    rbsb, brbsb = sh.rbsb, sh.brbsb
    pl_ap = [sh.plog[:, 0:512], sh.plog[:, 512:1024], sh.pg[:, 0:512]]
    bpl = [sh.bplogA, sh.bplogB, sh.bpg[0]]
    prot_ps = sh.pg[:, 512:1024]
    bprot_ps = sh.bpg[1]
    pav, bpav = sh.pav, sh.bpav
    prb, bprb = sh.prb, sh.bprb

    p.op("sync", lambda e: e.dma_start(cos2[:, :], consts["C_COS2"]),
         writes=(bcos2,), is_dma=True)
    p.op("sync", lambda e: e.dma_start(sin2[:, :], consts["C_SIN2"]),
         writes=(bsin2,), is_dma=True)
    p.op("sync", lambda e: e.dma_start(prot[:, :], consts["C_PROT"]),
         writes=(bprot,), is_dma=True)
    for i, bn in enumerate(("bq", "bk", "bv", "bo")):
        bsrc = w[bn]
        p.op("gpsimd", lambda e, bsrc=bsrc: e.dma_start(
            wtmp[0:1, 0:D], bass.AP(bsrc.tensor, 0, [[D, 1], [1, D]])),
            writes=(bwtmp,), is_dma=True)
        p.op("vector", lambda v, i=i: v.tensor_copy(
            brow[0:1, i * D:(i + 1) * D], wtmp[0:1, 0:D]),
            reads=(bwtmp,), writes=(bbrow,))

    out_toks = []
    for seq in range(SEQ_PER_CORE):
        cm.emit_ln_to_lnt(xin, seq, lnt, blnt)

        for (wi, kind, browoff) in ((w["Wq"], "q", 0), (w["Wk"], "k", 1),
                                    (w["Wv"], "v", 2)):
            cm.emit_weight_transpose(wi, WT, bWT, wtmp, bwtmp, wtb, bwtb)
            if kind == "v":
                for j in range(NT):
                    for oc in range(2):
                        s = oc
                        for cc in range(NE):
                            p.op("tensor", lambda t, j=j, oc=oc, cc=cc, s=s: t.matmul(
                                pl_ap[s],
                                lnt[:, cc * T + j * 128:cc * T + (j + 1) * 128],
                                WT[:, cc * D + oc * 512:cc * D + (oc + 1) * 512],
                                start=(cc == 0), stop=False),
                                reads=(blnt, bWT), writes=(bpl[s],))
                        p.op("tensor", lambda t, oc=oc, s=s, browoff=browoff: t.matmul(
                            pl_ap[s],
                            cm.onesrow[0:1, 0:128],
                            brow[0:1, browoff * D + oc * 512:browoff * D + (oc + 1) * 512],
                            start=False, stop=True),
                            reads=(cm.bonesrow, bbrow), writes=(bpl[s],))
                        p.op("scalar", lambda sc, j=j, oc=oc, s=s: sc.activation(
                            bass.AP(Vsb, j * 1040 + oc * 8 * 65,
                                    [[NT * 1040, 128], [65, 8], [1, 64]]),
                            pl_ap[s], AF.Identity),
                            reads=(bpl[s],), writes=(bV,))
            else:
                scale = 0.125 if kind == "q" else 1.0
                dst, bdst = (QT, bQT) if kind == "q" else (KT, bKT)
                for g in range(NE):
                    for tc in range(4):
                        s = (g * 4 + tc) % 2
                        for cc in range(NE):
                            p.op("tensor", lambda t, g=g, tc=tc, cc=cc, s=s: t.matmul(
                                pl_ap[s],
                                WT[:, cc * D + g * 128:cc * D + (g + 1) * 128],
                                lnt[:, cc * T + tc * 512:cc * T + (tc + 1) * 512],
                                start=(cc == 0), stop=False),
                                reads=(blnt, bWT), writes=(bpl[s],))
                        p.op("tensor", lambda t, g=g, s=s, browoff=browoff: t.matmul(
                            pl_ap[s],
                            brow[0:1, browoff * D + g * 128:browoff * D + (g + 1) * 128],
                            cm.onesrow[0:1, 0:512],
                            start=False, stop=True),
                            reads=(bbrow, cm.bonesrow), writes=(bpl[s],))
                        p.op("scalar", lambda sc, s=s, scale=scale: sc.activation(
                            qraw[s][:, :], pl_ap[s], AF.Identity, scale=scale),
                            reads=(bpl[s],), writes=(bqraw[s],))
                        p.op("tensor", lambda t, s=s: t.matmul(
                            prot_ps, prot[:, :], qraw[s][:, :],
                            start=True, stop=True),
                            reads=(bprot, bqraw[s]), writes=(bprot_ps,))
                        p.op("vector", lambda v, tc=tc, s=s: v.tensor_tensor(
                            rtmp[s][:, :], prot_ps,
                            sin2[:, tc * 512:(tc + 1) * 512], ALU.mult),
                            reads=(bprot_ps, bsin2), writes=(brtmp[s],))
                        p.op("vector", lambda v, tc=tc, s=s: v.tensor_tensor(
                            qraw[s][:, :], qraw[s][:, :],
                            cos2[:, tc * 512:(tc + 1) * 512], ALU.mult),
                            reads=(bqraw[s], bcos2), writes=(bqraw[s],))
                        p.op("vector", lambda v, g=g, tc=tc, s=s, dst=dst, bdst=bdst: v.tensor_tensor(
                            dst[:, g * T + tc * 512:g * T + (tc + 1) * 512],
                            qraw[s][:, :], rtmp[s][:, :], ALU.add),
                            reads=(bqraw[s], brtmp[s]), writes=(bdst,))
        p.op("vector", lambda v: v.memset(
            bass.AP(Vsb, 64, [[NT * 1040, 128], [1040, NT], [65, H], [1, 1]]), 1.0),
            reads=(), writes=(bV,))
        cm.emit_weight_transpose(w["Wo"], WT, bWT, wtmp, bwtmp, wtb, bwtb)

        qk_ps = [(sh.plog, (sh.bplogA, sh.bplogB)), (sh.pg, (sh.bpg[0], sh.bpg[1]))]
        for qc in range(4):
            for h in range(H):
                g = h // 2
                hr = (h % 2) * 64

                def qk(pt_i):
                    pr_ = (pt_i // 2) % 2
                    half = pt_i % 2
                    pst, pbufs = qk_ps[pr_]
                    p.op("tensor", lambda t, g=g, hr=hr, qc=qc, pt_i=pt_i, pst=pst, half=half: t.matmul(
                        pst[:, half * 512:(half + 1) * 512],
                        KT[hr:hr + 64, g * T + pt_i * 128:g * T + (pt_i + 1) * 128],
                        QT[hr:hr + 64, g * T + qc * 512:g * T + (qc + 1) * 512],
                        start=True, stop=True),
                        reads=(bKT, bQT), writes=(pbufs[half],))

                def expair(pair):
                    pr_ = pair % 2
                    pst, pbufs = qk_ps[pr_]
                    es = pair % 2
                    p.op("scalar", lambda sc, pst=pst, es=es: sc.activation(
                        expt[es][:, :], pst[:, 0:1024], AF.Exp),
                        reads=pbufs, writes=(bexpt[es],))

                def pv(pt_i):
                    es = (pt_i // 2) % 2
                    half = pt_i % 2
                    p.op("tensor", lambda t, h=h, pt_i=pt_i, es=es, half=half: t.matmul(
                        pav[0:65, :],
                        Vsb[:, pt_i * 1040 + h * 65:pt_i * 1040 + h * 65 + 65],
                        expt[es][:, half * 512:(half + 1) * 512],
                        start=(pt_i == 0), stop=(pt_i == NT - 1)),
                        reads=(bV, bexpt[es]), writes=(bpav,))

                for pair in range(NT // 2):
                    qk(2 * pair)
                    qk(2 * pair + 1)
                    expair(pair)
                    if pair > 0:
                        pv(2 * pair - 2)
                        pv(2 * pair - 1)
                pv(NT - 2)
                pv(NT - 1)
                avraw = sh.f32s[2]
                bavraw = sh.bf32s[2]
                p.op("scalar", lambda sc, avraw=avraw: sc.activation(
                    avraw[0:65, :], pav[0:65, :], AF.Identity),
                    reads=(bpav,), writes=(bavraw,))
                p.op("vector", lambda v, avraw=avraw: v.reciprocal(
                    recrow[0:1, :], avraw[64:65, :]),
                    reads=(bavraw,), writes=(brecrow,))
                p.op("tensor", lambda t: t.matmul(
                    prb[:, :], cm.onesf1[0:1, 0:64], recrow[0:1, :],
                    start=True, stop=True),
                    reads=(cm.bonesf1, brecrow), writes=(bprb,))
                p.op("scalar", lambda sc: sc.activation(
                    rbsb[:, :], prb[:, :], AF.Identity),
                    reads=(bprb,), writes=(brbsb,))
                p.op("vector", lambda v, g=g, hr=hr, avraw=avraw: v.tensor_tensor(
                    AT[hr:hr + 64, g * 512:(g + 1) * 512],
                    avraw[0:64, :], rbsb[:, :], ALU.mult),
                    reads=(bavraw, brbsb), writes=(bAT,))
            for ts_ in range(4):
                j = qc * 4 + ts_
                pso = []
                for oc in range(2):
                    s = oc
                    for k in range(NE):
                        p.op("tensor", lambda t, ts_=ts_, oc=oc, k=k, s=s: t.matmul(
                            pl_ap[s],
                            AT[:, k * 512 + ts_ * 128:k * 512 + (ts_ + 1) * 128],
                            WT[:, k * D + oc * 512:k * D + (oc + 1) * 512],
                            start=(k == 0), stop=False),
                            reads=(bAT, bWT), writes=(bpl[s],))
                    p.op("tensor", lambda t, oc=oc, s=s: t.matmul(
                        pl_ap[s],
                        cm.onesrow[0:1, 0:128],
                        brow[0:1, 3 * D + oc * 512:3 * D + (oc + 1) * 512],
                        start=False, stop=True),
                        reads=(cm.bonesrow, bbrow), writes=(bpl[s],))
                    pso.append((pl_ap[s], bpl[s]))
                out_toks.append(cm.residual_add_store(xin, xout, seq, j, pso))
    return out_toks



def emit_att0(p, sh, nc, xin, xout, w, gdram, rbd):
    cm = sh.cm
    lnt, blnt = sh.lnt, sh.blnt
    wtmp, bwtmp = sh.wtmp, sh.bwtmp
    Wsb, bWsb = sh.W1, sh.bW1
    Wrsb, bWrsb = sh.Wrsb, sh.bWrsb
    bndsb, bbnd = sh.bndsb, sh.bbnd
    brow, bbrow = sh.brow, sh.bbrow
    negtile, bneg = sh.negtile, sh.bneg
    QT, bQT = sh.A, sh.bA
    KT, bKT = sh.Bb, sh.bB
    Vsb, bV = sh.Cc, sh.bC_
    rslice, brslice = sh.rslice, sh.brslice
    rstage, brstage = sh.rstage, sh.brstage
    g2, bg2 = sh.g2, sh.bg2
    probs, bprobs = sh.probs, sh.bprobs
    probsT, bprobsT = sh.probsT, sh.bprobsT
    gstage, bgstage = sh.gstage, sh.bgstage
    AT, bAT = sh.AT0, sh.bAT0
    recrow, brecrow = sh.recrow, sh.brecrow
    rbsb, brbsb = sh.rbsb, sh.brbsb
    plog, bplog = sh.plog, sh.bplog
    pg = sh.pg
    bpg = sh.bpg
    pav, bpav = sh.pav, sh.bpav
    prb, bprb = sh.prb, sh.bprb
    NSLOT = 3
    bgslot = [Buf(f"gslot{i}") for i in range(NSLOT)]
    brbd = Buf("rbd")

    for (bn, coff, n) in (("bqv", 0, D), ("bprojv", D, D), ("brv", 2 * D, 160)):
        bsrc = w[bn]
        p.op("gpsimd", lambda e, bsrc=bsrc, n=n: e.dma_start(
            wtmp[0:1, 0:n], bass.AP(bsrc.tensor, 0, [[n, 1], [1, n]])),
            writes=(bwtmp,), is_dma=True)
        p.op("vector", lambda v, coff=coff, n=n: v.tensor_copy(
            brow[0:1, coff:coff + n], wtmp[0:1, 0:n]),
            reads=(bwtmp,), writes=(bbrow,))
    p.op("gpsimd", lambda e: e.dma_start(
        wtmp[0:NB, 0:1024], bass.AP(w["bndrev"].tensor, 0, [[1024, NB], [1, 1024]])),
        writes=(bwtmp,), is_dma=True)
    p.op("vector", lambda v: v.tensor_copy(bndsb[0:NB, :], wtmp[0:NB, 0:1024]),
         reads=(bwtmp,), writes=(bbnd,))
    cm.emit_weight_cast(w["Wr"], Wrsb, bWrsb, D, 160, wtmp, bwtmp)
    p.op("vector", lambda v: v.memset(negtile[:, :], NEG), writes=(bneg,))
    for sl in range(NSLOT):
        p.op("gpsimd", lambda e, sl=sl: e.dma_start(
            bass.AP(gdram, sl * 128 * 1279, [[1279, 128], [1, 128]]),
            negtile[:, :]),
            reads=(bneg,), writes=(bgslot[sl],), is_dma=True)
        p.op("gpsimd", lambda e, sl=sl: e.dma_start(
            bass.AP(gdram, sl * 128 * 1279 + 1152, [[1279, 128], [1, 127]]),
            negtile[:, 0:127]),
            reads=(bneg,), writes=(bgslot[sl],), is_dma=True)

    out_toks = []
    for seq in range(SEQ_PER_CORE):
        cm.emit_ln_to_lnt(xin, seq, lnt, blnt)

        for (wi, kind) in ((w["Wq"], "q"), (w["Wk"], "k"), (w["Wv"], "v")):
            cm.emit_weight_cast(wi, Wsb, bWsb, D, D, wtmp, bwtmp)
            if kind == "v":
                for j in range(NT):
                    for oc in range(2):
                        for cc in range(NE):
                            p.op("tensor", lambda t, j=j, oc=oc, cc=cc: t.matmul(
                                pg[:, oc * 512:(oc + 1) * 512],
                                lnt[:, cc * T + j * 128:cc * T + (j + 1) * 128],
                                Wsb[:, cc * D + oc * 512:cc * D + (oc + 1) * 512],
                                start=(cc == 0), stop=(cc == NE - 1)),
                                reads=(blnt, bWsb), writes=(bpg[oc],))
                        p.op("scalar", lambda sc, j=j, oc=oc: sc.activation(
                            bass.AP(Vsb, j * 1040 + oc * 8 * 65,
                                    [[NT * 1040, 128], [65, 8], [1, 64]]),
                            pg[:, oc * 512:(oc + 1) * 512], AF.Identity),
                            reads=(bpg[oc],), writes=(bV,))
            else:
                scale = 0.125 if kind == "q" else 1.0
                dst, bdst = (QT, bQT) if kind == "q" else (KT, bKT)
                has_bias = (kind == "q")
                for g in range(NE):
                    for tc in range(4):
                        oc2 = (g * 4 + tc) % 2
                        for cc in range(NE):
                            p.op("tensor", lambda t, g=g, tc=tc, cc=cc, oc2=oc2, has_bias=has_bias: t.matmul(
                                pg[:, oc2 * 512:(oc2 + 1) * 512],
                                Wsb[:, cc * D + g * 128:cc * D + (g + 1) * 128],
                                lnt[:, cc * T + tc * 512:cc * T + (tc + 1) * 512],
                                start=(cc == 0),
                                stop=(cc == NE - 1 and not has_bias)),
                                reads=(blnt, bWsb), writes=(bpg[oc2],))
                        if has_bias:
                            p.op("tensor", lambda t, g=g, oc2=oc2: t.matmul(
                                pg[:, oc2 * 512:(oc2 + 1) * 512],
                                brow[0:1, g * 128:(g + 1) * 128],
                                cm.onesrow[0:1, 0:512],
                                start=False, stop=True),
                                reads=(bbrow, cm.bonesrow), writes=(bpg[oc2],))
                        p.op("scalar", lambda sc, g=g, tc=tc, oc2=oc2, dst=dst, bdst=bdst, scale=scale: sc.activation(
                            dst[:, g * T + tc * 512:g * T + (tc + 1) * 512],
                            pg[:, oc2 * 512:(oc2 + 1) * 512], AF.Identity, scale=scale),
                            reads=(bpg[oc2],), writes=(bdst,))
        p.op("vector", lambda v: v.memset(
            bass.AP(Vsb, 64, [[NT * 1040, 128], [1040, NT], [65, H], [1, 1]]), 1.0),
            reads=(), writes=(bV,))
        cm.emit_weight_cast(w["Wproj"], Wsb, bWsb, D, D, wtmp, bwtmp)
        for half in range(2):
            for tc in range(4):
                for cc in range(NE):
                    p.op("tensor", lambda t, half=half, tc=tc, cc=cc: t.matmul(
                        pg[0:80, 0:512],
                        Wrsb[:, cc * 160 + half * 80:cc * 160 + half * 80 + 80],
                        lnt[:, cc * T + tc * 512:cc * T + (tc + 1) * 512],
                        start=(cc == 0), stop=False),
                        reads=(blnt, bWrsb), writes=(bpg[0],))
                p.op("tensor", lambda t, half=half: t.matmul(
                    pg[0:80, 0:512],
                    brow[0:1, 2 * D + half * 80:2 * D + half * 80 + 80],
                    cm.onesrow[0:1, 0:512],
                    start=False, stop=True),
                    reads=(bbrow, cm.bonesrow), writes=(bpg[0],))
                p.op("scalar", lambda sc: sc.activation(
                    rstage[0:80, 0:512], pg[0:80, 0:512], AF.Identity),
                    reads=(bpg[0],), writes=(brstage,))
                p.op("gpsimd", lambda e, half=half, tc=tc: e.dma_start(
                    bass.AP(rbd, half * 80 * T + tc * 512, [[T, 80], [1, 512]]),
                    rstage[0:80, 0:512]),
                    reads=(brstage,), writes=(brbd,), is_dma=True)

        idxs = [(i, h) for i in range(NT) for h in range(H)]

        def emit_g(idx):
            i, h = idxs[idx]
            sl = idx % NSLOT
            gs2 = idx % 2
            gs3 = idx % 3
            p.op("sync", lambda e, h=h, gs2=gs2: e.dma_start(
                rslice[gs2][0:NB, :],
                bass.AP(rbd, (h // 8 * 80 + (h % 8) * NB) * T, [[T, NB], [1, T]])),
                reads=(brbd,), writes=(brslice[gs2],), is_dma=True)
            for half in range(2):
                p.op("tensor", lambda t, i=i, half=half, gs2=gs2: t.matmul(
                    pg[:, half * 512:(half + 1) * 512],
                    rslice[gs2][0:NB, i * 128:(i + 1) * 128],
                    bndsb[0:NB, half * 512:(half + 1) * 512],
                    start=True, stop=True),
                    reads=(brslice[gs2], bbnd), writes=(bpg[half],))
            p.op("vector", lambda v, gs2=gs2: v.tensor_copy(gstage[gs2][:, :], pg[:, :]),
                reads=(bpg[0], bpg[1]), writes=(bgstage[gs2],))
            p.op("gpsimd", lambda e, sl=sl, gs2=gs2: e.dma_start(
                bass.AP(gdram, sl * 128 * 1279 + 128, [[1279, 128], [1, 1024]]),
                gstage[gs2][:, :]),
                reads=(bgstage[gs2],), writes=(bgslot[sl],), is_dma=True)
            p.op("sync", lambda e, sl=sl, gs3=gs3: e.dma_start(
                g2[gs3][:, :],
                bass.AP(gdram, sl * 128 * 1279 + 127, [[1278, 128], [1, 1152]])),
                reads=(bgslot[sl],), writes=(bg2[gs3],), is_dma=True)

        bplogch = {0: sh.bplogA, 512: sh.bplogB, 1024: sh.bplogC}

        def emit_qk(idx):
            i, h = idxs[idx]
            gs = idx % 3
            g = h // 2
            hr = (h % 2) * 64
            nb = min(i, 8) + 1
            bc = (9 - nb) * 128
            p0 = (i - (nb - 1)) * 128
            ps_ = idx % 2
            for (c0, c1) in psum_chunks(bc, 1152):
                bch = bplogch[(c0 // 512) * 512]
                p.op("tensor", lambda t, g=g, hr=hr, i=i, c0=c0, c1=c1, bc=bc, p0=p0: t.matmul(
                    plog[:, c0:c1],
                    QT[hr:hr + 64, g * T + i * 128:g * T + (i + 1) * 128],
                    KT[hr:hr + 64, g * T + p0 + (c0 - bc):g * T + p0 + (c1 - bc)],
                    start=True, stop=False),
                    reads=(bQT, bKT), writes=(bch,))
                p.op("tensor", lambda t, c0=c0, c1=c1, gs=gs: t.matmul(
                    plog[:, c0:c1],
                    cm.ident[:, :],
                    g2[gs][:, c0:c1],
                    start=False, stop=True),
                    reads=(cm.bident, bg2[gs]), writes=(bch,))
                p.op("scalar", lambda sc, c0=c0, c1=c1, ps_=ps_: sc.activation(
                    probs[ps_][:, c0:c1], plog[:, c0:c1], AF.Exp),
                    reads=(bch,), writes=(bprobs[ps_],))

        def emit_tail(idx):
            i, h = idxs[idx]
            g = h // 2
            hr = (h % 2) * 64
            nb = min(i, 8) + 1
            bc = (9 - nb) * 128
            ps_ = idx % 2
            blocks = list(range(bc // 128, 9))
            si = 0
            for b2 in range(0, len(blocks), 4):
                grp = blocks[b2:b2 + 4]
                pslot = si % len(cm.ppt)
                si += 1
                for k, m in enumerate(grp):
                    p.op("tensor", lambda t, m=m, k=k, pslot=pslot, ps_=ps_: t.transpose(
                        cm.ppt[pslot][:, k * 128:(k + 1) * 128],
                        probs[ps_][:, m * 128:(m + 1) * 128], cm.ident[:, :]),
                        reads=(bprobs[ps_], cm.bident), writes=(cm.bppt[pslot],))
                eng = "vector" if (si % 2) else "scalar"
                for k, m in enumerate(grp):
                    dstp = probsT[:, m * 128:(m + 1) * 128]
                    srcp = cm.ppt[pslot][:, k * 128:(k + 1) * 128]
                    if eng == "vector":
                        p.op(eng, lambda en, dstp=dstp, srcp=srcp: en.tensor_copy(dstp, srcp),
                             reads=(cm.bppt[pslot],), writes=(bprobsT,))
                    else:
                        p.op(eng, lambda en, dstp=dstp, srcp=srcp: en.copy(dstp, srcp),
                             reads=(cm.bppt[pslot],), writes=(bprobsT,))
            for k, m in enumerate(blocks):
                jj = i - (nb - 1) + k
                p.op("tensor", lambda t, jj=jj, h=h, m=m, k=k, nb=nb: t.matmul(
                    pav[0:65, 0:128],
                    Vsb[:, jj * 1040 + h * 65:jj * 1040 + h * 65 + 65],
                    probsT[:, m * 128:(m + 1) * 128],
                    start=(k == 0), stop=(k == nb - 1)),
                    reads=(bV, bprobsT), writes=(bpav,))
            avraw = sh.f32s[2]
            bavraw = sh.bf32s[2]
            p.op("scalar", lambda sc, avraw=avraw: sc.activation(
                avraw[0:65, 0:128], pav[0:65, 0:128], AF.Identity),
                reads=(bpav,), writes=(bavraw,))
            p.op("vector", lambda v, avraw=avraw: v.reciprocal(
                recrow[0:1, 0:128], avraw[64:65, 0:128]),
                reads=(bavraw,), writes=(brecrow,))
            p.op("tensor", lambda t: t.matmul(
                prb[0:64, 0:128], cm.onesf1[0:1, 0:64], recrow[0:1, 0:128],
                start=True, stop=True),
                reads=(cm.bonesf1, brecrow), writes=(bprb,))
            p.op("scalar", lambda sc: sc.activation(
                rbsb[0:64, 0:128], prb[0:64, 0:128], AF.Identity),
                reads=(bprb,), writes=(brbsb,))
            p.op("vector", lambda v, g=g, hr=hr, avraw=avraw: v.tensor_tensor(
                AT[hr:hr + 64, g * 128:(g + 1) * 128],
                avraw[0:64, 0:128], rbsb[0:64, 0:128], ALU.mult),
                reads=(bavraw, brbsb), writes=(bAT,))

        def emit_outproj(i):
            pso = []
            for oc in range(2):
                bch = bplogch[oc * 512]
                for k in range(NE):
                    p.op("tensor", lambda t, oc=oc, k=k: t.matmul(
                        plog[:, oc * 512:(oc + 1) * 512],
                        AT[:, k * 128:(k + 1) * 128],
                        Wsb[:, k * D + oc * 512:k * D + (oc + 1) * 512],
                        start=(k == 0), stop=False),
                        reads=(bAT, bWsb), writes=(bch,))
                p.op("tensor", lambda t, oc=oc: t.matmul(
                    plog[:, oc * 512:(oc + 1) * 512],
                    cm.onesrow[0:1, 0:128],
                    brow[0:1, D + oc * 512:D + (oc + 1) * 512],
                    start=False, stop=True),
                    reads=(cm.bonesrow, bbrow), writes=(bch,))
                pso.append((plog[:, oc * 512:(oc + 1) * 512], bch))
            out_toks.append(cm.residual_add_store(xin, xout, seq, i, pso))

        emit_g(0)
        emit_g(1)
        for idx in range(len(idxs)):
            i, h = idxs[idx]
            if idx + 2 < len(idxs):
                emit_g(idx + 2)
            emit_qk(idx)
            if idx > 0:
                emit_tail(idx - 1)
                pi_, ph_ = idxs[idx - 1]
                if ph_ == H - 1:
                    emit_outproj(pi_)
        emit_tail(len(idxs) - 1)
        emit_outproj(NT - 1)
    return out_toks



FFN0_W = ("theta", "phi", "theta_n", "phi_n")
FFN1_W = ("W_r", "W_k", "W_v", "mu_r", "mu_k")
ATT1_W = ("Wq", "Wk", "Wv", "Wo", "bq", "bk", "bv", "bo")
ATT0_W = ("Wq", "Wk", "Wv", "Wproj", "Wr", "bqv", "bprojv", "brv", "bndrev")
_SHAPES = {
    "theta": D * D, "phi": D * D, "theta_n": D * D, "phi_n": D * D,
    "W_r": D * D, "W_k": D * D, "W_v": D * D, "mu_r": D, "mu_k": D,
    "Wq": D * D, "Wk": D * D, "Wv": D * D, "Wo": D * D, "Wproj": D * D,
    "Wr": D * H * NB, "bq": D, "bk": D, "bv": D, "bo": D,
    "bqv": D, "bprojv": D, "brv": H * NB, "bndrev": NB * 1024,
}
ALL_CONSTS = ["C_IDENT", "C_ONESROW", "C_ONESCOL", "C_ONESF1",
              "C_COS2", "C_SIN2", "C_PROT"]


def _w_aps(nc, names, prefix=""):
    return {n: nc.dram_tensor(prefix + n, [_SHAPES[n]], F32, kind="ExternalInput").ap()
            for n in names}


def _build_single(which):
    nc = bass.Bass("TRN2", target_bir_lowering=False)
    xin = nc.dram_tensor("xin", [SEQ_PER_CORE * T * D], F32, kind="ExternalInput").ap()
    names = {"ffn0": FFN0_W, "ffn1": FFN1_W, "att1": ATT1_W, "att0": ATT0_W}[which]
    w = _w_aps(nc, names)
    consts = add_consts(nc, ALL_CONSTS)
    xout = nc.dram_tensor("xout", [SEQ_PER_CORE * T * D], F32, kind="ExternalOutput").ap()
    p = Sched()
    with ExitStack() as stack:
        ec = stack.enter_context
        sh = Shared(p, nc, ec, consts)
        if which == "ffn0":
            toks = emit_ffn0(p, sh, xin, xout, w)
        elif which == "ffn1":
            toks = emit_ffn1(p, sh, xin, xout, w)
        elif which == "att1":
            toks = emit_att1(p, sh, xin, xout, w, consts)
        else:
            gdram = nc.dram_tensor("gdram", [3 * 128 * 1279], BF16, kind="Internal")
            rbd = nc.dram_tensor("rbd", [2 * 80 * T], BF16, kind="Internal")
            toks = emit_att0(p, sh, nc, xin, xout, w, gdram, rbd)
        p.barrier("sync", toks)
        block = ec(nc.Block())
        p.finalize(nc, block, stack)
    return nc


def build_ffn0():
    return _build_single("ffn0")


def build_ffn1():
    return _build_single("ffn1")


def build_att1():
    return _build_single("att1")


def build_att0():
    return _build_single("att0")


def build_fused():
    nc = bass.Bass("TRN2", target_bir_lowering=False)
    xin = nc.dram_tensor("xin", [SEQ_PER_CORE * T * D], F32, kind="ExternalInput").ap()
    w_f0 = _w_aps(nc, FFN0_W, "f0_")
    w_a0 = _w_aps(nc, ATT0_W, "a0_")
    w_f1 = _w_aps(nc, FFN1_W, "f1_")
    w_a1 = _w_aps(nc, ATT1_W, "a1_")
    consts = add_consts(nc, ALL_CONSTS)
    xout = nc.dram_tensor("xout", [SEQ_PER_CORE * T * D], F32, kind="ExternalOutput").ap()
    xb = [nc.dram_tensor(f"xb{i}", [SEQ_PER_CORE * T * D], F32, kind="Internal").ap()
          for i in range(5)]
    gdram = nc.dram_tensor("gdram", [3 * 128 * 1279], BF16, kind="Internal")
    rbd = nc.dram_tensor("rbd", [2 * 80 * T], BF16, kind="Internal")

    p = Sched()
    with ExitStack() as stack:
        ec = stack.enter_context
        sh = Shared(p, nc, ec, consts)
        emit_ffn0(p, sh, xin, xb[0], w_f0)
        emit_att0(p, sh, nc, xb[0], xb[1], w_a0, gdram, rbd)
        emit_ffn0(p, sh, xb[1], xb[2], w_f0)
        emit_ffn1(p, sh, xb[2], xb[3], w_f1)
        emit_att1(p, sh, xb[3], xb[4], w_a1, consts)
        toks = emit_ffn1(p, sh, xb[4], xout, w_f1)
        p.barrier("sync", toks)
        block = ec(nc.Block())
        p.finalize(nc, block, stack)
    return nc



_cache = {}


def _get(name, builder):
    if name not in _cache:
        _cache[name] = builder()
    return _cache[name]


def _input_maps(inputs):
    consts = host_consts()
    xs = [np.ascontiguousarray(s).reshape(-1) for s in
          np.split(np.ascontiguousarray(inputs["L0_attn_X"]), NCORES, axis=0)]
    ffn0_w = {
        "theta": inputs["L0_ffn_theta"].reshape(-1),
        "phi": inputs["L0_ffn_phi"].reshape(-1),
        "theta_n": inputs["L0_ffn_theta_noise"].reshape(-1),
        "phi_n": inputs["L0_ffn_phi_noise"].reshape(-1),
    }
    att0_w = {
        "Wq": inputs["L0_attn_Wq"].reshape(-1),
        "Wk": inputs["L0_attn_Wk"].reshape(-1),
        "Wv": inputs["L0_attn_Wv"].reshape(-1),
        "Wproj": inputs["L0_attn_Wproj"].reshape(-1),
        "Wr": inputs["L0_attn_Wr"].reshape(-1),
        "bqv": inputs["L0_attn_bq"].reshape(-1),
        "bprojv": inputs["L0_attn_bproj"].reshape(-1),
        "brv": inputs["L0_attn_br"].reshape(-1),
        "bndrev": np.ascontiguousarray(inputs["L0_attn_b_nd"][:, ::-1]).reshape(-1),
    }
    ffn1_w = {
        "W_r": inputs["L1_ffn_W_r"].reshape(-1),
        "W_k": inputs["L1_ffn_W_k"].reshape(-1),
        "W_v": inputs["L1_ffn_W_v"].reshape(-1),
        "mu_r": inputs["L1_ffn_mu_r"].reshape(-1),
        "mu_k": inputs["L1_ffn_mu_k"].reshape(-1),
    }
    att1_w = {
        "Wq": inputs["L1_attn_Wq"].reshape(-1),
        "Wk": inputs["L1_attn_Wk"].reshape(-1),
        "Wv": inputs["L1_attn_Wv"].reshape(-1),
        "Wo": inputs["L1_attn_Wo"].reshape(-1),
        "bq": inputs["L1_attn_bq"].reshape(-1),
        "bk": inputs["L1_attn_bk"].reshape(-1),
        "bv": inputs["L1_attn_bv"].reshape(-1),
        "bo": inputs["L1_attn_bo"].reshape(-1),
    }
    return consts, xs, ffn0_w, att0_w, ffn1_w, att1_w


def kernel(**inputs):
    consts, xs, ffn0_w, att0_w, ffn1_w, att1_w = _input_maps(inputs)
    wmap = {}
    for pref, w in (("f0_", ffn0_w), ("a0_", att0_w), ("f1_", ffn1_w), ("a1_", att1_w)):
        for k, v in w.items():
            wmap[pref + k] = v
    for k in ALL_CONSTS:
        wmap[k] = consts[k]
    nc = _get("fused", build_fused)
    in_maps = [dict(wmap, xin=xs[c]) for c in range(NCORES)]
    res = run_bass_kernel_spmd(nc, in_maps, core_ids=list(range(NCORES)))
    out = np.concatenate(
        [np.asarray(r["xout"], np.float32).reshape(SEQ_PER_CORE, T, D)
         for r in res.results], axis=0)
    return out.astype(np.float32)


def kernel_unfused(**inputs):
    consts, xs, ffn0_w, att0_w, ffn1_w, att1_w = _input_maps(inputs)
    cmap = {k: consts[k] for k in ALL_CONSTS}

    def launch(nc, wm):
        nonlocal xs
        in_maps = [dict(cmap, xin=xs[c], **wm) for c in range(NCORES)]
        res = run_bass_kernel_spmd(nc, in_maps, core_ids=list(range(NCORES)))
        xs = [np.asarray(r["xout"], np.float32) for r in res.results]

    launch(_get("ffn0", build_ffn0), ffn0_w)
    launch(_get("att0", build_att0), att0_w)
    launch(_get("ffn0", build_ffn0), ffn0_w)
    launch(_get("ffn1", build_ffn1), ffn1_w)
    launch(_get("att1", build_att1), att1_w)
    launch(_get("ffn1", build_ffn1), ffn1_w)
    out = np.concatenate([x.reshape(SEQ_PER_CORE, T, D) for x in xs], axis=0)
    return out.astype(np.float32)


# version 08 (final)
# speedup vs baseline: 1.1801x; 1.0111x over previous
import math
from contextlib import ExitStack

import numpy as np
import ml_dtypes

import concourse.bass as bass
import concourse.mybir as mybir
from concourse.bass_utils import run_bass_kernel_spmd

F32 = mybir.dt.float32
BF16 = mybir.dt.bfloat16
AF = mybir.ActivationFunctionType
ALU = mybir.AluOpType
BF = ml_dtypes.bfloat16

B, T, D = 16, 2048, 1024
NCORES = 8
SEQ_PER_CORE = B // NCORES
NT = T // 128
NE = D // 128
H = 16
DH = 64
NB = 10
NEG = -1.0e9
ROPE_BASE = 10000.0

ENGINES = ("tensor", "vector", "scalar", "gpsimd", "sync")

SERIAL = False
SAME_ENGINE_WAITS = True


class Buf:
    __slots__ = ("name", "writers", "readers", "overlaps")

    def __init__(self, name="", overlaps=()):
        self.name = name
        self.writers = []
        self.readers = []
        self.overlaps = list(overlaps)


class ColView:

    def __init__(self, t, base, width):
        self.t = t
        self.base = base
        self.width = width

    def __getitem__(self, key):
        ps, cs = key
        start = self.base if cs.start is None else self.base + cs.start
        stop = self.base + (self.width if cs.stop is None else cs.stop)
        return self.t[ps, start:stop]


class _Op:
    __slots__ = ("emit", "deps", "marked", "is_dma", "semval")

    def __init__(self, emit, deps, is_dma):
        self.emit = emit
        self.deps = deps
        self.marked = False
        self.is_dma = is_dma
        self.semval = None


class Sched:
    def __init__(self):
        self.ops = {e: [] for e in ENGINES}
        self.last = None

    def op(self, eng, emit, reads=(), writes=(), is_dma=False):
        deps = []
        for b in reads:
            deps.extend(b.writers)
            for o in b.overlaps:
                deps.extend(o.writers)
        for b in writes:
            deps.extend(b.readers)
            deps.extend(b.writers)
            for o in b.overlaps:
                deps.extend(o.readers)
                deps.extend(o.writers)
        if SERIAL and self.last is not None:
            deps.append(self.last)
        uniq = []
        seen = set()
        for d in deps:
            de, di = d
            if (de, di) in seen:
                continue
            seen.add((de, di))
            dop = self.ops[de][di]
            if de == eng and not dop.is_dma and not SERIAL:
                if eng == "tensor" or not SAME_ENGINE_WAITS:
                    continue
            uniq.append(d)
        idx = len(self.ops[eng])
        rec = _Op(emit, uniq, is_dma)
        if is_dma:
            rec.marked = True
        self.ops[eng].append(rec)
        tok = (eng, idx)
        for d in uniq:
            self.ops[d[0]][d[1]].marked = True
        for b in reads:
            b.readers.append(tok)
        for b in writes:
            b.writers = [tok]
            b.readers = []
        self.last = tok
        return tok

    def barrier(self, eng, tokens):
        for t in tokens:
            self.ops[t[0]][t[1]].marked = True
        rec = _Op(None, list(dict.fromkeys(tokens)), False)
        self.ops[eng].append(rec)

    NPAR = 4

    def finalize(self, nc, block, stack):
        sems = {}
        dsems = {}
        for e in ENGINES:
            if any(o.marked and not o.is_dma for o in self.ops[e]):
                sems[e] = stack.enter_context(nc.semaphore(f"sem_{e}"))
            ndma = sum(1 for o in self.ops[e] if o.is_dma)
            if ndma:
                npar = min(self.NPAR, ndma)
                dsems[e] = [stack.enter_context(nc.semaphore(f"dsem_{e}{i}"))
                            for i in range(npar)]
            cnt = 0
            dcnt = 0
            for o in self.ops[e]:
                if o.is_dma:
                    par = dcnt % len(dsems[e])
                    o.semval = ("d", par, (dcnt // len(dsems[e]) + 1) * 16)
                    dcnt += 1
                elif o.marked:
                    cnt += 1
                    o.semval = ("c", 0, cnt)

        def mk(e):
            def run(eng):
                hw = {}
                for o in self.ops[e]:
                    for de, di in o.deps:
                        kind, par, v = self.ops[de][di].semval
                        sem = dsems[de][par] if kind == "d" else sems[de]
                        key = (de, kind, par)
                        if hw.get(key, 0) < v:
                            eng.wait_ge(sem, v)
                            hw[key] = v
                    if o.emit is None:
                        continue
                    inst = o.emit(eng)
                    if o.is_dma:
                        kind, par, v = o.semval
                        inst.then_inc(dsems[e][par], 16)
                    elif o.marked:
                        inst.then_inc(sems[e], 1)
            return run

        for e in ENGINES:
            if self.ops[e]:
                getattr(block, e)(mk(e))


def psum_chunks(c0, c1):
    out = []
    c = c0
    while c < c1:
        nxt = min(c1, (c // 512 + 1) * 512)
        out.append((c, nxt))
        c = nxt
    return out



def host_consts():
    c = {}
    c["C_IDENT"] = np.eye(128, dtype=np.float32).astype(BF)
    c["C_ONESROW"] = np.ones((1, 512), np.float32).astype(BF)
    c["C_ONESCOL"] = np.ones((128, 1), np.float32).astype(BF)
    c["C_ONESF1"] = np.ones((1, 64), np.float32)
    inv = 1.0 / ROPE_BASE ** (np.arange(0, DH, 2, dtype=np.float32) / DH)
    tpos = np.arange(T, dtype=np.float32)
    freqs = np.einsum("i,j->ij", tpos, inv)
    emb = np.concatenate([freqs, freqs], axis=-1)
    cosf = np.cos(emb).T.astype(np.float32)
    sinf = np.sin(emb).T.astype(np.float32)
    c["C_COS2"] = np.concatenate([cosf, cosf], 0).astype(BF)
    c["C_SIN2"] = np.concatenate([sinf, sinf], 0).astype(BF)
    p64 = np.zeros((64, 64), np.float32)
    for i in range(32):
        p64[2 * i + 1, 2 * i] = -1.0
        p64[2 * i, 2 * i + 1] = 1.0
    P = np.zeros((128, 128), np.float32)
    P[:64, :64] = p64
    P[64:, 64:] = p64
    c["C_PROT"] = P.astype(BF)
    return c


CONST_SHAPES = {
    "C_IDENT": ([128, 128], BF16),
    "C_ONESROW": ([1, 512], BF16),
    "C_ONESCOL": ([128, 1], BF16),
    "C_ONESF1": ([1, 64], F32),
    "C_COS2": ([128, T], BF16),
    "C_SIN2": ([128, T], BF16),
    "C_PROT": ([128, 128], BF16),
}


def add_consts(nc, names):
    aps = {}
    for n in names:
        s, d = CONST_SHAPES[n]
        aps[n] = nc.dram_tensor(n, s, d, kind="ExternalInput").ap()
    return aps



class Common:
    def __init__(self, p, nc, ec, consts, n_ppt=2):
        self.p = p
        self.nc = nc
        self.consts = consts
        self.xtile = [ec(nc.sbuf_tensor(f"xtile{i}", [128, D], F32)) for i in range(2)]
        self.bxtile = [Buf(f"xtile{i}") for i in range(2)]
        self.sq = None
        self.bsq = Buf("lnsq")
        self.cols = ec(nc.sbuf_tensor("lncols", [128, 80], F32))
        self.bsums, self.bsumsq, self.bsmall = Buf("sums"), Buf("sumsq"), Buf("small")
        self.lntile = ec(nc.sbuf_tensor("lntile", [128, D], BF16))
        self.blntile = Buf("lntile")
        self.ident = ec(nc.sbuf_tensor("identc", [128, 128], BF16))
        self.bident = Buf("ident")
        self.ppt = [ec(nc.psum_tensor(f"pptc{i}", [128, 512], BF16)) for i in range(n_ppt)]
        self.bppt = [Buf(f"ppt{i}") for i in range(n_ppt)]
        self.onesrow = ec(nc.sbuf_tensor("onesrowc", [1, 512], BF16))
        self.bonesrow = Buf("onesrow")
        self.onescol = ec(nc.sbuf_tensor("onescolc", [128, 1], BF16))
        self.bonescol = Buf("onescol")
        self.onesf1 = ec(nc.sbuf_tensor("onesf1c", [1, 64], F32))
        self.bonesf1 = Buf("onesf1")
        p.op("sync", lambda e: e.dma_start(self.ident[:, :], consts["C_IDENT"]),
             writes=(self.bident,), is_dma=True)
        p.op("sync", lambda e: e.dma_start(self.onesrow[:, :], consts["C_ONESROW"]),
             writes=(self.bonesrow,), is_dma=True)
        p.op("sync", lambda e: e.dma_start(self.onescol[:, :], consts["C_ONESCOL"]),
             writes=(self.bonescol,), is_dma=True)
        p.op("sync", lambda e: e.dma_start(self.onesf1[:, :], consts["C_ONESF1"]),
             writes=(self.bonesf1,), is_dma=True)

    def load_xtile(self, xin, seq, j, slot, q="sync"):
        off = seq * T * D + j * 128 * D
        self.p.op(q, lambda e, off=off, slot=slot: e.dma_start(
            self.xtile[slot][:, :], bass.AP(xin.tensor, off, [[D, 128], [1, D]])),
            writes=(self.bxtile[slot],), is_dma=True)

    def emit_ln_to_lnt(self, xin, seq, lnt, blnt):
        p = self.p
        cols = self.cols
        for j in range(NT):
            s = j % 2
            self.load_xtile(xin, seq, j, s)
            p.op("vector", lambda v, j=j, s=s: v.reduce_sum(
                cols[:, j:j + 1], self.xtile[s][:, :], mybir.AxisListType.X),
                reads=(self.bxtile[s],), writes=(self.bsums,))
            p.op("scalar", lambda sc, j=j, s=s: sc.activation(
                self.xtile[s][:, :], self.xtile[s][:, :], AF.Square,
                accum_out=cols[:, 16 + j:17 + j]),
                reads=(self.bxtile[s],), writes=(self.bxtile[s], self.bsumsq))
        p.op("vector", lambda v: v.tensor_scalar(
            cols[:, 32:48], cols[:, 0:16], 1.0 / D, None, ALU.mult),
            reads=(self.bsums,), writes=(self.bsmall,))
        p.op("vector", lambda v: v.tensor_scalar(
            cols[:, 48:64], cols[:, 16:32], 1.0 / D, None, ALU.mult),
            reads=(self.bsumsq,), writes=(self.bsmall,))
        p.op("vector", lambda v: v.scalar_tensor_tensor(
            cols[:, 64:80], cols[:, 32:48], -1.0, cols[:, 32:48], ALU.mult, ALU.mult),
            reads=(self.bsmall,), writes=(self.bsmall,))
        p.op("vector", lambda v: v.tensor_tensor(
            cols[:, 64:80], cols[:, 48:64], cols[:, 64:80], ALU.add),
            reads=(self.bsmall,), writes=(self.bsmall,))
        p.op("vector", lambda v: v.tensor_scalar(
            cols[:, 64:80], cols[:, 64:80], 1e-5, None, ALU.add),
            reads=(self.bsmall,), writes=(self.bsmall,))
        p.op("vector", lambda v: v.reciprocal(cols[:, 64:80], cols[:, 64:80]),
            reads=(self.bsmall,), writes=(self.bsmall,))
        p.op("scalar", lambda sc: sc.activation(
            cols[:, 64:80], cols[:, 64:80], AF.Sqrt),
            reads=(self.bsmall,), writes=(self.bsmall,))
        si = 0
        for j in range(NT):
            s = j % 2
            self.load_xtile(xin, seq, j, s)
            p.op("vector", lambda v, j=j, s=s: v.tensor_scalar(
                self.lntile[:, :], self.xtile[s][:, :],
                cols[:, 32 + j:33 + j], cols[:, 64 + j:65 + j],
                ALU.subtract, ALU.mult),
                reads=(self.bxtile[s], self.bsmall), writes=(self.blntile,))
            for e2 in range(0, NE, 4):
                ps = si % len(self.ppt)
                si += 1
                for k in range(4):
                    e = e2 + k
                    p.op("tensor", lambda t, e=e, ps=ps, k=k: t.transpose(
                        self.ppt[ps][:, k * 128:(k + 1) * 128],
                        self.lntile[:, e * 128:(e + 1) * 128], self.ident[:, :]),
                        reads=(self.blntile, self.bident), writes=(self.bppt[ps],))
                eng = "vector" if (si % 2) else "scalar"
                for k in range(4):
                    e = e2 + k
                    dst = lnt[:, e * T + j * 128:e * T + (j + 1) * 128]
                    src = self.ppt[ps][:, k * 128:(k + 1) * 128]
                    if eng == "vector":
                        p.op(eng, lambda en, dst=dst, src=src: en.tensor_copy(dst, src),
                             reads=(self.bppt[ps],), writes=(blnt,))
                    else:
                        p.op(eng, lambda en, dst=dst, src=src: en.copy(dst, src),
                             reads=(self.bppt[ps],), writes=(blnt,))

    def emit_weight_cast(self, wdram, wsb, bwsb, rows, cols, wtmp, bwtmp):
        p = self.p
        nr = rows // 128
        for r in range(nr):
            off = r * 128 * cols
            p.op("gpsimd", lambda e, off=off, cols=cols: e.dma_start(
                wtmp[:, 0:cols], bass.AP(wdram.tensor, off, [[cols, 128], [1, cols]])),
                writes=(bwtmp,), is_dma=True)
            p.op("vector", lambda v, r=r, cols=cols: v.tensor_copy(
                wsb[:, r * cols:(r + 1) * cols], wtmp[:, 0:cols]),
                reads=(bwtmp,), writes=(bwsb,))

    def emit_weight_transpose(self, wdram, wtsb, bwtsb, wtmp, bwtmp, wtb, bwtb):
        p = self.p
        si = 0
        for r in range(NE):
            off = r * 128 * D
            p.op("gpsimd", lambda e, off=off: e.dma_start(
                wtmp[:, 0:D], bass.AP(wdram.tensor, off, [[D, 128], [1, D]])),
                writes=(bwtmp,), is_dma=True)
            p.op("vector", lambda v: v.tensor_copy(wtb[:, 0:D], wtmp[:, 0:D]),
                 reads=(bwtmp,), writes=(bwtb,))
            for c2 in range(0, NE, 4):
                ps = si % len(self.ppt)
                si += 1
                for k in range(4):
                    c = c2 + k
                    p.op("tensor", lambda t, c=c, ps=ps, k=k: t.transpose(
                        self.ppt[ps][:, k * 128:(k + 1) * 128],
                        wtb[:, c * 128:(c + 1) * 128], self.ident[:, :]),
                        reads=(bwtb, self.bident), writes=(self.bppt[ps],))
                eng = "vector" if (si % 2) else "scalar"
                for k in range(4):
                    c = c2 + k
                    dst = wtsb[:, c * D + r * 128:c * D + (r + 1) * 128]
                    src = self.ppt[ps][:, k * 128:(k + 1) * 128]
                    if eng == "vector":
                        p.op(eng, lambda en, dst=dst, src=src: en.tensor_copy(dst, src),
                             reads=(self.bppt[ps],), writes=(bwtsb,))
                    else:
                        p.op(eng, lambda en, dst=dst, src=src: en.copy(dst, src),
                             reads=(self.bppt[ps],), writes=(bwtsb,))

    def residual_add_store(self, xin, xout, seq, j, psums, scale=None):
        p = self.p
        s = j % 2
        self.load_xtile(xin, seq, j, s)
        for oc, (ps, bps) in enumerate(psums):
            if scale is None:
                p.op("vector", lambda v, oc=oc, s=s, ps=ps, bps=bps: v.tensor_tensor(
                    self.xtile[s][:, oc * 512:(oc + 1) * 512],
                    self.xtile[s][:, oc * 512:(oc + 1) * 512], ps, ALU.add),
                    reads=(bps, self.bxtile[s]), writes=(self.bxtile[s],))
            else:
                p.op("vector", lambda v, oc=oc, s=s, ps=ps, bps=bps: v.scalar_tensor_tensor(
                    self.xtile[s][:, oc * 512:(oc + 1) * 512],
                    ps, scale, self.xtile[s][:, oc * 512:(oc + 1) * 512],
                    ALU.mult, ALU.add),
                    reads=(bps, self.bxtile[s]), writes=(self.bxtile[s],))
        off = seq * T * D + j * 128 * D
        return p.op("sync", lambda e, off=off, s=s: e.dma_start(
            bass.AP(xout.tensor, off, [[D, 128], [1, D]]), self.xtile[s][:, :]),
            reads=(self.bxtile[s],), is_dma=True)



class Shared:
    def __init__(self, p, nc, ec, consts):
        self.p = p
        self.nc = nc
        self.cm = Common(p, nc, ec, consts, n_ppt=1)
        self.lnt = ec(nc.sbuf_tensor("lnt", [128, NE * T], BF16))
        self.blnt = Buf("lnt")
        self.A = ec(nc.sbuf_tensor("arA", [128, NE * T], BF16))
        self.bA = Buf("arA")
        self.Bb = ec(nc.sbuf_tensor("arB", [128, NE * T], BF16))
        self.bB = Buf("arB")
        self.Cc = ec(nc.sbuf_tensor("arC", [128, NT * (H * 65)], BF16))
        self.bC_ = Buf("arC")
        self.W1 = ec(nc.sbuf_tensor("W1", [128, NE * D], BF16))
        self.bW1 = Buf("W1")
        self.W2 = None
        self.bW2 = Buf("W2")
        self.Wrsb = ec(nc.sbuf_tensor("Wrsb", [128, NE * (H * NB)], BF16))
        self.bWrsb = Buf("Wrsb")
        self.wtmp = ec(nc.sbuf_tensor("wtmp", [128, D], F32))
        self.bwtmp = Buf("wtmp")
        self.wtb = ec(nc.sbuf_tensor("wtb", [128, D], BF16))
        self.bwtb = Buf("wtb")
        self.halfpi = ec(nc.sbuf_tensor("halfpi", [128, 1], F32))
        self.bhalfpi = Buf("halfpi")
        self.f32s = [ec(nc.sbuf_tensor(f"f32s{i}", [128, 512], F32)) for i in range(4)]
        self.bf32s = [Buf(f"f32s{i}") for i in range(4)]
        self.krelu = [ec(nc.sbuf_tensor(f"krelu{i}", [128, 512], BF16)) for i in range(2)]
        self.bkrelu = [Buf(f"krelu{i}") for i in range(2)]
        self.rsb = [ec(nc.sbuf_tensor(f"rsb{i}", [128, 512], BF16)) for i in range(2)]
        self.brsb = [Buf(f"rsb{i}") for i in range(2)]
        self.expt = [ec(nc.sbuf_tensor(f"expt{i}", [128, 1024], BF16)) for i in range(2)]
        self.bexpt = [Buf(f"expt{i}") for i in range(2)]
        self.mus = ec(nc.sbuf_tensor("mus", [128, 32], F32))
        self.bmus = Buf("mus")
        self.brow = ec(nc.sbuf_tensor("brow", [1, 4 * D], BF16))
        self.bbrow = Buf("brow")
        self.prot = ec(nc.sbuf_tensor("prot", [128, 128], BF16))
        self.bprot = Buf("prot")
        self.bndsb = ec(nc.sbuf_tensor("bndsb", [NB, 1024], BF16))
        self.bbnd = Buf("bnd")
        self.rstage = ec(nc.sbuf_tensor("rstage", [80, 512], BF16))
        self.brstage = Buf("rstage")
        self.negtile = ec(nc.sbuf_tensor("negtile", [128, 128], BF16))
        self.bneg = Buf("negtile")
        self.rslice = [ec(nc.sbuf_tensor(f"rslice{i}", [NB, T], BF16)) for i in range(2)]
        self.brslice = [Buf(f"rslice{i}") for i in range(2)]
        self.recrow = ec(nc.sbuf_tensor("recrow", [1, 512], F32))
        self.brecrow = Buf("recrow")
        self.rbsb = ec(nc.sbuf_tensor("rbsb", [64, 512], F32))
        self.brbsb = Buf("rbsb")
        self.pg = ec(nc.psum_tensor("pg", [128, 1024], F32))
        self.bpg = [Buf("pg0"), Buf("pg1")]
        self.plog = ec(nc.psum_tensor("plog", [128, 1152], F32))
        self.bplog = Buf("plog")
        self.bplogA = Buf("plogA", overlaps=(self.bplog,))
        self.bplogB = Buf("plogB", overlaps=(self.bplog,))
        self.bplogC = Buf("plogC", overlaps=(self.bplog,))
        self.bplog.overlaps.extend([self.bplogA, self.bplogB, self.bplogC])
        self.pav = ec(nc.psum_tensor("pav", [65, 512], F32))
        self.bpav = Buf("pav")
        self.prb = ec(nc.psum_tensor("prb", [64, 512], F32))
        self.bprb = Buf("prb")
        self.g2 = [ColView(self.lnt, 0, 1152), ColView(self.lnt, 1152, 1152),
                   ColView(self.lnt, 12288, 1152)]
        self.probs = [ColView(self.lnt, 2304, 1152), ColView(self.lnt, 3456, 1152)]
        self.probsT = ColView(self.lnt, 4608, 1152)
        self.gstage = [ColView(self.lnt, 5760, 1024), ColView(self.lnt, 13440, 1024)]
        self.AT0 = ColView(self.lnt, 6784, NE * 128)
        self.bg2 = [Buf(f"g2_{i}", overlaps=(self.blnt,)) for i in range(3)]
        self.bprobs = [Buf(f"probs{i}", overlaps=(self.blnt,)) for i in range(2)]
        self.bprobsT = Buf("probsT", overlaps=(self.blnt,))
        self.bgstage = [Buf(f"gstage{i}", overlaps=(self.blnt,)) for i in range(2)]
        self.bAT0 = Buf("AT0", overlaps=(self.blnt,))
        for bb in self.bg2 + self.bprobs + self.bgstage + [self.bprobsT, self.bAT0]:
            self.blnt.overlaps.append(bb)
        self.AT1 = ColView(self.lnt, 8192, NE * 512)
        self.bAT1 = Buf("AT1", overlaps=(self.blnt,))
        self.blnt.overlaps.append(self.bAT1)
        self.W2 = ColView(self.Cc, 0, NE * D)
        self.cos2 = ColView(self.Cc, 0, T)
        self.sin2 = ColView(self.Cc, T, T)
        self.qraw = [ColView(self.Cc, 2 * T + 512 * i, 512) for i in range(2)]
        self.rtmp = [ColView(self.Cc, 2 * T + 1024 + 512 * i, 512) for i in range(2)]
        cgroup = []
        self.bcos2 = Buf("cos2")
        self.bsin2 = Buf("sin2")
        self.bqraw = [Buf(f"qraw{i}") for i in range(2)]
        self.brtmp = [Buf(f"rtmp{i}") for i in range(2)]
        cgroup = [self.bcos2, self.bsin2] + self.bqraw + self.brtmp + [self.bC_, self.bW2]
        for bb in cgroup:
            bb.overlaps.extend([o for o in cgroup if o is not bb])

        p.op("vector", lambda v: v.memset(self.halfpi[:, :], float(np.pi / 2)),
             writes=(self.bhalfpi,))



def emit_ffn0(p, sh, xin, xout, w):
    cm = sh.cm
    lnt, blnt = sh.lnt, sh.blnt
    wtmp, bwtmp = sh.wtmp, sh.bwtmp
    Cb = ColView(sh.A, 0, NE * D)
    Sb = ColView(sh.A, NE * D, NE * D)
    bC = bS = sh.bA
    sqr = [sh.f32s[0], sh.f32s[1]]
    sqi = [sh.f32s[2], sh.f32s[3]]
    froot = sqi
    bsqr = [sh.bf32s[0], sh.bf32s[1]]
    bsqi = [sh.bf32s[2], sh.bf32s[3]]
    bfroot = bsqi
    pr_ap = [sh.pg[:, 0:512], sh.pg[:, 512:1024]]
    bpr = sh.bpg
    pi_ap = [sh.plog[:, 0:512], sh.plog[:, 512:1024]]
    bpi = [sh.bplogA, sh.bplogB]

    for (wa, wb, dst, bdst, is_cos) in (
            (w["theta"], w["theta_n"], Cb, bC, True),
            (w["phi"], w["phi_n"], Sb, bS, False)):
        for r in range(NE):
            off = r * 128 * D
            p.op("gpsimd", lambda e, wa=wa, off=off: e.dma_start(
                wtmp[:, 0:D], bass.AP(wa.tensor, off, [[D, 128], [1, D]])),
                writes=(bwtmp,), is_dma=True)
            p.op("gpsimd", lambda e, wb=wb, off=off: e.dma_start(
                wtmp[:, 0:D], bass.AP(wb.tensor, off, [[D, 128], [1, D]]),
                accum_op=ALU.add),
                reads=(bwtmp,), writes=(bwtmp,), is_dma=True)
            if is_cos:
                p.op("scalar", lambda s: s.activation(
                    wtmp[:, 0:D], wtmp[:, 0:D], AF.Sin, bias=sh.halfpi[:, 0:1]),
                    reads=(bwtmp, sh.bhalfpi), writes=(bwtmp,))
            else:
                p.op("scalar", lambda s: s.activation(
                    wtmp[:, 0:D], wtmp[:, 0:D], AF.Sin),
                    reads=(bwtmp,), writes=(bwtmp,))
            p.op("vector", lambda v, dst=dst, r=r, bdst=bdst: v.tensor_scalar(
                dst[:, r * D:(r + 1) * D], wtmp[:, 0:D], 0.5, None, ALU.mult),
                reads=(bwtmp,), writes=(bdst,))

    out_toks = []
    for seq in range(SEQ_PER_CORE):
        cm.emit_ln_to_lnt(xin, seq, lnt, blnt)
        for j in range(NT):
            for oc in range(2):
                s = oc
                for k in range(NE):
                    p.op("tensor", lambda t, j=j, oc=oc, k=k, s=s: t.matmul(
                        pr_ap[s],
                        lnt[:, k * T + j * 128:k * T + (j + 1) * 128],
                        Cb[:, k * D + oc * 512:k * D + (oc + 1) * 512],
                        start=(k == 0), stop=(k == NE - 1)),
                        reads=(blnt, bC), writes=(bpr[s],))
                for k in range(NE):
                    p.op("tensor", lambda t, j=j, oc=oc, k=k, s=s: t.matmul(
                        pi_ap[s],
                        lnt[:, k * T + j * 128:k * T + (j + 1) * 128],
                        Sb[:, k * D + oc * 512:k * D + (oc + 1) * 512],
                        start=(k == 0), stop=(k == NE - 1)),
                        reads=(blnt, bS), writes=(bpi[s],))
                p.op("scalar", lambda s2, s=s: s2.activation(
                    sqr[s][:, :], pr_ap[s], AF.Square),
                    reads=(bpr[s],), writes=(bsqr[s],))
                p.op("scalar", lambda s2, s=s: s2.activation(
                    sqi[s][:, :], pi_ap[s], AF.Square),
                    reads=(bpi[s],), writes=(bsqi[s],))
                p.op("vector", lambda v, s=s: v.tensor_tensor(
                    sqr[s][:, :], sqr[s][:, :], sqi[s][:, :], ALU.add),
                    reads=(bsqr[s], bsqi[s]), writes=(bsqr[s],))
                p.op("scalar", lambda s2, s=s: s2.activation(
                    froot[s][:, :], sqr[s][:, :], AF.Sqrt),
                    reads=(bsqr[s],), writes=(bfroot[s],))
            out_toks.append(cm.residual_add_store(
                xin, xout, seq, j,
                [(froot[0][:, :], bfroot[0]), (froot[1][:, :], bfroot[1])]))
    return out_toks



def emit_ffn1(p, sh, xin, xout, w):
    cm = sh.cm
    lnt, blnt = sh.lnt, sh.blnt
    kact = lnt
    wtmp, bwtmp, wtb, bwtb = sh.wtmp, sh.bwtmp, sh.wtb, sh.bwtb
    WTa, bWTa = sh.W1, sh.bW1
    WTb, bWTb = sh.W2, sh.bW2
    mus, bmus = sh.mus, sh.bmus
    xr, bxr = sh.A, sh.bA
    xk, bxk = sh.Bb, sh.bB
    krelu, bkrelu = sh.krelu, sh.bkrelu
    rsb, brsb = sh.rsb, sh.brsb
    vtmp = [sh.f32s[0], sh.f32s[1]]
    bvtmp = [sh.bf32s[0], sh.bf32s[1]]
    pf_ap = [sh.pg[:, 0:512], sh.pg[:, 512:1024],
             sh.plog[:, 0:512], sh.plog[:, 512:1024]]
    bpf = [sh.bpg[0], sh.bpg[1], sh.bplogA, sh.bplogB]

    for (src, colbase) in ((w["mu_r"], 0), (w["mu_k"], 8)):
        for cc in range(NE):
            p.op("gpsimd", lambda e, src=src, cc=cc, colbase=colbase: e.dma_start(
                mus[:, colbase + cc:colbase + cc + 1],
                bass.AP(src.tensor, cc * 128, [[1, 128], [1, 1]])),
                writes=(bmus,), is_dma=True)
    p.op("vector", lambda v: v.tensor_scalar(
        mus[:, 16:32], mus[:, 0:16], -1.0, 1.0, ALU.mult, ALU.add),
        reads=(bmus,), writes=(bmus,))

    out_toks = []
    for seq in range(SEQ_PER_CORE):
        cm.emit_ln_to_lnt(xin, seq, lnt, blnt)
        for cc in range(NE):
            for (dst, bdst, mcol) in ((xr, bxr, 0), (xk, bxk, 8)):
                p.op("vector", lambda v, cc=cc, dst=dst, mcol=mcol: v.tensor_scalar(
                    dst[:, cc * T:(cc + 1) * T], lnt[:, cc * T:(cc + 1) * T],
                    mus[:, mcol + cc:mcol + cc + 1], None, ALU.mult),
                    reads=(blnt, bmus), writes=(bdst,))
                p.op("vector", lambda v, cc=cc, dst=dst, mcol=mcol: v.scalar_tensor_tensor(
                    dst[:, cc * T + 1:(cc + 1) * T],
                    lnt[:, cc * T:(cc + 1) * T - 1],
                    mus[:, 16 + mcol + cc:17 + mcol + cc],
                    dst[:, cc * T + 1:(cc + 1) * T],
                    ALU.mult, ALU.add),
                    reads=(blnt, bmus, bdst), writes=(bdst,))
        cm.emit_weight_transpose(w["W_k"], WTa, bWTa, wtmp, bwtmp, wtb, bwtb)
        for g in range(NE):
            for tc in range(4):
                s = (g * 4 + tc) % 2
                for cc in range(NE):
                    p.op("tensor", lambda t, g=g, tc=tc, cc=cc, s=s: t.matmul(
                        pf_ap[s],
                        WTa[:, cc * D + g * 128:cc * D + (g + 1) * 128],
                        xk[:, cc * T + tc * 512:cc * T + (tc + 1) * 512],
                        start=(cc == 0), stop=(cc == NE - 1)),
                        reads=(bWTa, bxk), writes=(bpf[s],))
                p.op("scalar", lambda sc, s=s: sc.activation(
                    krelu[s][:, :], pf_ap[s], AF.Relu),
                    reads=(bpf[s],), writes=(bkrelu[s],))
                p.op("vector", lambda v, g=g, tc=tc, s=s: v.scalar_tensor_tensor(
                    kact[:, g * T + tc * 512:g * T + (tc + 1) * 512],
                    krelu[s][:, :], 0.5, krelu[s][:, :], ALU.mult, ALU.mult),
                    reads=(bkrelu[s],), writes=(blnt,))
        cm.emit_weight_transpose(w["W_r"], WTb, bWTb, wtmp, bwtmp, wtb, bwtb)
        cm.emit_weight_transpose(w["W_v"], WTa, bWTa, wtmp, bwtmp, wtb, bwtb)
        for j in range(NT):
            pso = []
            for oc in range(2):
                s = oc
                for cc in range(NE):
                    p.op("tensor", lambda t, j=j, oc=oc, cc=cc, s=s: t.matmul(
                        pf_ap[s],
                        xr[:, cc * T + j * 128:cc * T + (j + 1) * 128],
                        WTb[:, cc * D + oc * 512:cc * D + (oc + 1) * 512],
                        start=(cc == 0), stop=(cc == NE - 1)),
                        reads=(bxr, bWTb), writes=(bpf[s],))
                p.op("scalar", lambda sc, s=s: sc.activation(
                    rsb[s][:, :], pf_ap[s], AF.Sigmoid),
                    reads=(bpf[s],), writes=(brsb[s],))
                for dd in range(NE):
                    p.op("tensor", lambda t, j=j, oc=oc, dd=dd, s=s: t.matmul(
                        pf_ap[2 + s],
                        kact[:, dd * T + j * 128:dd * T + (j + 1) * 128],
                        WTa[:, dd * D + oc * 512:dd * D + (oc + 1) * 512],
                        start=(dd == 0), stop=(dd == NE - 1)),
                        reads=(blnt, bWTa), writes=(bpf[2 + s],))
                p.op("vector", lambda v, s=s: v.tensor_tensor(
                    vtmp[s][:, :], pf_ap[2 + s], rsb[s][:, :], ALU.mult),
                    reads=(bpf[2 + s], brsb[s]), writes=(bvtmp[s],))
                pso.append((vtmp[s][:, :], bvtmp[s]))
            out_toks.append(cm.residual_add_store(xin, xout, seq, j, pso))
    return out_toks



def emit_att1(p, sh, xin, xout, w, consts):
    cm = sh.cm
    lnt, blnt = sh.lnt, sh.blnt
    wtmp, bwtmp, wtb, bwtb = sh.wtmp, sh.bwtmp, sh.wtb, sh.bwtb
    WT, bWT = sh.W1, sh.bW1
    cos2, sin2 = sh.cos2, sh.sin2
    bcos2, bsin2 = sh.bcos2, sh.bsin2
    prot, bprot = sh.prot, sh.bprot
    brow, bbrow = sh.brow, sh.bbrow
    QT, bQT = sh.A, sh.bA
    KT, bKT = sh.Bb, sh.bB
    Vsb, bV = sh.Cc, sh.bC_
    qraw, bqraw = sh.qraw, sh.bqraw
    rtmp, brtmp = sh.rtmp, sh.brtmp
    expt, bexpt = sh.expt, sh.bexpt
    AT, bAT = sh.AT1, sh.bAT1
    recrow, brecrow = sh.recrow, sh.brecrow
    rbsb, brbsb = sh.rbsb, sh.brbsb
    pl_ap = [sh.plog[:, 0:512], sh.plog[:, 512:1024], sh.pg[:, 0:512]]
    bpl = [sh.bplogA, sh.bplogB, sh.bpg[0]]
    prot_ps = sh.pg[:, 512:1024]
    bprot_ps = sh.bpg[1]
    pav, bpav = sh.pav, sh.bpav
    prb, bprb = sh.prb, sh.bprb

    p.op("sync", lambda e: e.dma_start(cos2[:, :], consts["C_COS2"]),
         writes=(bcos2,), is_dma=True)
    p.op("sync", lambda e: e.dma_start(sin2[:, :], consts["C_SIN2"]),
         writes=(bsin2,), is_dma=True)
    p.op("sync", lambda e: e.dma_start(prot[:, :], consts["C_PROT"]),
         writes=(bprot,), is_dma=True)
    for i, bn in enumerate(("bq", "bk", "bv", "bo")):
        bsrc = w[bn]
        p.op("gpsimd", lambda e, bsrc=bsrc: e.dma_start(
            wtmp[0:1, 0:D], bass.AP(bsrc.tensor, 0, [[D, 1], [1, D]])),
            writes=(bwtmp,), is_dma=True)
        p.op("vector", lambda v, i=i: v.tensor_copy(
            brow[0:1, i * D:(i + 1) * D], wtmp[0:1, 0:D]),
            reads=(bwtmp,), writes=(bbrow,))

    out_toks = []
    for seq in range(SEQ_PER_CORE):
        cm.emit_ln_to_lnt(xin, seq, lnt, blnt)

        for (wi, kind, browoff) in ((w["Wq"], "q", 0), (w["Wk"], "k", 1),
                                    (w["Wv"], "v", 2)):
            cm.emit_weight_transpose(wi, WT, bWT, wtmp, bwtmp, wtb, bwtb)
            if kind == "v":
                for j in range(NT):
                    for oc in range(2):
                        s = oc
                        for cc in range(NE):
                            p.op("tensor", lambda t, j=j, oc=oc, cc=cc, s=s: t.matmul(
                                pl_ap[s],
                                lnt[:, cc * T + j * 128:cc * T + (j + 1) * 128],
                                WT[:, cc * D + oc * 512:cc * D + (oc + 1) * 512],
                                start=(cc == 0), stop=False),
                                reads=(blnt, bWT), writes=(bpl[s],))
                        p.op("tensor", lambda t, oc=oc, s=s, browoff=browoff: t.matmul(
                            pl_ap[s],
                            cm.onesrow[0:1, 0:128],
                            brow[0:1, browoff * D + oc * 512:browoff * D + (oc + 1) * 512],
                            start=False, stop=True),
                            reads=(cm.bonesrow, bbrow), writes=(bpl[s],))
                        p.op("scalar", lambda sc, j=j, oc=oc, s=s: sc.activation(
                            bass.AP(Vsb, j * 1040 + oc * 8 * 65,
                                    [[NT * 1040, 128], [65, 8], [1, 64]]),
                            pl_ap[s], AF.Identity),
                            reads=(bpl[s],), writes=(bV,))
            else:
                scale = 0.125 if kind == "q" else 1.0
                dst, bdst = (QT, bQT) if kind == "q" else (KT, bKT)
                for g in range(NE):
                    for tc in range(4):
                        s = (g * 4 + tc) % 2
                        for cc in range(NE):
                            p.op("tensor", lambda t, g=g, tc=tc, cc=cc, s=s: t.matmul(
                                pl_ap[s],
                                WT[:, cc * D + g * 128:cc * D + (g + 1) * 128],
                                lnt[:, cc * T + tc * 512:cc * T + (tc + 1) * 512],
                                start=(cc == 0), stop=False),
                                reads=(blnt, bWT), writes=(bpl[s],))
                        p.op("tensor", lambda t, g=g, s=s, browoff=browoff: t.matmul(
                            pl_ap[s],
                            brow[0:1, browoff * D + g * 128:browoff * D + (g + 1) * 128],
                            cm.onesrow[0:1, 0:512],
                            start=False, stop=True),
                            reads=(bbrow, cm.bonesrow), writes=(bpl[s],))
                        p.op("scalar", lambda sc, s=s, scale=scale: sc.activation(
                            qraw[s][:, :], pl_ap[s], AF.Identity, scale=scale),
                            reads=(bpl[s],), writes=(bqraw[s],))
                        p.op("tensor", lambda t, s=s: t.matmul(
                            prot_ps, prot[:, :], qraw[s][:, :],
                            start=True, stop=True),
                            reads=(bprot, bqraw[s]), writes=(bprot_ps,))
                        p.op("vector", lambda v, tc=tc, s=s: v.tensor_tensor(
                            rtmp[s][:, :], prot_ps,
                            sin2[:, tc * 512:(tc + 1) * 512], ALU.mult),
                            reads=(bprot_ps, bsin2), writes=(brtmp[s],))
                        p.op("vector", lambda v, tc=tc, s=s: v.tensor_tensor(
                            qraw[s][:, :], qraw[s][:, :],
                            cos2[:, tc * 512:(tc + 1) * 512], ALU.mult),
                            reads=(bqraw[s], bcos2), writes=(bqraw[s],))
                        p.op("vector", lambda v, g=g, tc=tc, s=s, dst=dst, bdst=bdst: v.tensor_tensor(
                            dst[:, g * T + tc * 512:g * T + (tc + 1) * 512],
                            qraw[s][:, :], rtmp[s][:, :], ALU.add),
                            reads=(bqraw[s], brtmp[s]), writes=(bdst,))
        p.op("vector", lambda v: v.memset(
            bass.AP(Vsb, 64, [[NT * 1040, 128], [1040, NT], [65, H], [1, 1]]), 1.0),
            reads=(), writes=(bV,))
        cm.emit_weight_transpose(w["Wo"], WT, bWT, wtmp, bwtmp, wtb, bwtb)

        qk_ps = [(sh.plog, (sh.bplogA, sh.bplogB)), (sh.pg, (sh.bpg[0], sh.bpg[1]))]
        for qc in range(4):
            for h in range(H):
                g = h // 2
                hr = (h % 2) * 64

                def qk(pt_i):
                    pr_ = (pt_i // 2) % 2
                    half = pt_i % 2
                    pst, pbufs = qk_ps[pr_]
                    p.op("tensor", lambda t, g=g, hr=hr, qc=qc, pt_i=pt_i, pst=pst, half=half: t.matmul(
                        pst[:, half * 512:(half + 1) * 512],
                        KT[hr:hr + 64, g * T + pt_i * 128:g * T + (pt_i + 1) * 128],
                        QT[hr:hr + 64, g * T + qc * 512:g * T + (qc + 1) * 512],
                        start=True, stop=True),
                        reads=(bKT, bQT), writes=(pbufs[half],))

                def expair(pair):
                    pr_ = pair % 2
                    pst, pbufs = qk_ps[pr_]
                    es = pair % 2
                    p.op("scalar", lambda sc, pst=pst, es=es: sc.activation(
                        expt[es][:, :], pst[:, 0:1024], AF.Exp),
                        reads=pbufs, writes=(bexpt[es],))

                def pv(pt_i):
                    es = (pt_i // 2) % 2
                    half = pt_i % 2
                    p.op("tensor", lambda t, h=h, pt_i=pt_i, es=es, half=half: t.matmul(
                        pav[0:65, :],
                        Vsb[:, pt_i * 1040 + h * 65:pt_i * 1040 + h * 65 + 65],
                        expt[es][:, half * 512:(half + 1) * 512],
                        start=(pt_i == 0), stop=(pt_i == NT - 1)),
                        reads=(bV, bexpt[es]), writes=(bpav,))

                for pair in range(NT // 2):
                    qk(2 * pair)
                    qk(2 * pair + 1)
                    expair(pair)
                    if pair > 0:
                        pv(2 * pair - 2)
                        pv(2 * pair - 1)
                pv(NT - 2)
                pv(NT - 1)
                p.op("vector", lambda v: v.reciprocal(
                    recrow[0:1, :], pav[64:65, :]),
                    reads=(bpav,), writes=(brecrow,))
                p.op("tensor", lambda t: t.matmul(
                    prb[:, :], cm.onesf1[0:1, 0:64], recrow[0:1, :],
                    start=True, stop=True),
                    reads=(cm.bonesf1, brecrow), writes=(bprb,))
                p.op("scalar", lambda sc: sc.activation(
                    rbsb[:, :], prb[:, :], AF.Identity),
                    reads=(bprb,), writes=(brbsb,))
                p.op("vector", lambda v, g=g, hr=hr: v.tensor_tensor(
                    AT[hr:hr + 64, g * 512:(g + 1) * 512],
                    pav[0:64, :], rbsb[:, :], ALU.mult),
                    reads=(bpav, brbsb), writes=(bAT,))
            for ts_ in range(4):
                j = qc * 4 + ts_
                pso = []
                for oc in range(2):
                    s = oc
                    for k in range(NE):
                        p.op("tensor", lambda t, ts_=ts_, oc=oc, k=k, s=s: t.matmul(
                            pl_ap[s],
                            AT[:, k * 512 + ts_ * 128:k * 512 + (ts_ + 1) * 128],
                            WT[:, k * D + oc * 512:k * D + (oc + 1) * 512],
                            start=(k == 0), stop=False),
                            reads=(bAT, bWT), writes=(bpl[s],))
                    p.op("tensor", lambda t, oc=oc, s=s: t.matmul(
                        pl_ap[s],
                        cm.onesrow[0:1, 0:128],
                        brow[0:1, 3 * D + oc * 512:3 * D + (oc + 1) * 512],
                        start=False, stop=True),
                        reads=(cm.bonesrow, bbrow), writes=(bpl[s],))
                    pso.append((pl_ap[s], bpl[s]))
                out_toks.append(cm.residual_add_store(xin, xout, seq, j, pso))
    return out_toks



def emit_att0(p, sh, nc, xin, xout, w, gdram, rbd):
    cm = sh.cm
    lnt, blnt = sh.lnt, sh.blnt
    wtmp, bwtmp = sh.wtmp, sh.bwtmp
    Wsb, bWsb = sh.W1, sh.bW1
    Wrsb, bWrsb = sh.Wrsb, sh.bWrsb
    bndsb, bbnd = sh.bndsb, sh.bbnd
    brow, bbrow = sh.brow, sh.bbrow
    negtile, bneg = sh.negtile, sh.bneg
    QT, bQT = sh.A, sh.bA
    KT, bKT = sh.Bb, sh.bB
    Vsb, bV = sh.Cc, sh.bC_
    rslice, brslice = sh.rslice, sh.brslice
    rstage, brstage = sh.rstage, sh.brstage
    g2, bg2 = sh.g2, sh.bg2
    probs, bprobs = sh.probs, sh.bprobs
    probsT, bprobsT = sh.probsT, sh.bprobsT
    gstage, bgstage = sh.gstage, sh.bgstage
    AT, bAT = sh.AT0, sh.bAT0
    recrow, brecrow = sh.recrow, sh.brecrow
    rbsb, brbsb = sh.rbsb, sh.brbsb
    plog, bplog = sh.plog, sh.bplog
    pg = sh.pg
    bpg = sh.bpg
    pav, bpav = sh.pav, sh.bpav
    prb, bprb = sh.prb, sh.bprb
    NSLOT = 3
    bgslot = [Buf(f"gslot{i}") for i in range(NSLOT)]
    brbd = Buf("rbd")

    for (bn, coff, n) in (("bqv", 0, D), ("bprojv", D, D), ("brv", 2 * D, 160)):
        bsrc = w[bn]
        p.op("gpsimd", lambda e, bsrc=bsrc, n=n: e.dma_start(
            wtmp[0:1, 0:n], bass.AP(bsrc.tensor, 0, [[n, 1], [1, n]])),
            writes=(bwtmp,), is_dma=True)
        p.op("vector", lambda v, coff=coff, n=n: v.tensor_copy(
            brow[0:1, coff:coff + n], wtmp[0:1, 0:n]),
            reads=(bwtmp,), writes=(bbrow,))
    p.op("gpsimd", lambda e: e.dma_start(
        wtmp[0:NB, 0:1024], bass.AP(w["bndrev"].tensor, 0, [[1024, NB], [1, 1024]])),
        writes=(bwtmp,), is_dma=True)
    p.op("vector", lambda v: v.tensor_copy(bndsb[0:NB, :], wtmp[0:NB, 0:1024]),
         reads=(bwtmp,), writes=(bbnd,))
    cm.emit_weight_cast(w["Wr"], Wrsb, bWrsb, D, 160, wtmp, bwtmp)
    p.op("vector", lambda v: v.memset(negtile[:, :], NEG), writes=(bneg,))
    for sl in range(NSLOT):
        p.op("gpsimd", lambda e, sl=sl: e.dma_start(
            bass.AP(gdram, sl * 128 * 1279, [[1279, 128], [1, 128]]),
            negtile[:, :]),
            reads=(bneg,), writes=(bgslot[sl],), is_dma=True)
        p.op("gpsimd", lambda e, sl=sl: e.dma_start(
            bass.AP(gdram, sl * 128 * 1279 + 1152, [[1279, 128], [1, 127]]),
            negtile[:, 0:127]),
            reads=(bneg,), writes=(bgslot[sl],), is_dma=True)

    out_toks = []
    for seq in range(SEQ_PER_CORE):
        cm.emit_ln_to_lnt(xin, seq, lnt, blnt)

        for (wi, kind) in ((w["Wq"], "q"), (w["Wk"], "k"), (w["Wv"], "v")):
            cm.emit_weight_cast(wi, Wsb, bWsb, D, D, wtmp, bwtmp)
            if kind == "v":
                for j in range(NT):
                    for oc in range(2):
                        for cc in range(NE):
                            p.op("tensor", lambda t, j=j, oc=oc, cc=cc: t.matmul(
                                pg[:, oc * 512:(oc + 1) * 512],
                                lnt[:, cc * T + j * 128:cc * T + (j + 1) * 128],
                                Wsb[:, cc * D + oc * 512:cc * D + (oc + 1) * 512],
                                start=(cc == 0), stop=(cc == NE - 1)),
                                reads=(blnt, bWsb), writes=(bpg[oc],))
                        p.op("scalar", lambda sc, j=j, oc=oc: sc.activation(
                            bass.AP(Vsb, j * 1040 + oc * 8 * 65,
                                    [[NT * 1040, 128], [65, 8], [1, 64]]),
                            pg[:, oc * 512:(oc + 1) * 512], AF.Identity),
                            reads=(bpg[oc],), writes=(bV,))
            else:
                scale = 0.125 if kind == "q" else 1.0
                dst, bdst = (QT, bQT) if kind == "q" else (KT, bKT)
                has_bias = (kind == "q")
                for g in range(NE):
                    for tc in range(4):
                        oc2 = (g * 4 + tc) % 2
                        for cc in range(NE):
                            p.op("tensor", lambda t, g=g, tc=tc, cc=cc, oc2=oc2, has_bias=has_bias: t.matmul(
                                pg[:, oc2 * 512:(oc2 + 1) * 512],
                                Wsb[:, cc * D + g * 128:cc * D + (g + 1) * 128],
                                lnt[:, cc * T + tc * 512:cc * T + (tc + 1) * 512],
                                start=(cc == 0),
                                stop=(cc == NE - 1 and not has_bias)),
                                reads=(blnt, bWsb), writes=(bpg[oc2],))
                        if has_bias:
                            p.op("tensor", lambda t, g=g, oc2=oc2: t.matmul(
                                pg[:, oc2 * 512:(oc2 + 1) * 512],
                                brow[0:1, g * 128:(g + 1) * 128],
                                cm.onesrow[0:1, 0:512],
                                start=False, stop=True),
                                reads=(bbrow, cm.bonesrow), writes=(bpg[oc2],))
                        p.op("scalar", lambda sc, g=g, tc=tc, oc2=oc2, dst=dst, bdst=bdst, scale=scale: sc.activation(
                            dst[:, g * T + tc * 512:g * T + (tc + 1) * 512],
                            pg[:, oc2 * 512:(oc2 + 1) * 512], AF.Identity, scale=scale),
                            reads=(bpg[oc2],), writes=(bdst,))
        p.op("vector", lambda v: v.memset(
            bass.AP(Vsb, 64, [[NT * 1040, 128], [1040, NT], [65, H], [1, 1]]), 1.0),
            reads=(), writes=(bV,))
        cm.emit_weight_cast(w["Wproj"], Wsb, bWsb, D, D, wtmp, bwtmp)
        for half in range(2):
            for tc in range(4):
                for cc in range(NE):
                    p.op("tensor", lambda t, half=half, tc=tc, cc=cc: t.matmul(
                        pg[0:80, 0:512],
                        Wrsb[:, cc * 160 + half * 80:cc * 160 + half * 80 + 80],
                        lnt[:, cc * T + tc * 512:cc * T + (tc + 1) * 512],
                        start=(cc == 0), stop=False),
                        reads=(blnt, bWrsb), writes=(bpg[0],))
                p.op("tensor", lambda t, half=half: t.matmul(
                    pg[0:80, 0:512],
                    brow[0:1, 2 * D + half * 80:2 * D + half * 80 + 80],
                    cm.onesrow[0:1, 0:512],
                    start=False, stop=True),
                    reads=(bbrow, cm.bonesrow), writes=(bpg[0],))
                p.op("scalar", lambda sc: sc.activation(
                    rstage[0:80, 0:512], pg[0:80, 0:512], AF.Identity),
                    reads=(bpg[0],), writes=(brstage,))
                p.op("gpsimd", lambda e, half=half, tc=tc: e.dma_start(
                    bass.AP(rbd, half * 80 * T + tc * 512, [[T, 80], [1, 512]]),
                    rstage[0:80, 0:512]),
                    reads=(brstage,), writes=(brbd,), is_dma=True)

        idxs = [(i, h) for i in range(NT) for h in range(H)]

        def emit_g(idx):
            i, h = idxs[idx]
            sl = idx % NSLOT
            gs2 = idx % 2
            gs3 = idx % 3
            p.op("sync", lambda e, h=h, gs2=gs2: e.dma_start(
                rslice[gs2][0:NB, :],
                bass.AP(rbd, (h // 8 * 80 + (h % 8) * NB) * T, [[T, NB], [1, T]])),
                reads=(brbd,), writes=(brslice[gs2],), is_dma=True)
            for half in range(2):
                p.op("tensor", lambda t, i=i, half=half, gs2=gs2: t.matmul(
                    pg[:, half * 512:(half + 1) * 512],
                    rslice[gs2][0:NB, i * 128:(i + 1) * 128],
                    bndsb[0:NB, half * 512:(half + 1) * 512],
                    start=True, stop=True),
                    reads=(brslice[gs2], bbnd), writes=(bpg[half],))
            p.op("vector", lambda v, gs2=gs2: v.tensor_copy(gstage[gs2][:, :], pg[:, :]),
                reads=(bpg[0], bpg[1]), writes=(bgstage[gs2],))
            p.op("gpsimd", lambda e, sl=sl, gs2=gs2: e.dma_start(
                bass.AP(gdram, sl * 128 * 1279 + 128, [[1279, 128], [1, 1024]]),
                gstage[gs2][:, :]),
                reads=(bgstage[gs2],), writes=(bgslot[sl],), is_dma=True)
            p.op("sync", lambda e, sl=sl, gs3=gs3: e.dma_start(
                g2[gs3][:, :],
                bass.AP(gdram, sl * 128 * 1279 + 127, [[1278, 128], [1, 1152]])),
                reads=(bgslot[sl],), writes=(bg2[gs3],), is_dma=True)

        bplogch = {0: sh.bplogA, 512: sh.bplogB, 1024: sh.bplogC}

        def emit_qk(idx):
            i, h = idxs[idx]
            gs = idx % 3
            g = h // 2
            hr = (h % 2) * 64
            nb = min(i, 8) + 1
            bc = (9 - nb) * 128
            p0 = (i - (nb - 1)) * 128
            ps_ = idx % 2
            for (c0, c1) in psum_chunks(bc, 1152):
                bch = bplogch[(c0 // 512) * 512]
                p.op("tensor", lambda t, g=g, hr=hr, i=i, c0=c0, c1=c1, bc=bc, p0=p0: t.matmul(
                    plog[:, c0:c1],
                    QT[hr:hr + 64, g * T + i * 128:g * T + (i + 1) * 128],
                    KT[hr:hr + 64, g * T + p0 + (c0 - bc):g * T + p0 + (c1 - bc)],
                    start=True, stop=False),
                    reads=(bQT, bKT), writes=(bch,))
                p.op("tensor", lambda t, c0=c0, c1=c1, gs=gs: t.matmul(
                    plog[:, c0:c1],
                    cm.ident[:, :],
                    g2[gs][:, c0:c1],
                    start=False, stop=True),
                    reads=(cm.bident, bg2[gs]), writes=(bch,))
                p.op("scalar", lambda sc, c0=c0, c1=c1, ps_=ps_: sc.activation(
                    probs[ps_][:, c0:c1], plog[:, c0:c1], AF.Exp),
                    reads=(bch,), writes=(bprobs[ps_],))

        def emit_tail(idx):
            i, h = idxs[idx]
            g = h // 2
            hr = (h % 2) * 64
            nb = min(i, 8) + 1
            bc = (9 - nb) * 128
            ps_ = idx % 2
            blocks = list(range(bc // 128, 9))
            si = 0
            for b2 in range(0, len(blocks), 4):
                grp = blocks[b2:b2 + 4]
                pslot = si % len(cm.ppt)
                si += 1
                for k, m in enumerate(grp):
                    p.op("tensor", lambda t, m=m, k=k, pslot=pslot, ps_=ps_: t.transpose(
                        cm.ppt[pslot][:, k * 128:(k + 1) * 128],
                        probs[ps_][:, m * 128:(m + 1) * 128], cm.ident[:, :]),
                        reads=(bprobs[ps_], cm.bident), writes=(cm.bppt[pslot],))
                eng = "vector" if (si % 2) else "scalar"
                for k, m in enumerate(grp):
                    dstp = probsT[:, m * 128:(m + 1) * 128]
                    srcp = cm.ppt[pslot][:, k * 128:(k + 1) * 128]
                    if eng == "vector":
                        p.op(eng, lambda en, dstp=dstp, srcp=srcp: en.tensor_copy(dstp, srcp),
                             reads=(cm.bppt[pslot],), writes=(bprobsT,))
                    else:
                        p.op(eng, lambda en, dstp=dstp, srcp=srcp: en.copy(dstp, srcp),
                             reads=(cm.bppt[pslot],), writes=(bprobsT,))
            for k, m in enumerate(blocks):
                jj = i - (nb - 1) + k
                p.op("tensor", lambda t, jj=jj, h=h, m=m, k=k, nb=nb: t.matmul(
                    pav[0:65, 0:128],
                    Vsb[:, jj * 1040 + h * 65:jj * 1040 + h * 65 + 65],
                    probsT[:, m * 128:(m + 1) * 128],
                    start=(k == 0), stop=(k == nb - 1)),
                    reads=(bV, bprobsT), writes=(bpav,))
            p.op("vector", lambda v: v.reciprocal(recrow[0:1, 0:128], pav[64:65, 0:128]),
                 reads=(bpav,), writes=(brecrow,))
            p.op("tensor", lambda t: t.matmul(
                prb[0:64, 0:128], cm.onesf1[0:1, 0:64], recrow[0:1, 0:128],
                start=True, stop=True),
                reads=(cm.bonesf1, brecrow), writes=(bprb,))
            p.op("scalar", lambda sc: sc.activation(
                rbsb[0:64, 0:128], prb[0:64, 0:128], AF.Identity),
                reads=(bprb,), writes=(brbsb,))
            p.op("vector", lambda v, g=g, hr=hr: v.tensor_tensor(
                AT[hr:hr + 64, g * 128:(g + 1) * 128],
                pav[0:64, 0:128], rbsb[0:64, 0:128], ALU.mult),
                reads=(bpav, brbsb), writes=(bAT,))

        def emit_outproj(i):
            pso = []
            for oc in range(2):
                bch = bplogch[oc * 512]
                for k in range(NE):
                    p.op("tensor", lambda t, oc=oc, k=k: t.matmul(
                        plog[:, oc * 512:(oc + 1) * 512],
                        AT[:, k * 128:(k + 1) * 128],
                        Wsb[:, k * D + oc * 512:k * D + (oc + 1) * 512],
                        start=(k == 0), stop=False),
                        reads=(bAT, bWsb), writes=(bch,))
                p.op("tensor", lambda t, oc=oc: t.matmul(
                    plog[:, oc * 512:(oc + 1) * 512],
                    cm.onesrow[0:1, 0:128],
                    brow[0:1, D + oc * 512:D + (oc + 1) * 512],
                    start=False, stop=True),
                    reads=(cm.bonesrow, bbrow), writes=(bch,))
                pso.append((plog[:, oc * 512:(oc + 1) * 512], bch))
            out_toks.append(cm.residual_add_store(xin, xout, seq, i, pso))

        emit_g(0)
        emit_g(1)
        for idx in range(len(idxs)):
            i, h = idxs[idx]
            if idx + 2 < len(idxs):
                emit_g(idx + 2)
            emit_qk(idx)
            if idx > 0:
                emit_tail(idx - 1)
                pi_, ph_ = idxs[idx - 1]
                if ph_ == H - 1:
                    emit_outproj(pi_)
        emit_tail(len(idxs) - 1)
        emit_outproj(NT - 1)
    return out_toks



FFN0_W = ("theta", "phi", "theta_n", "phi_n")
FFN1_W = ("W_r", "W_k", "W_v", "mu_r", "mu_k")
ATT1_W = ("Wq", "Wk", "Wv", "Wo", "bq", "bk", "bv", "bo")
ATT0_W = ("Wq", "Wk", "Wv", "Wproj", "Wr", "bqv", "bprojv", "brv", "bndrev")
_SHAPES = {
    "theta": D * D, "phi": D * D, "theta_n": D * D, "phi_n": D * D,
    "W_r": D * D, "W_k": D * D, "W_v": D * D, "mu_r": D, "mu_k": D,
    "Wq": D * D, "Wk": D * D, "Wv": D * D, "Wo": D * D, "Wproj": D * D,
    "Wr": D * H * NB, "bq": D, "bk": D, "bv": D, "bo": D,
    "bqv": D, "bprojv": D, "brv": H * NB, "bndrev": NB * 1024,
}
ALL_CONSTS = ["C_IDENT", "C_ONESROW", "C_ONESCOL", "C_ONESF1",
              "C_COS2", "C_SIN2", "C_PROT"]


def _w_aps(nc, names, prefix=""):
    return {n: nc.dram_tensor(prefix + n, [_SHAPES[n]], F32, kind="ExternalInput").ap()
            for n in names}


def _build_single(which):
    nc = bass.Bass("TRN2", target_bir_lowering=False)
    xin = nc.dram_tensor("xin", [SEQ_PER_CORE * T * D], F32, kind="ExternalInput").ap()
    names = {"ffn0": FFN0_W, "ffn1": FFN1_W, "att1": ATT1_W, "att0": ATT0_W}[which]
    w = _w_aps(nc, names)
    consts = add_consts(nc, ALL_CONSTS)
    xout = nc.dram_tensor("xout", [SEQ_PER_CORE * T * D], F32, kind="ExternalOutput").ap()
    p = Sched()
    with ExitStack() as stack:
        ec = stack.enter_context
        sh = Shared(p, nc, ec, consts)
        if which == "ffn0":
            toks = emit_ffn0(p, sh, xin, xout, w)
        elif which == "ffn1":
            toks = emit_ffn1(p, sh, xin, xout, w)
        elif which == "att1":
            toks = emit_att1(p, sh, xin, xout, w, consts)
        else:
            gdram = nc.dram_tensor("gdram", [3 * 128 * 1279], BF16, kind="Internal")
            rbd = nc.dram_tensor("rbd", [2 * 80 * T], BF16, kind="Internal")
            toks = emit_att0(p, sh, nc, xin, xout, w, gdram, rbd)
        p.barrier("sync", toks)
        block = ec(nc.Block())
        p.finalize(nc, block, stack)
    return nc


def build_ffn0():
    return _build_single("ffn0")


def build_ffn1():
    return _build_single("ffn1")


def build_att1():
    return _build_single("att1")


def build_att0():
    return _build_single("att0")


def build_fused():
    nc = bass.Bass("TRN2", target_bir_lowering=False)
    xin = nc.dram_tensor("xin", [SEQ_PER_CORE * T * D], F32, kind="ExternalInput").ap()
    w_f0 = _w_aps(nc, FFN0_W, "f0_")
    w_a0 = _w_aps(nc, ATT0_W, "a0_")
    w_f1 = _w_aps(nc, FFN1_W, "f1_")
    w_a1 = _w_aps(nc, ATT1_W, "a1_")
    consts = add_consts(nc, ALL_CONSTS)
    xout = nc.dram_tensor("xout", [SEQ_PER_CORE * T * D], F32, kind="ExternalOutput").ap()
    xb = [nc.dram_tensor(f"xb{i}", [SEQ_PER_CORE * T * D], F32, kind="Internal").ap()
          for i in range(5)]
    gdram = nc.dram_tensor("gdram", [3 * 128 * 1279], BF16, kind="Internal")
    rbd = nc.dram_tensor("rbd", [2 * 80 * T], BF16, kind="Internal")

    p = Sched()
    with ExitStack() as stack:
        ec = stack.enter_context
        sh = Shared(p, nc, ec, consts)
        emit_ffn0(p, sh, xin, xb[0], w_f0)
        emit_att0(p, sh, nc, xb[0], xb[1], w_a0, gdram, rbd)
        emit_ffn0(p, sh, xb[1], xb[2], w_f0)
        emit_ffn1(p, sh, xb[2], xb[3], w_f1)
        emit_att1(p, sh, xb[3], xb[4], w_a1, consts)
        toks = emit_ffn1(p, sh, xb[4], xout, w_f1)
        p.barrier("sync", toks)
        block = ec(nc.Block())
        p.finalize(nc, block, stack)
    return nc



_cache = {}


def _get(name, builder):
    if name not in _cache:
        _cache[name] = builder()
    return _cache[name]


def _input_maps(inputs):
    consts = host_consts()
    xs = [np.ascontiguousarray(s).reshape(-1) for s in
          np.split(np.ascontiguousarray(inputs["L0_attn_X"]), NCORES, axis=0)]
    ffn0_w = {
        "theta": inputs["L0_ffn_theta"].reshape(-1),
        "phi": inputs["L0_ffn_phi"].reshape(-1),
        "theta_n": inputs["L0_ffn_theta_noise"].reshape(-1),
        "phi_n": inputs["L0_ffn_phi_noise"].reshape(-1),
    }
    att0_w = {
        "Wq": inputs["L0_attn_Wq"].reshape(-1),
        "Wk": inputs["L0_attn_Wk"].reshape(-1),
        "Wv": inputs["L0_attn_Wv"].reshape(-1),
        "Wproj": inputs["L0_attn_Wproj"].reshape(-1),
        "Wr": inputs["L0_attn_Wr"].reshape(-1),
        "bqv": inputs["L0_attn_bq"].reshape(-1),
        "bprojv": inputs["L0_attn_bproj"].reshape(-1),
        "brv": inputs["L0_attn_br"].reshape(-1),
        "bndrev": np.ascontiguousarray(inputs["L0_attn_b_nd"][:, ::-1]).reshape(-1),
    }
    ffn1_w = {
        "W_r": inputs["L1_ffn_W_r"].reshape(-1),
        "W_k": inputs["L1_ffn_W_k"].reshape(-1),
        "W_v": inputs["L1_ffn_W_v"].reshape(-1),
        "mu_r": inputs["L1_ffn_mu_r"].reshape(-1),
        "mu_k": inputs["L1_ffn_mu_k"].reshape(-1),
    }
    att1_w = {
        "Wq": inputs["L1_attn_Wq"].reshape(-1),
        "Wk": inputs["L1_attn_Wk"].reshape(-1),
        "Wv": inputs["L1_attn_Wv"].reshape(-1),
        "Wo": inputs["L1_attn_Wo"].reshape(-1),
        "bq": inputs["L1_attn_bq"].reshape(-1),
        "bk": inputs["L1_attn_bk"].reshape(-1),
        "bv": inputs["L1_attn_bv"].reshape(-1),
        "bo": inputs["L1_attn_bo"].reshape(-1),
    }
    return consts, xs, ffn0_w, att0_w, ffn1_w, att1_w


def kernel(**inputs):
    consts, xs, ffn0_w, att0_w, ffn1_w, att1_w = _input_maps(inputs)
    wmap = {}
    for pref, w in (("f0_", ffn0_w), ("a0_", att0_w), ("f1_", ffn1_w), ("a1_", att1_w)):
        for k, v in w.items():
            wmap[pref + k] = v
    for k in ALL_CONSTS:
        wmap[k] = consts[k]
    nc = _get("fused", build_fused)
    in_maps = [dict(wmap, xin=xs[c]) for c in range(NCORES)]
    res = run_bass_kernel_spmd(nc, in_maps, core_ids=list(range(NCORES)))
    out = np.concatenate(
        [np.asarray(r["xout"], np.float32).reshape(SEQ_PER_CORE, T, D)
         for r in res.results], axis=0)
    return out.astype(np.float32)


def kernel_unfused(**inputs):
    consts, xs, ffn0_w, att0_w, ffn1_w, att1_w = _input_maps(inputs)
    cmap = {k: consts[k] for k in ALL_CONSTS}

    def launch(nc, wm):
        nonlocal xs
        in_maps = [dict(cmap, xin=xs[c], **wm) for c in range(NCORES)]
        res = run_bass_kernel_spmd(nc, in_maps, core_ids=list(range(NCORES)))
        xs = [np.asarray(r["xout"], np.float32) for r in res.results]

    launch(_get("ffn0", build_ffn0), ffn0_w)
    launch(_get("att0", build_att0), att0_w)
    launch(_get("ffn0", build_ffn0), ffn0_w)
    launch(_get("ffn1", build_ffn1), ffn1_w)
    launch(_get("att1", build_att1), att1_w)
    launch(_get("ffn1", build_ffn1), ffn1_w)
    out = np.concatenate([x.reshape(SEQ_PER_CORE, T, D) for x in xs], axis=0)
    return out.astype(np.float32)
